# version 16
# baseline (speedup 1.0000x reference)
import contextlib
import math
import os
import numpy as np
import ml_dtypes
import concourse.bass as bass
import concourse.mybir as mybir
from concourse.ap import AP
from concourse.bass_utils import run_bass_kernel_spmd

F32 = mybir.dt.float32
BF16 = mybir.dt.bfloat16
AF = mybir.ActivationFunctionType
ALU = mybir.AluOpType
AX = mybir.AxisListType

B, S, D = 2, 8192, 1024
NH, HD = 16, 64
RW = 128
R = 2 * RW + 1
E, HF = 8, 2048
TPC = S // 4
HPG, NG = 4, 4
W_PAD, KC = 1536, 768
EPS = 1e-5

DEBUG = False
INTERLEAVE_PREP = True
RUN_GROUPS = NG
RUN_HEADS = HPG
RUN_ATTN = True
PREP_STAGES = 9



class Sched:

    ENGS = ("sync", "gpsimd", "tensor", "scalar", "vector")

    def __init__(self, nc, es):
        self.nc = nc
        self.events = []
        self.sems = {}
        self.ctr = {}
        self.es = es

    def sem(self, name):
        self.sems[name] = self.es.enter_context(self.nc.semaphore(name))
        self.ctr[name] = 0

    def ev(self, eng, fn, waits=(), inc=None, k=1):
        waits = [(s, v) for (s, v) in waits if v > 0]
        if inc is not None:
            self.ctr[inc] += k
        self.events.append((eng, fn, waits, inc, k))
        return self.ctr[inc] if inc is not None else None

    def build(self, block):
        n_ev = len(self.events)
        for eng_name in self.ENGS:
            def body(eng, eng_name=eng_name):
                waited = {}
                for (e, fn, waits, inc, k) in self.events:
                    if e != eng_name:
                        continue
                    for (s, v) in waits:
                        if waited.get(s, 0) < v:
                            eng.wait_ge(self.sems[s], v)
                            waited[s] = v
                    inst = fn(eng)
                    if inc is not None:
                        inst.then_inc(self.sems[inc], k)
                assert len(self.events) == n_ev, "events created during replay!"
            getattr(block, eng_name)(body)


def classify(p, qic):
    lo = 128 * p - 4 * (128 * qic + 127) - 3
    hi = 128 * p + 127 - 4 * 128 * qic
    if lo >= 129:
        return 'R'
    if hi <= -129:
        return 'L'
    return 'S'


def classify_qb(p, qb):
    cs = {classify(p, qic) for qic in range(4 * qb, 4 * qb + 4)}
    if cs == {'R'}:
        return 'R'
    if cs == {'L'}:
        return 'L'
    return 'M'



def ap3(t, off, dims):
    return AP(t.tensor if isinstance(t, AP) else t, off, dims)


def build_program():
    nc = bass.Bass("TRN2", target_bir_lowering=False, debug=False,
                   detect_race_conditions=bool(os.environ.get("KRACE")))

    def inp(name, shape):
        return nc.dram_tensor(name, shape, F32, kind="ExternalInput").ap()

    x_rows = inp("x_rows", [S, D])
    x_own_rows = inp("x_own_rows", [TPC, D])
    xT_own = inp("xT_own", [D, TPC])
    WQ_T = inp("WQ_T", [D, D])
    WK_T = inp("WK_T", [D, D])
    WV_T = inp("WV_T", [D, D])
    bQKV = inp("bQKV", [128, 24])
    WrBc = inp("WrBc", [65, 261])
    Wr0 = inp("Wr0", [65, 1])
    Wr256 = inp("Wr256", [65, 1])
    gate_w_in = inp("gate_w", [D, E])
    ew1 = inp("ew1", [E, D, HF])
    ew2 = inp("ew2", [E, D, HF])
    ew3 = inp("ew3", [E, HF, D])
    sw1 = inp("sw1", [D, HF])
    sw2 = inp("sw2", [D, HF])
    sw3 = inp("sw3", [HF, D])

    OUT = nc.dram_tensor("OUT", [D, TPC], F32, kind="ExternalOutput").ap()

    ln1_dram = nc.dram_tensor("ln1_dram", [S, D], BF16, kind="Internal")
    ln1own_dram = nc.dram_tensor("ln1own_dram", [TPC, D], BF16, kind="Internal")
    rp_dram = [nc.dram_tensor(f"rp_dram{h}", [TPC, W_PAD], BF16, kind="Internal")
               for h in range(HPG)]

    ident_f32 = nc.inline_tensor(np.eye(128, dtype=np.float32), "ident_f32")
    ident_b16 = nc.inline_tensor(np.eye(128, dtype=ml_dtypes.bfloat16), "ident_b16")
    ones_col_f32 = nc.inline_tensor(np.ones((128, 1), dtype=np.float32), "ones_col")

    es = contextlib.ExitStack()
    sch = Sched(nc, es)
    for s in ("sLD", "sGP", "sSC", "sPE", "sACT", "sDVE", "sST",
              "sLN0", "sLN1", "sLNX0", "sLNX1", "sLNS0", "sLNS1",
              "sPOS0", "sPOS1", "sQAUG", "sQG1", "sQG2", "sQG3", "sCW",
              "sW120", "sW121", "sW30", "sW31", "sWB0", "sWB1"):
        sch.sem(s)
    QSEMS = ("sQAUG", "sQG1", "sQG2", "sQG3")
    RPSEMS = ("sLNX0", "sLNX1", "sLNS0", "sLNS1")

    sb = nc.alloc_sbuf_tensor
    ident_sb = sb("ident_sb", [128, 128], F32).ap()
    identb_sb = sb("identb_sb", [128, 128], BF16).ap()
    ones_sb = sb("ones_sb", [128, 1], F32).ap()
    bqkv_sb = sb("bqkv_sb", [128, 24], F32).ap()
    bq8_sb = sb("bq8_sb", [128, 8], F32).ap()
    wrbc8 = sb("wrbc8", [65, 261], BF16).ap()
    wr0_8 = sb("wr0_8", [65, 1], BF16).ap()
    wr256_8 = sb("wr256_8", [65, 1], BF16).ap()

    nrmWr = sb("nrmWr", [1, 2], F32).ap()
    gate_sb = sb("gate_sb", [128, 8 * 8], BF16).ap()
    constcols = sb("constcols", [128, 4], F32).ap()
    ones_bf = sb("ones_bf", [128, 1], BF16).ap()
    onesrow65 = sb("onesrow65", [65, 128], BF16).ap()
    wr01_8 = sb("wr01_8", [65, 2], BF16).ap()
    wr01_f = sb("wr01_f", [65, 2], F32).ap()
    scal_sb = sb("scal_sb", [1, 16], F32).ap()
    maxbr64 = sb("maxbr64", [65, 1], F32).ap()

    maxk_row = sb("maxk_row", [1, NH], F32).ap()
    o_dram = nc.dram_tensor("o_dram", [D, TPC], BF16, kind="Internal")

    psA = nc.alloc_psum_tensor("psA", [128, 1024], F32).ap()
    psB = nc.alloc_psum_tensor("psB", [128, 1024], F32).ap()
    psC = nc.alloc_psum_tensor("psC", [128, 1024], F32).ap()
    psD = nc.alloc_psum_tensor("psD", [128, 1024], F32).ap()

    CONST_COLS = {}

    def constcol(val):
        if val not in CONST_COLS:
            i = len(CONST_COLS)
            CONST_COLS[val] = constcols[:, i:i + 1]
            sch.ev("vector", lambda e, i=i, val=val: e.memset(constcols[:, i:i + 1], val), inc="sDVE")
        return CONST_COLS[val]

    def emit_const_loads():
        sch.ev("sync", lambda e: e.dma_start(out=ident_sb, in_=ident_f32.ap()), inc="sLD", k=16)
        sch.ev("sync", lambda e: e.dma_start(out=identb_sb, in_=ident_b16.ap()), inc="sLD", k=16)
        sch.ev("sync", lambda e: e.dma_start(out=ones_sb, in_=ones_col_f32.ap()), inc="sLD", k=16)
        sch.ev("sync", lambda e: e.dma_start(out=bqkv_sb, in_=bQKV), inc="sLD", k=16)
        sch.ev("sync", lambda e: e.dma_start(out=wr_stat[:, 0:261], in_=WrBc), inc="sLD", k=16)
        sch.ev("sync", lambda e: e.dma_start(out=wr01_f[:, 0:1], in_=Wr0), inc="sLD", k=16)
        sch.ev("sync", lambda e: e.dma_start(out=wr01_f[:, 1:2], in_=Wr256), inc="sLD", k=16)
        ld0 = sch.ctr["sLD"]
        gv = ap3(gate_w_in, 0, [[8, 128], [128 * 8, 8], [1, 8]])
        go = ap3(gate_sb, 0, [[64, 128], [8, 8], [1, 8]])
        sch.ev("gpsimd", lambda e: e.dma_start(out=go, in_=gv), inc="sCW", k=16)
        sch.ev("vector", lambda e: e.tensor_scalar(wrbc8[0:64, :], wr_stat[0:64, :], 8.0, None, ALU.mult),
               waits=[("sLD", ld0)], inc="sDVE")
        sch.ev("vector", lambda e: e.memset(scal_sb, 0.0), inc="sDVE")
        sch.ev("vector", lambda e: e.memset(maxk_row, 0.0), inc="sDVE")
        d_wr = sch.ev("vector", lambda e: e.tensor_copy(wrbc8[64:65, :], wr_stat[64:65, :]), inc="sDVE")
        constcol(EPS)
        sch.ev("vector", lambda e: e.memset(ones_bf, 1.0), inc="sDVE")
        sch.ev("vector", lambda e: e.memset(onesrow65[64:65, :], 1.0), inc="sDVE")
        ld1 = sch.ctr["sLD"]
        sch.ev("vector", lambda e: e.tensor_scalar(wr01_8[0:64, :], wr01_f[0:64, :], 8.0, None, ALU.mult),
               waits=[("sLD", ld1)], inc="sDVE")
        sch.ev("vector", lambda e: e.tensor_copy(wr01_8[64:65, :], wr01_f[64:65, :]), inc="sDVE")
        sch.ev("vector", lambda e: e.tensor_scalar(bq8_sb, bqkv_sb[:, 0:8], 0.125, None, ALU.mult),
               waits=[("sLD", ld0)], inc="sDVE")
        d_br = sch.ev("vector", lambda e: e.reduce_max(maxbr64[64:65, 0:1], wrbc8[64:65, 0:261], axis=AX.X),
                      waits=[("sDVE", d_wr)], inc="sDVE")
        sch.ev("gpsimd", lambda e: e.dma_start(out=ap3(scal_sb, 1, [[16, 1], [1, 1]]),
                                               in_=ap3(maxbr64, 64, [[1, 1], [1, 1]])),
               waits=[("sDVE", d_br)], inc="sGP", k=16)
        a_sq = sch.ev("scalar", lambda e: e.activation(wr_sq, wrbc8[0:64, :], AF.Square),
                      waits=[("sDVE", d_wr)], inc="sACT")
        p_nw = sch.ev("tensor", lambda e: e.matmul(psA[0:1, 0:261], ones_bf[0:64, :], wr_sq,
                                                   start=True, stop=True),
                      waits=[("sACT", a_sq), ("sDVE", sch.ctr["sDVE"])], inc="sPE")
        a_nw = sch.ev("scalar", lambda e: e.activation(row261, psA[0:1, 0:261], AF.Sqrt),
                      waits=[("sPE", p_nw)], inc="sACT")
        sch.ev("vector", lambda e: e.reduce_max(scal_sb[0:1, 0:1], row261, axis=AX.X),
               waits=[("sACT", a_nw)], inc="sDVE")
        return ld0

    es_p1 = contextlib.ExitStack()
    sbp = lambda name, shape, dt: es_p1.enter_context(nc.sbuf_tensor(name, shape, dt)).ap()
    wr_stat = sbp("wr_stat", [65, 261], F32)
    row261 = sbp("row261", [1, 261], F32)
    wr_sq = sbp("wr_sq", [64, 261], BF16)
    p1_bufx = [sbp(f"p1x{i}", [128, 4096], F32) for i in range(2)]
    p1_bufl = [sbp(f"p1l{i}", [128, 4096], BF16) for i in range(2)]
    p1_sq = [sbp(f"p1sq{i}", [128, 1024], F32) for i in range(2)]
    p1_stat = [sbp(f"p1st{i}", [128, 24], F32) for i in range(2)]

    def emit_ln_phase(src_ap, dst_dram, nblk, ld0, dbg_dram=None):
        prev_dve = sch.ctr["sDVE"]
        prev_act = sch.ctr["sACT"]
        prev_st = {0: sch.ctr["sLNS0"], 1: sch.ctr["sLNS1"]}
        entry = [("sDVE", prev_dve), ("sACT", prev_act)]
        ap_done = {}
        st_done = {}
        dbg_guard = {}
        for b in range(nblk):
            bx, bl = p1_bufx[b % 2], p1_bufl[b % 2]
            sq, st = p1_sq[b % 2], p1_stat[b % 2]
            ldsem = f"sLNX{b % 2}"
            stsem = f"sLNS{b % 2}"
            waits = [("sLD", ld0)]
            if b >= 2:
                waits += ap_done[b - 2]
                if (b - 2) in dbg_guard:
                    waits.append(("sST", dbg_guard[b - 2]))
            else:
                waits += entry
            for hb, qn in ((0, "sync"), (1, "scalar")):
                src = ap3(src_ap, (b * 512 + hb * 256) * D, [[D, 128], [128 * D, 2], [1, D]])
                dst_sb = ap3(bx, hb * 2048, [[4096, 128], [1024, 2], [1, 1024]])
                sch.ev(qn, lambda e, src=src, dst_sb=dst_sb: e.dma_start(out=dst_sb, in_=src),
                       waits=waits, inc=ldsem, k=16)
            ldb = sch.ctr[ldsem]
            w_prev = ap_done[b - 2] if b >= 2 else []
            for a in range(4):
                xt = bx[:, a * 1024:(a + 1) * 1024]
                sch.ev("vector", lambda e, xt=xt, st=st, a=a: e.reduce_sum(st[:, a:a + 1], xt, axis=AX.X),
                       waits=[(ldsem, ldb)] + w_prev, inc="sDVE")
                sch.ev("scalar", lambda e, xt=xt, sq=sq, st=st, a=a: e.activation(
                    sq, xt, AF.Square, accum_out=st[:, 4 + a:5 + a]),
                       waits=[(ldsem, ldb)] + w_prev, inc="sACT")
            d_sums = sch.ctr["sDVE"]
            a_sqs = sch.ctr["sACT"]
            m1 = sch.ev("vector", lambda e, st=st: e.tensor_scalar(st[:, 8:12], st[:, 0:4], 1.0 / D, None, ALU.mult),
                        waits=[("sDVE", d_sums)], inc="sDVE")
            m2 = sch.ev("vector", lambda e, st=st: e.tensor_scalar(st[:, 12:16], st[:, 4:8], 1.0 / D, None, ALU.mult),
                        waits=[("sACT", a_sqs)], inc="sDVE")
            m3 = sch.ev("vector", lambda e, st=st: e.tensor_mul(st[:, 16:20], st[:, 8:12], st[:, 8:12]),
                        waits=[("sDVE", m1)], inc="sDVE")
            m4 = sch.ev("vector", lambda e, st=st: e.tensor_sub(st[:, 12:16], st[:, 12:16], st[:, 16:20]),
                        waits=[("sDVE", m3)], inc="sDVE")
            epsc = constcol(EPS)
            sch.ev("scalar", lambda e, st=st, epsc=epsc: e.activation(st[:, 16:20], st[:, 12:16], AF.Sqrt, bias=epsc),
                   waits=[("sDVE", m4)], inc="sACT")
            a_rt = sch.ctr["sACT"]
            m5 = sch.ev("vector", lambda e, st=st: e.reciprocal(st[:, 20:24], st[:, 16:20]),
                        waits=[("sACT", a_rt)], inc="sDVE")
            m6 = sch.ev("vector", lambda e, st=st: e.scalar_tensor_tensor(
                st[:, 0:4], st[:, 8:12], -1.0, st[:, 20:24], ALU.mult, ALU.mult),
                        waits=[("sDVE", m5)], inc="sDVE")
            w_st = ([(stsem, st_done[b - 2])] if b >= 2
                    else [(stsem, prev_st[b % 2])])
            for a in range(4):
                xt = bx[:, a * 1024:(a + 1) * 1024]
                lt = bl[:, a * 1024:(a + 1) * 1024]
                if a < 2:
                    sch.ev("scalar", lambda e, xt=xt, lt=lt, st=st, a=a: e.activation(
                        lt, xt, AF.Identity, bias=st[:, 0 + a:1 + a], scale=st[:, 20 + a:21 + a]),
                           waits=w_st + [("sDVE", m6)], inc="sACT")
                else:
                    sch.ev("vector", lambda e, xt=xt, lt=lt, st=st, a=a: e.tensor_scalar(
                        lt, xt, st[:, 8 + a:9 + a], st[:, 20 + a:21 + a], ALU.subtract, ALU.mult),
                           waits=w_st + [("sDVE", m6)], inc="sDVE")
            ap_done[b] = [("sACT", sch.ctr["sACT"]), ("sDVE", sch.ctr["sDVE"])]
            dstv = ap3(dst_dram, b * 512 * D, [[D, 128], [128 * D, 4], [1, D]])
            srcv = ap3(bl, 0, [[4096, 128], [1024, 4], [1, 1024]])
            sch.ev("gpsimd", lambda e, dstv=dstv, srcv=srcv: e.dma_start(out=dstv, in_=srcv),
                   waits=ap_done[b], inc=stsem, k=16)
            if dbg_dram is not None:
                dbv = ap3(dbg_dram, b * 512 * D, [[D, 128], [128 * D, 4], [1, D]])
                sch.ev("scalar", lambda e, dbv=dbv, srcv=srcv: e.dma_start(out=dbv, in_=srcv),
                       waits=ap_done[b], inc="sST", k=16)
            if dbg_dram is not None and b == 0:
                D0S = nc.dram_tensor("DBG0_ST", [128, 24], F32, kind="ExternalOutput").ap()
                D0X = nc.dram_tensor("DBG0_X", [128, 4096], F32, kind="ExternalOutput").ap()
                D0L = nc.dram_tensor("DBG0_L", [128, 4096], BF16, kind="ExternalOutput").ap()
                sch.ev("scalar", lambda e, st=st: e.dma_start(out=D0S, in_=st),
                       waits=ap_done[b], inc="sST", k=16)
                sch.ev("scalar", lambda e, bx=bx: e.dma_start(out=D0X, in_=bx),
                       waits=ap_done[b], inc="sST", k=16)
                sch.ev("scalar", lambda e, bl=bl: e.dma_start(out=D0L, in_=bl),
                       waits=ap_done[b], inc="sST", k=16)
                dbg_guard[b] = sch.ctr["sST"]
            st_done[b] = sch.ctr[stsem]
        return st_done[nblk - 1]


    JB = S // 512
    QBn = TPC // 512

    def emit_group_proj(g, fence, psd_in):
        for (wsb, wt) in ((wk_sb, WK_T), (wv_sb, WV_T)):
            iv = ap3(wt, 256 * g, [[D, 128], [128 * D, 8], [1, 256]])
            ov = ap3(wsb, 0, [[2048, 128], [256, 8], [1, 256]])
            sch.ev("gpsimd", lambda e, iv=iv, ov=ov: e.dma_start(out=ov, in_=iv),
                   waits=list(psd_in), inc="sGP", k=16)
        w_w = sch.ctr["sGP"]
        mm_done = {}
        kb_free = {"A0": fence, "A1": fence, "B0": fence, "B1": fence, "C0": fence, "C1": fence}
        kv_mile = {}
        ln_ld = {}
        for jb in range(JB):
            buf = lnT[jb % 2]
            w_ln = list(psd_in)
            if jb >= 2:
                w_ln = [("sPE", mm_done[jb - 2])]
            lsem = f"sLN{jb % 2}"
            for dc in range(8):
                iv = ap3(ln1_dram, jb * 512 * D + dc * 128, [[D, 512], [1, 128]])
                sch.ev("sync" if dc < 4 else "scalar",
                       lambda e, buf=buf, dc=dc, iv=iv: e.dma_start_transpose(
                           out=buf[:, dc * 512:(dc + 1) * 512], in_=iv),
                       waits=w_ln, inc=lsem, k=16)
            ln_ld[jb] = sch.ctr[lsem]
            for fc in range(2):
                w_mm = [(lsem, ln_ld[jb]), ("sGP", w_w)] + kb_free[f"A{fc}"]
                for dc in range(8):
                    sch.ev("tensor", lambda e, fc=fc, dc=dc, buf=buf: e.matmul(
                        psA[:, fc * 512:(fc + 1) * 512],
                        wk_sb[:, dc * 256 + fc * 128: dc * 256 + fc * 128 + 128],
                        buf[:, dc * 512:(dc + 1) * 512],
                        start=(dc == 0), stop=(dc == 7)),
                           waits=(w_mm if dc == 0 else []), inc=("sPE" if dc == 7 else None))
                p_mm = sch.ctr["sPE"]
                hhA, hhB = 2 * fc, 2 * fc + 1
                bcol = 8 + 2 * g + fc
                aev = sch.ev("scalar", lambda e, hhA=hhA, fc=fc, jb=jb, bcol=bcol: e.activation(
                    K_aug[hhA][0:64, jb * 512:(jb + 1) * 512], psA[0:64, fc * 512:(fc + 1) * 512],
                    AF.Identity, bias=bqkv_sb[0:64, bcol:bcol + 1]),
                       waits=[("sPE", p_mm)], inc="sACT")
                dev = sch.ev("vector", lambda e, hhB=hhB, fc=fc, jb=jb, bcol=bcol: e.tensor_scalar(
                    K_aug[hhB][0:64, jb * 512:(jb + 1) * 512], psA[64:128, fc * 512:(fc + 1) * 512],
                    bqkv_sb[64:128, bcol:bcol + 1], None, ALU.add),
                       waits=[("sPE", p_mm)], inc="sDVE")
                kb_free[f"A{fc}"] = [("sACT", aev), ("sDVE", dev)]
            for sub in range(4):
                jc = 4 * jb + sub
                half = sub % 2
                w_mm = [(lsem, ln_ld[jb]), ("sGP", w_w)] + kb_free[f"B{half}"]
                for dc in range(8):
                    sch.ev("tensor", lambda e, half=half, dc=dc, sub=sub, buf=buf: e.matmul(
                        psB[:, half * 512: half * 512 + 256],
                        buf[:, dc * 512 + sub * 128: dc * 512 + sub * 128 + 128],
                        wv_sb[:, dc * 256:(dc + 1) * 256],
                        start=(dc == 0), stop=(dc == 7)),
                           waits=(w_mm if dc == 0 else []), inc=("sPE" if dc == 7 else None))
                p_mm = sch.ctr["sPE"]
                srcv = ap3(psB, half * 512, [[1024, 128], [64, 4], [1, 64]])
                dstv = ap3(V_sb, jc * 260, [[64 * 260, 128], [65, 4], [1, 64]])
                dev = sch.ev("vector", lambda e, srcv=srcv, dstv=dstv: e.tensor_copy(dstv, srcv),
                             waits=[("sPE", p_mm)], inc="sDVE")
                kb_free[f"B{half}"] = [("sDVE", dev)]
            mm_done[jb] = sch.ctr["sPE"]
        ov = ap3(V_sb, 64, [[64 * 260, 128], [260, 64], [65, 4], [1, 1]])
        v_ones = sch.ev("vector", lambda e, ov=ov: e.memset(ov, 1.0),
                        waits=fence, inc="sDVE")
        ivq = ap3(WQ_T, 256 * g, [[D, 128], [128 * D, 8], [1, 256]])
        ovq = ap3(wq_sb, 0, [[2048, 128], [256, 8], [1, 256]])
        sch.ev("gpsimd", lambda e, ivq=ivq, ovq=ovq: e.dma_start(out=ovq, in_=ivq),
               waits=[("sPE", mm_done[JB - 1])], inc="sGP", k=16)
        w_wq = sch.ctr["sGP"]
        psDb = psD.bitcast(BF16)
        q_copy = {}
        q_mm = {}
        pool_free = {0: list(psd_in) + fence, 1: list(psd_in) + fence}
        region_free = {0: list(psd_in), 1: list(psd_in)}
        for qb in range(QBn):
            buf = lnT[qb % 2]
            w_ln = ([("sPE", mm_done[JB - 2 + (qb % 2)])] if qb < 2
                    else [("sPE", q_mm[qb - 2])])
            for j in range(4):
                k = qb * 4 + j
                lsem = f"sLN{k % 2}"
                tok_buf = P_pool[k % 2]
                iv = ap3(ln1own_dram, (qb * 512 + j * 128) * D, [[D, 128], [1, D]])
                sch.ev("sync" if k % 2 == 0 else "scalar",
                       lambda e, tok_buf=tok_buf, iv=iv: e.dma_start(out=tok_buf, in_=iv),
                       waits=pool_free[k % 2], inc=lsem, k=16)
                ld_k = sch.ctr[lsem]
                p_tr = None
                for dc in range(8):
                    r = (k * 8 + dc) % 2
                    p_tr = sch.ev("tensor", lambda e, tok_buf=tok_buf, dc=dc, r=r: e.transpose(
                        psDb[:, r * 1024:r * 1024 + 128],
                        tok_buf[:, dc * 128:(dc + 1) * 128], identb_sb),
                                  waits=[(lsem, ld_k)] + region_free[r], inc="sPE")
                    d_c = sch.ev("vector", lambda e, buf=buf, dc=dc, j=j, r=r: e.tensor_copy(
                        buf[:, dc * 512 + j * 128:dc * 512 + j * 128 + 128],
                        psDb[:, r * 1024:r * 1024 + 128]),
                                 waits=[("sPE", p_tr)] + w_ln, inc="sDVE")
                    region_free[r] = [("sDVE", d_c)]
                pool_free[k % 2] = [("sPE", p_tr)]
            q_copy[qb] = sch.ctr["sDVE"]
            for fc in range(2):
                w_mm = [("sDVE", q_copy[qb]), ("sGP", w_wq)] + kb_free[f"C{fc}"]
                for dc in range(8):
                    sch.ev("tensor", lambda e, fc=fc, dc=dc, buf=buf: e.matmul(
                        psC[:, fc * 512:(fc + 1) * 512],
                        wq_sb[:, dc * 256 + fc * 128: dc * 256 + fc * 128 + 128],
                        buf[:, dc * 512:(dc + 1) * 512],
                        start=(dc == 0), stop=(dc == 7)),
                           waits=(w_mm if dc == 0 else []), inc=("sPE" if dc == 7 else None))
                p_mm = sch.ctr["sPE"]
                aev = sch.ev("scalar", lambda e, fc=fc, qb=qb, g=g: e.activation(
                    Q_sb[:, fc * TPC + qb * 512: fc * TPC + qb * 512 + 512],
                    psC[:, fc * 512:(fc + 1) * 512],
                    AF.Identity, scale=0.125, bias=bq8_sb[:, 2 * g + fc:2 * g + fc + 1]),
                       waits=[("sPE", p_mm)], inc="sACT")
                kb_free[f"C{fc}"] = [("sACT", aev)]
            q_mm[qb] = sch.ctr["sPE"]
        kv_ready = {"k": [("sACT", sch.ctr["sACT"]), ("sDVE", sch.ctr["sDVE"])],
                    "q": [("sACT", sch.ctr["sACT"])],
                    "v": [("sDVE", max(v_ones, sch.ctr["sDVE"]))],
                    "psd": [("sPE", sch.ctr["sPE"]), ("sDVE", sch.ctr["sDVE"])]}
        return kv_ready

    def emit_head_prep(g, hh, kv_ready, psfree, guards, out):
        h = 4 * g + hh
        qaug = qaugs[hh % 2]
        QSEM, RPSEM = QSEMS[hh], RPSEMS[hh]
        fc, rhalf = hh // 2, 64 * (hh % 2)
        wq = kv_ready["q"] + kv_ready["k"] + list(guards)
        d_q = sch.ev("vector", lambda e, fc=fc, rhalf=rhalf: e.tensor_copy(
            qh1[0:64, :], Q_sb[rhalf:rhalf + 64, fc * TPC:(fc + 1) * TPC]),
                     waits=wq, inc="sDVE")
        d_q = sch.ev("vector", lambda e: e.memset(qh1[64:65, :], 1.0), waits=wq, inc="sDVE")
        for t in "LBR":
            sch.ev("vector", lambda e, t=t, fc=fc, rhalf=rhalf: e.tensor_copy(
                qaug[t][0:64, :], Q_sb[rhalf:rhalf + 64, fc * TPC:(fc + 1) * TPC]),
                   waits=wq, inc="sDVE")
        d_base = sch.ctr["sDVE"]
        out["qaug"] = [("sDVE", d_base)]
        yield
        bank0_free = list(psfree)
        d_ks = sch.ev("vector", lambda e, hh=hh: e.tensor_copy(
            ksub[:, 0:256], ap3(K_aug[hh], 0, [[S, 64], [32, 256]])),
                      waits=kv_ready["k"] + list(guards), inc="sDVE")
        mg = 192 * h + 64
        d_ks = sch.ev("vector", lambda e, hh=hh, mg=mg: e.tensor_copy(
            ksub[:, 256:320], K_aug[hh][0:64, mg:mg + 64]),
                      waits=kv_ready["k"] + list(guards) + [("sDVE", d_ks)], inc="sDVE")
        tr_free = list(psfree)
        mm_free = list(bank0_free)
        for qc in range(16):
            p_m = sch.ev("tensor", lambda e, qc=qc: e.matmul(
                psD[:, 0:320], qh1[0:64, qc * 128:(qc + 1) * 128], ksub,
                start=True, stop=True),
                         waits=mm_free + [("sDVE", d_ks), ("sDVE", d_q)], inc="sPE")
            d_m = sch.ev("vector", lambda e: e.reduce_max(maxk_col2, psD[:, 0:320], axis=AX.X),
                         waits=[("sPE", p_m)], inc="sDVE")
            mm_free = [("sDVE", d_m)]
            yield
            p_tt = sch.ev("tensor", lambda e: e.transpose(psD[0:1, 640:768], maxk_col2,
                                                          ident_sb),
                          waits=[("sDVE", d_m)] + tr_free, inc="sPE")
            d_mc = sch.ev("vector", lambda e, qc=qc: e.tensor_copy(
                tmp_row[0:1, qc * 128:(qc + 1) * 128], psD[0:1, 640:768]),
                          waits=[("sPE", p_tt)], inc="sDVE")
            tr_free = [("sDVE", d_mc)]
            yield
        rfree = mm_free + tr_free
        ctr_free = list(guards)
        r01_free = list(guards)
        r01_psum = list(psfree)
        pe_w = rfree
        mp_free = tr_free
        for qic in range(16):
            p_rp = sch.ev("tensor", lambda e, qic=qic: e.matmul(
                psD[:, 0:261], qh1[:, qic * 128:(qic + 1) * 128], wrbc8[:, 0:261],
                start=True, stop=True),
                          waits=pe_w + [("sDVE", d_q)], inc="sPE")
            p_r01 = sch.ev("tensor", lambda e, qic=qic: e.matmul(
                psD[:, 512:514], qh1[:, qic * 128:(qic + 1) * 128], wr01_8[:, 0:2],
                start=True, stop=True), waits=r01_psum, inc="sPE")
            a_c = sch.ev("scalar", lambda e: e.activation(
                rp_pad_sb[:, 640:901], psD[:, 0:261], AF.Copy),
                         waits=[("sPE", p_r01)] + ctr_free, inc="sACT")
            pe_w = [("sACT", a_c)]
            d_mp = sch.ev("vector", lambda e: e.reduce_max(maxk_col2, psD[:, 0:261], axis=AX.X),
                          waits=[("sPE", p_r01), ("sACT", a_c)], inc="sDVE")
            yield
            p_tp = sch.ev("tensor", lambda e: e.transpose(psD[0:1, 640:768], maxk_col2,
                                                          ident_sb),
                          waits=[("sDVE", d_mp)] + mp_free, inc="sPE")
            d_mq = sch.ev("vector", lambda e, qic=qic: e.tensor_copy(
                qn_row[0:1, qic * 128:(qic + 1) * 128], psD[0:1, 640:768]),
                          waits=[("sPE", p_tp)], inc="sDVE")
            mp_free = [("sDVE", d_mq)]
            pe_w.append(("sPE", p_tp))
            d_c2 = sch.ev("vector", lambda e: e.tensor_copy(rp01[:, 0:2], psD[:, 512:514]),
                          waits=[("sPE", p_tp)] + r01_free, inc="sDVE")
            d_l = sch.ev("vector", lambda e: e.tensor_copy(
                rp_pad_sb[:, 0:640], ap3(rp01, 0, [[2, 128], [0, 640]])),
                         waits=[("sDVE", d_c2)] + ctr_free, inc="sDVE")
            d_r2 = sch.ev("vector", lambda e: e.tensor_copy(
                rp_pad_sb[:, 901:W_PAD], ap3(rp01, 1, [[2, 128], [0, W_PAD - 901]])),
                         waits=[("sDVE", d_l)], inc="sDVE")
            pe_w.append(("sDVE", d_r2))
            ov = ap3(rp_dram[hh], qic * 128 * W_PAD, [[W_PAD, 128], [1, W_PAD]])
            gc = sch.ev("sync", lambda e, ov=ov: e.dma_start(out=ov, in_=rp_pad_sb),
                        waits=[("sACT", a_c), ("sDVE", d_r2)], inc=RPSEM, k=16)
            ctr_free = [(RPSEM, gc)]
            r01_free = [(RPSEM, gc)]
            r01_psum = [("sDVE", d_c2), ("sDVE", d_mq)]
            yield
        out["rp_stored"] = [(RPSEM, sch.ctr[RPSEM])]
        d_c = sch.ev("vector", lambda e: e.scalar_tensor_tensor(
            c_row, tmp_row, 8.0, qn_row, ALU.add, ALU.add),
                     waits=mp_free + [("sDVE", sch.ctr["sDVE"])], inc="sDVE")
        rfree2 = pe_w
        for qb in range(QBn):
            for col in range(2):
                row = rp0_row if col == 0 else rp256_row
                p_r = sch.ev("tensor", lambda e, qb=qb, col=col: e.matmul(
                    psD[0:1, 0:512], wr01_8[:, col:col + 1], qh1[:, qb * 512:(qb + 1) * 512],
                    start=True, stop=True),
                             waits=rfree2 + [("sDVE", d_q)], inc="sPE")
                d_r = sch.ev("vector", lambda e, qb=qb, row=row: e.tensor_copy(
                    row[0:1, qb * 512:(qb + 1) * 512], psD[0:1, 0:512]),
                             waits=[("sPE", p_r)], inc="sDVE")
                rfree2 = [("sDVE", d_r)]
                yield
        rfree = rfree2
        t_marks = []
        qn_free = [("sDVE", d_c)]
        for t, mk, fine_row in (("L", rp0_row, rp0_row), ("R", rp256_row, rp256_row),
                                ("B", None, rp0_row)):
            if mk is None:
                d_t = sch.ev("vector", lambda e: e.tensor_scalar(
                    tmp_row, c_row, -1.0, None, ALU.mult),
                             waits=[("sDVE", d_c)], inc="sDVE")
            else:
                d_t = sch.ev("vector", lambda e, mk=mk: e.tensor_sub(tmp_row, mk, c_row),
                             waits=rfree + [("sDVE", d_c)], inc="sDVE")
            d_co = sch.ev("vector", lambda e: e.tensor_copy(qn2_row, tmp_row),
                          waits=[("sDVE", d_t)] + qn_free, inc="sDVE")
            g_co = sch.ev("gpsimd", lambda e, t=t: e.dma_start(
                out=qaug[t][64:65, :], in_=qn2_row), waits=[("sDVE", d_co)], inc=QSEM, k=16)
            d_f = sch.ev("vector", lambda e, fine_row=fine_row: e.tensor_sub(
                fine_row, tmp_row, qn2_row), waits=[("sDVE", d_co)], inc="sDVE")
            g_f = sch.ev("gpsimd", lambda e, t=t, fine_row=fine_row: e.dma_start(
                out=qaug[t][65:66, :], in_=fine_row),
                         waits=[("sDVE", d_f)], inc=QSEM, k=16)
            qn_free = [(QSEM, g_f)]
            t_marks.append((QSEM, g_f))
            yield
        out["qaug"] = t_marks + [("sDVE", d_base)]
        out["rp_stored"] = [(RPSEM, sch.ctr[RPSEM])]
        out["psD_user"] = pe_w + rfree + mp_free
        out["tail"] = [("sDVE", sch.ctr["sDVE"]), ("sACT", sch.ctr["sACT"]),
                       ("sPE", sch.ctr["sPE"]), (RPSEM, sch.ctr[RPSEM]),
                       (QSEM, sch.ctr[QSEM])]

    ot_chain = {}

    def emit_head_attn(g, hh, prep, sfree_in, ot_free, pos_entry, bg=None):
        h = 4 * g + hh
        qaug = qaugs[hh % 2]
        sbanks = [psA, psB]
        ot_dma_free = ot_chain
        s_free = {0: list(sfree_in[0]), 1: list(sfree_in[1])}
        exp_mile = {}
        pv_mile = {}
        pos_ld = {}
        pos_free = {}
        s_stop = {}
        O_T = psC
        ot_in = list(ot_free)
        for qbp in range(2):

            def load_pos(qic):
                p_first = max(0, 4 * qic - 1)
                n_p = min(63, 4 * qic + 4) - p_first + 1
                width = 128 * n_p
                off = qic * 128 * W_PAD + KC + (128 * p_first - 512 * qic)
                iv = ap3(rp_dram[hh], off, [[W_PAD - 4, 128], [1, width]])
                buf = pos_pool[qic % 2]
                psem = f"sPOS{qic % 2}"
                w = prep["rp_stored"] + pos_free.get(qic - 2, list(pos_entry))
                sch.ev("gpsimd", lambda e, buf=buf, iv=iv, width=width: e.dma_start(
                    out=buf[:, 0:width], in_=iv), waits=w, inc=psem, k=16)
                pos_ld[qic] = (psem, sch.ctr[psem])

            qic_lo, qic_hi = 8 * qbp, 8 * qbp + 8

            def emit_S(p, qbp=qbp, qic_lo=qic_lo, qic_hi=qic_hi):
                if qbp == 0 and p == 0:
                    load_pos(0)
                for qic in range(qic_lo, qic_hi):
                    if qic >= 1 and max(1, 4 * qic - 3) == p and qic not in pos_ld:
                        load_pos(qic)
                bank = sbanks[p % 2]
                ops = []
                for qb2 in range(2):
                    qb = 2 * qbp + qb2
                    cls = classify_qb(p, qb)
                    if cls in "LR":
                        ops.append(("far512", qb2, cls, None))
                    else:
                        for qic in range(4 * qb, 4 * qb + 4):
                            c = classify(p, qic)
                            if c == 'S':
                                ops.append(("band", qb2, c, qic))
                            else:
                                ops.append(("far128", qb2, c, qic))
                started = {0: False, 1: False}
                n_emitted = 0
                w_first = s_free[p % 2] + prep["qaug"]
                pos_w = sorted({o[3] for o in ops if o[0] == "band"})
                for qic in pos_w:
                    w_first = w_first + [pos_ld[qic]]
                last_by_half = {}
                for idx, (kind, qb2, c, qic) in enumerate(ops):
                    if kind == "band":
                        last_by_half[qb2] = idx
                    else:
                        last_by_half[qb2] = idx
                seen = {0: 0, 1: 0}
                tot = {0: sum(1 for o in ops if o[1] == 0) + sum(1 for o in ops if o[1] == 0 and o[0] == "band"),
                       1: sum(1 for o in ops if o[1] == 1) + sum(1 for o in ops if o[1] == 1 and o[0] == "band")}
                for (kind, qb2, c, qic) in ops:
                    qb = 2 * qbp + qb2
                    first = not started[qb2]
                    started[qb2] = True
                    if kind == "far512":
                        seen[qb2] += 1
                        stop = seen[qb2] == tot[qb2]
                        sch.ev("tensor", lambda e, bank=bank, qb2=qb2, c=c, qb=qb, p=p, hh=hh, first=first, stop=stop: e.matmul(
                            bank[:, qb2 * 512:(qb2 + 1) * 512],
                            K_aug[hh][:, p * 128:(p + 1) * 128],
                            qaug[c][:, qb * 512:(qb + 1) * 512],
                            start=first, stop=stop),
                               waits=(w_first if n_emitted == 0 else []),
                               inc=None)
                        n_emitted += 1
                    elif kind == "far128":
                        seen[qb2] += 1
                        stop = seen[qb2] == tot[qb2]
                        col = qb2 * 512 + (qic - 4 * qb) * 128
                        sch.ev("tensor", lambda e, bank=bank, col=col, c=c, qic=qic, p=p, hh=hh, first=first, stop=stop: e.matmul(
                            bank[:, col:col + 128],
                            K_aug[hh][:, p * 128:(p + 1) * 128],
                            qaug[c][:, qic * 128:(qic + 1) * 128],
                            start=first, stop=stop),
                               waits=(w_first if n_emitted == 0 else []),
                               inc=None)
                        n_emitted += 1
                    else:
                        col = qb2 * 512 + (qic - 4 * qb) * 128
                        p_first_q = max(0, 4 * qic - 1)
                        pcol = (p - p_first_q) * 128
                        buf = pos_pool[qic % 2]
                        seen[qb2] += 2
                        stop = seen[qb2] == tot[qb2]
                        sch.ev("tensor", lambda e, bank=bank, col=col, buf=buf, pcol=pcol, first=first: e.matmul(
                            bank[:, col:col + 128], buf[:, pcol:pcol + 128], ident_sb,
                            start=first, stop=False, is_transpose=True),
                               waits=(w_first if n_emitted == 0 else []),
                               inc=None)
                        n_emitted += 1
                        sch.ev("tensor", lambda e, bank=bank, col=col, qic=qic, p=p, hh=hh, stop=stop: e.matmul(
                            bank[:, col:col + 128],
                            K_aug[hh][:, p * 128:(p + 1) * 128],
                            qaug["B"][:, qic * 128:(qic + 1) * 128],
                            start=False, stop=stop),
                               inc=None)
                        n_emitted += 1
                eng, fn, wts, inc0, k0 = sch.events[-1]
                assert eng == "tensor" and inc0 is None
                sch.ctr["sPE"] += 1
                sch.events[-1] = (eng, fn, wts, "sPE", 1)
                p_s = sch.ctr["sPE"]
                for qic in pos_w:
                    pos_free[qic] = [("sPE", p_s)]
                s_stop[p] = p_s

            def emit_exp(p):
                w_exp = [("sPE", s_stop[p])] + pv_mile.get(p - 2, [])
                a_e = sch.ev("scalar", lambda e, bank=sbanks[p % 2], p=p: e.activation(
                    P_pool[p % 2], bank, AF.Exp), waits=w_exp, inc="sACT")
                exp_mile[p] = a_e
                s_free[p % 2] = [("sACT", a_e)]

            def emit_PV(p):
                for qb2 in range(2):
                    sch.ev("tensor", lambda e, qb2=qb2, p=p, hh=hh: e.matmul(
                        O_T[0:65, qb2 * 512:(qb2 + 1) * 512],
                        V_sb[:, p * 260 + hh * 65: p * 260 + hh * 65 + 65],
                        P_pool[p % 2][:, qb2 * 512:(qb2 + 1) * 512],
                        start=(p == 0), stop=(p == 63)),
                           waits=([("sACT", exp_mile[p])] + (ot_in if p == 0 else []))
                                 if qb2 == 0 else [],
                           inc=("sPE" if qb2 == 1 else None))
                pv_mile[p] = [("sPE", sch.ctr["sPE"])]

            for it in range(66):
                if it <= 63:
                    emit_S(it)
                if 0 <= it - 1 <= 63:
                    emit_exp(it - 1)
                if 0 <= it - 2 <= 63:
                    emit_PV(it - 2)
                if bg is not None:
                    next(bg, None)
            pv_last = sch.ctr["sPE"]
            ev_marks = []
            bcast_free = []
            for qb2 in range(2):
                qb = 2 * qbp + qb2
                def _recip(e, qb2=qb2):
                    with nc.allow_low_precision("bf16 recip of softmax denominators is within tolerance"):
                        return e.reciprocal(rcp_sb[64:65, 0:512],
                                            O_T[64:65, qb2 * 512:(qb2 + 1) * 512])
                d_r = sch.ev("vector", _recip, waits=[("sPE", pv_last)] + bcast_free,
                             inc="sDVE")
                p_b = sch.ev("tensor", lambda e, qb2=qb2: e.matmul(
                    O_T[64:128, qb2 * 512:(qb2 + 1) * 512],
                    onesrow65[64:65, 0:64],
                    rcp_sb[64:65, 0:512],
                    start=True, stop=True),
                             waits=[("sDVE", d_r)] + bcast_free, inc="sPE")
                d_b = sch.ev("vector", lambda e, qb2=qb2: e.tensor_copy(
                    rb_sb, O_T[64:128, qb2 * 512:(qb2 + 1) * 512]),
                             waits=[("sPE", p_b)] + ot_dma_free.get("rb", []), inc="sDVE")
                bcast_free = [("sDVE", d_b)]
                d_o = sch.ev("vector", lambda e, qb2=qb2: e.scalar_tensor_tensor(
                    o_tile,
                    O_T[0:64, qb2 * 512:(qb2 + 1) * 512],
                    1.0, rb_sb, ALU.mult, ALU.mult),
                             waits=[("sDVE", d_b)] + ot_dma_free.get("o", []), inc="sDVE")
                ot_dma_free["rb"] = [("sDVE", d_o)]
                ov = ap3(o_dram, (64 * h) * TPC + qb * 512, [[TPC, 64], [1, 512]])
                g_o = sch.ev("sync", lambda e, ov=ov: e.dma_start(
                    out=ov, in_=o_tile),
                             waits=[("sDVE", d_o)], inc="sSC", k=16)
                ot_dma_free["o"] = [("sSC", g_o)]
                ev_marks.append(("sDVE", d_o))
                ev_marks.append(("sPE", p_b))
            ot_in = list(ev_marks)
        if bg is not None:
            for _ in bg:
                pass
        return {"s_free": [s_free[0], s_free[1]], "ot_done": list(ev_marks),
                "pv_last": [("sPE", sch.ctr["sPE"])],
                "exp_done": [("sACT", sch.ctr["sACT"])],
                "o_stored": [("sSC", sch.ctr["sSC"])]}


    def emit_pc_pd(attn_done):
        es_pc = contextlib.ExitStack()
        sbc = lambda name, shape, dt: es_pc.enter_context(nc.sbuf_tensor(name, shape, dt)).ap()
        x2 = [sbc(f"x2_{dc}", [128, TPC], F32) for dc in range(8)]
        ln2 = [sbc(f"ln2_{dc}", [128, 1024], BF16) for dc in range(8)]
        g_sb = sbc("g_sb", [128, 16 * 1024], BF16)
        w3c = [sbc(f"w3c{i}", [128, 4 * 1024], BF16) for i in range(2)]
        w1c4 = [sbc(f"w1c4_{i}", [128, 8 * 512], BF16) for i in range(2)]
        w2c4 = [sbc(f"w2c4_{i}", [128, 8 * 512], BF16) for i in range(2)]
        wb2 = [sbc(f"wb2_{i}", [128, 1024], BF16) for i in range(2)]
        s_tmp = sbc("s_tmp", [128, 1024], BF16)
        sq_scr = sbc("sq_scr", [128, 1024], F32)
        gl = sbc("gl", [128, 8], F32)
        gl2 = sbc("gl2", [128, 8], F32)
        gl3 = sbc("gl3", [128, 8], F32)
        gcol = sbc("gcol", [128, 6], F32)
        wtok = sbc("wtok", [128, 8], BF16)
        w8_rows = sbc("w8_rows", [8, 1024], BF16)
        ident_bf = sbc("ident_bf", [128, 128], BF16)
        onesrow_f = sbc("onesrow_f", [1, 128], F32)
        mean_row = sbc("mean_row", [1, TPC], F32)
        rstd_row = sbc("rstd_row", [1, TPC], F32)
        tmp2_row = sbc("tmp2_row", [1, TPC], F32)
        we_dram = nc.dram_tensor("we_dram", [E, TPC], BF16, kind="Internal")

        d0 = sch.ev("vector", lambda e: e.tensor_copy(ident_bf, ident_sb),
                    waits=attn_done, inc="sDVE")
        sch.ev("vector", lambda e: e.memset(onesrow_f, 1.0), inc="sDVE")
        consts_d = sch.ctr["sDVE"]

        x_mark = {}
        for dc in range(8):
            iv = ap3(xT_own, dc * 128 * TPC, [[TPC, 128], [1, TPC]])
            sch.ev("sync" if dc % 2 == 0 else "scalar",
                   lambda e, iv=iv, dc=dc: e.dma_start(out=x2[dc], in_=iv),
                   waits=attn_done, inc="sLD", k=16)
            x_mark[dc] = ("sLD", sch.ctr["sLD"])
        o_stage = [s_tmp, sq_scr]
        stt_mark = {}
        for k in range(16):
            dc, hf = k // 2, k % 2
            ov = ap3(o_dram, dc * 128 * TPC + hf * 1024, [[TPC, 128], [1, 1024]])
            w = ([stt_mark[k - 2]] if k >= 2 else list(attn_done))
            sch.ev("gpsimd", lambda e, ov=ov, k=k: e.dma_start(
                out=o_stage[k % 2][:, 0:1024], in_=ov),
                   waits=w, inc="sGP", k=16)
            o_mark = ("sGP", sch.ctr["sGP"])
            d = sch.ev("vector", lambda e, dc=dc, hf=hf, k=k: e.scalar_tensor_tensor(
                x2[dc][:, hf * 1024:(hf + 1) * 1024],
                o_stage[k % 2][:, 0:1024],
                bqkv_sb[:, 16 + dc:17 + dc],
                x2[dc][:, hf * 1024:(hf + 1) * 1024], ALU.add, ALU.add),
                       waits=[o_mark, x_mark[dc], ("sDVE", consts_d)], inc="sDVE")
            stt_mark[k] = ("sDVE", d)
        x2_all = [("sDVE", sch.ctr["sDVE"])]

        a_prev = []
        sqh_prev = {0: x2_all, 1: x2_all}
        for qb in range(QBn):
            for dc in range(8):
                sch.ev("tensor", lambda e, qb=qb, dc=dc: e.matmul(
                    psA[0:1, 0:512], ones_sb, x2[dc][:, qb * 512:(qb + 1) * 512],
                    start=(dc == 0), stop=(dc == 7)),
                       waits=(x2_all + a_prev if dc == 0 else []),
                       inc=("sPE" if dc == 7 else None))
            p_sum = sch.ctr["sPE"]
            for dc in range(8):
                h = dc % 2
                a_s = sch.ev("scalar", lambda e, qb=qb, dc=dc, h=h: e.activation(
                    sq_scr[:, h * 512:(h + 1) * 512],
                    x2[dc][:, qb * 512:(qb + 1) * 512], AF.Square),
                             waits=sqh_prev[h] + (a_prev if dc == 0 else []), inc="sACT")
                m_s = sch.ev("tensor", lambda e, dc=dc, h=h: e.matmul(
                    psB[0:1, 0:512], ones_sb, sq_scr[:, h * 512:(h + 1) * 512],
                    start=(dc == 0), stop=(dc == 7)),
                       waits=[("sACT", a_s)], inc="sPE")
                sqh_prev[h] = [("sPE", m_s)]
            p_sq = sch.ctr["sPE"]
            d_m = sch.ev("vector", lambda e, qb=qb: e.tensor_scalar(
                mean_row[0:1, qb * 512:(qb + 1) * 512], psA[0:1, 0:512],
                1.0 / D, None, ALU.mult),
                         waits=[("sPE", p_sum)], inc="sDVE")
            d_v = sch.ev("vector", lambda e, qb=qb: e.tensor_scalar(
                tmp2_row[0:1, qb * 512:(qb + 1) * 512], psB[0:1, 0:512],
                1.0 / D, None, ALU.mult),
                         waits=[("sPE", p_sq), ("sDVE", d_m)], inc="sDVE")
            a_prev = [("sDVE", d_v)]
        d_m2r = sch.ev("vector", lambda e: e.tensor_mul(rstd_row, mean_row, mean_row),
                       waits=a_prev, inc="sDVE")
        d_var = sch.ev("vector", lambda e: e.tensor_sub(tmp2_row, tmp2_row, rstd_row),
                       waits=[("sDVE", d_m2r)], inc="sDVE")
        a_sr = sch.ev("scalar", lambda e: e.activation(rstd_row, tmp2_row, AF.Sqrt,
                                                       bias=constcol(EPS)[0:1, :]),
                      waits=[("sDVE", d_var)], inc="sACT")
        d_rs = sch.ev("vector", lambda e: e.reciprocal(tmp2_row, rstd_row),
                      waits=[("sACT", a_sr)], inc="sDVE")

        psA_free = [("sDVE", sch.ctr["sDVE"]), ("sPE", sch.ctr["sPE"])]
        psB_free = list(psA_free)
        regs = [(psC, 0), (psC, 512), (psD, 0), (psD, 512)]
        regs_free = [list(psA_free) for _ in range(4)]
        gl_free = []
        tr_free = []
        sq_free = [("sDVE", sch.ctr["sDVE"])]
        stmp_free = []
        x2_add_done = []
        we_done = {}
        ln2_done = []
        bc_free = psA_free
        ln2_used = []

        NQH = (E + 1) * 4
        NKH = (E + 1) * 4
        w12_ld, w12_freed = {}, {}
        w3_ld, w3_freed = {}, {}
        wb_ld, wb_freed = {}, {}
        gch_free = {0: list(attn_done), 1: list(attn_done)}

        def emit_w12_load(q):
            halfq, r = divmod(q, NQH)
            e_, hcg = divmod(r, 4)
            if e_ == E:
                i1 = ap3(sw1, hcg * 512, [[HF, 128], [128 * HF, 8], [1, 512]])
                i2 = ap3(sw2, hcg * 512, [[HF, 128], [128 * HF, 8], [1, 512]])
            else:
                off = e_ * D * HF + hcg * 512
                i1 = ap3(ew1, off, [[HF, 128], [128 * HF, 8], [1, 512]])
                i2 = ap3(ew2, off, [[HF, 128], [128 * HF, 8], [1, 512]])
            par = q % 2
            wsem = f"sW12{par}"
            od = [[4096, 128], [512, 8], [1, 512]]
            w = w12_freed.get(q - 2, list(attn_done))
            sch.ev("gpsimd", lambda e, i1=i1, par=par, od=od: e.dma_start(
                out=ap3(w1c4[par], 0, od), in_=i1), waits=w, inc=wsem, k=16)
            sch.ev("gpsimd", lambda e, i2=i2, par=par, od=od: e.dma_start(
                out=ap3(w2c4[par], 0, od), in_=i2), inc=wsem, k=16)
            w12_ld[q] = (wsem, sch.ctr[wsem])

        def emit_w3_load(k):
            halfk, r = divmod(k, NKH)
            e_, ch = divmod(r, 4)
            if e_ == E:
                iv = ap3(sw3, ch * 4 * 128 * D, [[D, 128], [128 * D, 4], [1, D]])
            else:
                iv = ap3(ew3, e_ * HF * D + ch * 4 * 128 * D,
                         [[D, 128], [128 * D, 4], [1, D]])
            par = k % 2
            ksem = f"sW3{par}"
            w = w3_freed.get(k - 2, list(attn_done))
            sch.ev("gpsimd", lambda e, iv=iv, par=par: e.dma_start(
                out=w3c[par], in_=iv), waits=w, inc=ksem, k=16)
            w3_ld[k] = (ksem, sch.ctr[ksem])

        def emit_wb_load(j, half):
            e_ = j - half * E
            par = j % 2
            bsem = f"sWB{par}"
            wv = ap3(we_dram, e_ * TPC + half * 1024, [[0, 128], [1, 1024]])
            w = wb_freed.get(j - 2, []) + we_done[half]
            sch.ev("gpsimd", lambda e, wv=wv, par=par: e.dma_start(
                out=wb2[par], in_=wv), waits=w, inc=bsem, k=16)
            wb_ld[j] = (bsem, sch.ctr[bsem])

        emit_w12_load(0)
        emit_w12_load(1)
        emit_w3_load(0)
        emit_w3_load(1)

        for half in range(2):
            toff = half * 1024
            for qbl in range(2):
                qb = 2 * half + qbl
                p_bm = sch.ev("tensor", lambda e, qb=qb: e.matmul(
                    psA[:, 0:512], onesrow_f, mean_row[0:1, qb * 512:(qb + 1) * 512],
                    start=True, stop=True),
                              waits=[("sDVE", d_rs), ("sDVE", consts_d)] + bc_free + psA_free,
                              inc="sPE")
                p_br = sch.ev("tensor", lambda e, qb=qb: e.matmul(
                    psB[:, 0:512], onesrow_f, tmp2_row[0:1, qb * 512:(qb + 1) * 512],
                    start=True, stop=True), waits=psB_free, inc="sPE")
                p_bc = sch.ctr["sPE"]
                d_prev = sq_free
                for dc in range(8):
                    d_s = sch.ev("vector", lambda e, dc=dc, qb=qb: e.tensor_sub(
                        sq_scr[:, 0:512], x2[dc][:, qb * 512:(qb + 1) * 512], psA[:, 0:512]),
                                 waits=[("sPE", p_bc)] + d_prev + ln2_used, inc="sDVE")
                    d_l = sch.ev("vector", lambda e, dc=dc, qbl=qbl: e.tensor_mul(
                        ln2[dc][:, qbl * 512:(qbl + 1) * 512], sq_scr[:, 0:512], psB[:, 0:512]),
                                 waits=[("sDVE", d_s)], inc="sDVE")
                    d_prev = [("sDVE", d_l)]
                sq_free = d_prev
                psA_free = d_prev
                psB_free = d_prev
                bc_free = d_prev
            ln2_done = [("sDVE", sch.ctr["sDVE"])]

            for tcl in range(8):
                for dc in range(8):
                    sch.ev("tensor", lambda e, tcl=tcl, dc=dc: e.matmul(
                        psC[:, 0:8], ln2[dc][:, tcl * 128:(tcl + 1) * 128],
                        gate_sb[:, dc * 8:(dc + 1) * 8],
                        start=(dc == 0), stop=(dc == 7)),
                           waits=(ln2_done + gl_free + regs_free[0] + [("sCW", 16)]
                                  if dc == 0 else []),
                           inc=("sPE" if dc == 7 else None))
                p_g = sch.ctr["sPE"]
                c1 = sch.ev("vector", lambda e: e.tensor_copy(gl, psC[:, 0:8]),
                            waits=[("sPE", p_g)], inc="sDVE")
                regs_free[0] = [("sDVE", c1)]
                gl_free = [("sDVE", c1)]
                c2 = sch.ev("vector", lambda e: e.reduce_max(gcol[:, 0:1], gl, axis=AX.X),
                            waits=[("sDVE", c1)], inc="sDVE")
                c3 = sch.ev("vector", lambda e: e.tensor_scalar(
                    gl2, gl, gcol[:, 0:1], None, ALU.is_ge), waits=[("sDVE", c2)], inc="sDVE")
                c4 = sch.ev("vector", lambda e: e.scalar_tensor_tensor(
                    gl2, gl2, -1e30, gl, ALU.mult, ALU.add), waits=[("sDVE", c3)], inc="sDVE")
                c5 = sch.ev("vector", lambda e: e.reduce_max(gcol[:, 1:2], gl2, axis=AX.X),
                            waits=[("sDVE", c4)], inc="sDVE")
                c6 = sch.ev("vector", lambda e: e.tensor_scalar(
                    gl2, gl, gcol[:, 1:2], None, ALU.is_ge), waits=[("sDVE", c5)], inc="sDVE")
                c7 = sch.ev("vector", lambda e: e.tensor_scalar(
                    gcol[:, 2:3], gcol[:, 0:1], -1.0, None, ALU.mult),
                            waits=[("sDVE", c2)], inc="sDVE")
                c8 = sch.ev("scalar", lambda e: e.activation(
                    gl3, gl, AF.Exp, bias=gcol[:, 2:3]),
                            waits=[("sDVE", c7), ("sDVE", c6)], inc="sACT")
                c9 = sch.ev("vector", lambda e: e.tensor_mul(gl3, gl3, gl2),
                            waits=[("sACT", c8)], inc="sDVE")
                c10 = sch.ev("vector", lambda e: e.reduce_sum(gcol[:, 3:4], gl3, axis=AX.X),
                             waits=[("sDVE", c9)], inc="sDVE")
                c11 = sch.ev("vector", lambda e: e.reciprocal(gcol[:, 4:5], gcol[:, 3:4]),
                             waits=[("sDVE", c10)], inc="sDVE")
                c12 = sch.ev("vector", lambda e: e.tensor_scalar(
                    wtok, gl3, gcol[:, 4:5], None, ALU.mult), waits=[("sDVE", c11)], inc="sDVE")
                p_t = sch.ev("tensor", lambda e: e.transpose(
                    psD.bitcast(BF16)[0:8, 0:128], wtok, ident_bf),
                             waits=[("sDVE", c12), ("sDVE", consts_d)] + tr_free + regs_free[2],
                             inc="sPE")
                c13 = sch.ev("vector", lambda e, tcl=tcl: e.tensor_copy(
                    w8_rows[:, tcl * 128:(tcl + 1) * 128], psD.bitcast(BF16)[0:8, 0:128]),
                             waits=[("sPE", p_t)], inc="sDVE")
                tr_free = [("sDVE", c13)]
                regs_free[2] = [("sDVE", c13)]
            w8_done = sch.ctr["sDVE"]
            ww = ap3(we_dram, toff, [[TPC, 8], [1, 1024]])
            m = sch.ev("gpsimd", lambda e, ww=ww: e.dma_start(out=ww, in_=w8_rows),
                       waits=[("sDVE", w8_done)], inc="sGP", k=16)
            we_done[half] = [("sGP", m)]

            for e_ in range(E + 1):
                shared = e_ == E
                j = half * E + e_
                if not shared:
                    emit_wb_load(j, half)
                g_ready = {}
                for hcg in range(4):
                    q = (half * (E + 1) + e_) * 4 + hcg
                    ld = w12_ld[q]
                    par = q % 2
                    for hh2 in range(4):
                        hc = hcg * 4 + hh2
                        co = hh2 * 128
                        for nb in range(2):
                            for dc in range(8):
                                sch.ev("tensor", lambda e, dc=dc, nb=nb, par=par, co=co: e.matmul(
                                    psA[:, nb * 512:(nb + 1) * 512],
                                    w1c4[par][:, dc * 512 + co:dc * 512 + co + 128],
                                    ln2[dc][:, nb * 512:(nb + 1) * 512],
                                    start=(dc == 0), stop=(dc == 7)),
                                       waits=(psA_free + [ld] + ln2_done
                                              if (nb == 0 and dc == 0) else []),
                                       inc=("sPE" if dc == 7 else None))
                        p_h1 = sch.ctr["sPE"]
                        for nb in range(2):
                            for dc in range(8):
                                sch.ev("tensor", lambda e, dc=dc, nb=nb, par=par, co=co: e.matmul(
                                    psB[:, nb * 512:(nb + 1) * 512],
                                    w2c4[par][:, dc * 512 + co:dc * 512 + co + 128],
                                    ln2[dc][:, nb * 512:(nb + 1) * 512],
                                    start=(dc == 0), stop=(dc == 7)),
                                       waits=(psB_free + [ld]
                                              if (nb == 0 and dc == 0) else []),
                                       inc=("sPE" if dc == 7 else None))
                        p_h2 = sch.ctr["sPE"]
                        a_si = sch.ev("scalar", lambda e: e.activation(s_tmp, psA, AF.Silu),
                                      waits=[("sPE", p_h1)] + stmp_free, inc="sACT")
                        d_g = sch.ev("vector", lambda e, hc=hc: e.tensor_mul(
                            g_sb[:, hc * 1024:(hc + 1) * 1024], s_tmp, psB),
                                     waits=[("sPE", p_h2), ("sACT", a_si)] + gch_free[hc // 8],
                                     inc="sDVE")
                        psA_free = [("sACT", a_si)]
                        psB_free = [("sDVE", d_g)]
                        stmp_free = [("sDVE", d_g)]
                        if not shared:
                            d_g = sch.ev("vector", lambda e, hc=hc, wpar=j % 2: e.tensor_mul(
                                g_sb[:, hc * 1024:(hc + 1) * 1024],
                                g_sb[:, hc * 1024:(hc + 1) * 1024], wb2[wpar]),
                                         waits=[wb_ld[j], ("sDVE", d_g)], inc="sDVE")
                        g_ready[hc] = d_g
                    w12_freed[q] = [("sPE", p_h2)]
                    if q + 2 < 2 * NQH:
                        emit_w12_load(q + 2)
                if not shared:
                    wb_freed[j] = [("sDVE", g_ready[15])]
                for ch in range(4):
                    k = (half * (E + 1) + e_) * 4 + ch
                    kpar = k % 2
                    gw = [("sDVE", g_ready[ch * 4 + 3]), w3_ld[k]]
                    for nb in range(2):
                        for dout in range(8):
                            r = (nb * 8 + dout) % 4
                            bank, cb = regs[r]
                            for i in range(4):
                                hc = ch * 4 + i
                                sch.ev("tensor", lambda e, i=i, hc=hc, dout=dout, nb=nb, kpar=kpar, bank=bank, cb=cb: e.matmul(
                                    bank[:, cb:cb + 512],
                                    w3c[kpar][:, i * 1024 + dout * 128:i * 1024 + dout * 128 + 128],
                                    g_sb[:, hc * 1024 + nb * 512:hc * 1024 + nb * 512 + 512],
                                    start=(i == 0), stop=(i == 3)),
                                       waits=(gw + regs_free[r] if i == 0 else []),
                                       inc=("sPE" if i == 3 else None))
                            p_w3 = sch.ctr["sPE"]
                            d_x = sch.ev("vector", lambda e, dout=dout, nb=nb, toff=toff, bank=bank, cb=cb: e.tensor_add(
                                x2[dout][:, toff + nb * 512:toff + (nb + 1) * 512],
                                x2[dout][:, toff + nb * 512:toff + (nb + 1) * 512],
                                bank[:, cb:cb + 512]),
                                         waits=[("sPE", p_w3)], inc="sDVE")
                            regs_free[r] = [("sDVE", d_x)]
                    w3_freed[k] = [("sPE", sch.ctr["sPE"])]
                    if ch % 2 == 1:
                        gch_free[ch // 2] = list(w3_freed[k])
                    if k + 2 < 2 * NKH:
                        emit_w3_load(k + 2)
            ln2_used = [("sPE", sch.ctr["sPE"])]
            x2_add_done = [("sDVE", sch.ctr["sDVE"])]

        for dc in range(8):
            ovv = ap3(OUT, dc * 128 * TPC, [[TPC, 128], [1, TPC]])
            sch.ev("scalar", lambda e, ovv=ovv, dc=dc: e.dma_start(out=ovv, in_=x2[dc]),
                   waits=x2_add_done, inc="sST", k=16)
        return es_pc

    ld0 = emit_const_loads()
    DBGL = nc.dram_tensor("DBG_LN1", [S, D], BF16, kind="ExternalOutput").ap() if DEBUG else None
    ln1_done = emit_ln_phase(x_rows, ln1_dram, S // 512, ld0, dbg_dram=DBGL)
    ln1own_done = emit_ln_phase(x_own_rows, ln1own_dram, TPC // 512, ld0)

    es_p1.close()

    es_attn = contextlib.ExitStack()
    sba = lambda name, shape, dt: es_attn.enter_context(nc.sbuf_tensor(name, shape, dt)).ap()
    wk_sb = sba("wk_sb", [128, 8 * 256], BF16)
    wv_sb = sba("wv_sb", [128, 8 * 256], BF16)
    wq_sb = wk_sb
    lnT = [sba(f"lnT{i}", [128, 8 * 512], BF16) for i in range(2)]
    K_aug = [sba(f"K_aug{hh}", [66, S], BF16) for hh in range(HPG)]
    V_sb = sba("V_sb", [128, 64 * 260], BF16)
    Q_sb = sba("Q_sb", [128, 2 * TPC], BF16)
    qh1 = sba("qh1", [65, TPC], BF16)
    qaugs = [{t: sba(f"qaug{t}{i}", [66, TPC], BF16) for t in "LBR"}
             for i in range(2)]
    rp_pad_sb = sba("rp_pad_sb", [128, W_PAD], BF16)
    rp01 = sba("rp01", [128, 2], BF16)
    pos_pool = [sba(f"pos{i}", [128, 768], F32) for i in range(2)]
    P_pool = [sba(f"P{i}", [128, 1024], BF16) for i in range(2)]
    rcp_sb = sba("rcp_sb", [65, 512], BF16)
    rb_sb = sba("rb_sb", [64, 512], BF16)
    o_tile = sba("o_tile", [64, 512], BF16)
    maxk_col2 = sba("maxk_col2", [128, 1], F32)
    ksub = sba("ksub", [64, 320], BF16)
    qn_row = sba("qn_row", [1, TPC], BF16)
    qn2_row = sba("qn2_row", [1, TPC], BF16)
    c_row = sba("c_row", [1, TPC], F32)
    tmp_row = sba("tmp_row", [1, TPC], F32)
    rp0_row = sba("rp0_row", [1, TPC], BF16)
    rp256_row = sba("rp256_row", [1, TPC], BF16)

    fence0 = [(s, v) for s, v in sch.ctr.items()]
    for hh in range(HPG):
        sch.ev("vector", lambda e, hh=hh: e.memset(K_aug[hh][64:66, :], 1.0),
               waits=fence0, inc="sDVE")
    fence0 = fence0 + [("sDVE", sch.ctr["sDVE"])]
    fence = fence0
    sfree_in = [fence0, fence0]
    ot_free = fence0
    psD_free = fence0
    prep_guards = []
    dbg_marks = []
    pos_prev = fence0
    for g in range(RUN_GROUPS):
        kv = emit_group_proj(g, fence, psD_free)
        out0 = {}
        for _ in emit_head_prep(g, 0, kv, psfree=kv["psd"], guards=fence, out=out0):
            pass
        preps = {0: out0}
        for hh in range(RUN_HEADS):
            bg = None
            if hh + 1 < RUN_HEADS:
                outn = {}
                bg = emit_head_prep(g, hh + 1, kv, psfree=preps[hh]["psD_user"],
                                    guards=preps[hh]["tail"] + list(prep_guards or fence),
                                    out=outn)
                preps[hh + 1] = outn
                if not INTERLEAVE_PREP:
                    for _ in bg:
                        pass
                    bg = None
            res = emit_head_attn(g, hh, preps[hh], sfree_in, ot_free, pos_prev, bg=bg)
            sfree_in = res["s_free"]
            ot_free = res["ot_done"]
            pos_prev = res["pv_last"]
            prep_guards = res["pv_last"] + res["exp_done"] + res["o_stored"] + [
                ("sGP", sch.ctr["sGP"]), ("sDVE", sch.ctr["sDVE"]),
                ("sPOS0", sch.ctr["sPOS0"]), ("sPOS1", sch.ctr["sPOS1"])] + [
                (s, sch.ctr[s]) for s in QSEMS + RPSEMS]
        psD_free = preps[RUN_HEADS - 1]["psD_user"]
        fence = prep_guards + [("sACT", sch.ctr["sACT"])]

    attn_done = prep_guards + [(s, v) for s, v in sch.ctr.items()]
    if DEBUG:
        DBG_OACC = nc.dram_tensor("DBG_OACC", [D, TPC], BF16, kind="ExternalOutput").ap()
        mo = sch.ev("scalar", lambda e: e.dma_start(out=DBG_OACC, in_=o_dram.ap()),
                    waits=attn_done, inc="sST", k=16)
        attn_done = attn_done + [("sST", mo)]

    es_attn.close()
    es_pc = emit_pc_pd(attn_done)

    st_total = sch.ctr["sST"]
    sch.ev("sync", lambda e: e.wait_ge(sch.sems["sST"], st_total))

    block = es.enter_context(nc.Block())
    sch.build(block)
    es.close()
    return nc


_prog_cache = {}


def _get_prog():
    if "nc" not in _prog_cache:
        _prog_cache["nc"] = build_program()
    return _prog_cache["nc"]



def stage_inputs(core, **inputs):
    b, cp = core // 4, core % 4
    x = np.asarray(inputs["L0_attn_x"])
    Wqkv = np.asarray(inputs["L0_attn_Wqkv"])
    bqkv = np.asarray(inputs["L0_attn_bqkv"])
    Wr = np.asarray(inputs["L0_attn_Wr"])
    br = np.asarray(inputs["L0_attn_br"])

    d = {}
    d["x_rows"] = np.ascontiguousarray(x[b])
    d["x_own_rows"] = np.ascontiguousarray(x[b, cp::4])
    d["xT_own"] = np.ascontiguousarray(x[b, cp::4].T)

    Wh = Wqkv.reshape(NH, 3, HD, D)
    d["WQ_T"] = np.ascontiguousarray(Wh[:, 0].reshape(D, D).T)
    d["WK_T"] = np.ascontiguousarray(Wh[:, 1].reshape(D, D).T)
    d["WV_T"] = np.ascontiguousarray(Wh[:, 2].reshape(D, D).T)
    bh = bqkv.reshape(NH, 3, HD)
    bq = bh[:, 0].reshape(D)
    bk = bh[:, 1].reshape(D)
    bv = bh[:, 2].reshape(D)
    d["bQKV"] = np.ascontiguousarray(
        np.concatenate([bq.reshape(8, 128).T, bk.reshape(8, 128).T, bv.reshape(8, 128).T], axis=1))
    WrB = np.concatenate([Wr.T, br[None, :]], axis=0)
    idx = np.clip(np.arange(261) - cp, 0, 256)
    d["WrBc"] = np.ascontiguousarray(WrB[:, idx])
    d["Wr0"] = np.ascontiguousarray(WrB[:, 0:1])
    d["Wr256"] = np.ascontiguousarray(WrB[:, 256:257])
    d["gate_w"] = np.ascontiguousarray(inputs["L0_ffn_gate_w"])
    for k in ("ew1", "ew2", "ew3", "sw1", "sw2", "sw3"):
        d[k] = np.ascontiguousarray(inputs["L0_ffn_" + k])
    return d


def kernel(**inputs):
    nc = _get_prog()
    in_maps = [stage_inputs(c, **inputs) for c in range(8)]
    res = run_bass_kernel_spmd(nc, in_maps, core_ids=list(range(8)))
    out = np.empty((B, S, D), dtype=np.float32)
    for c in range(8):
        b, cp = c // 4, c % 4
        out[b, cp::4, :] = res.results[c]["OUT"].T
    return out


# version 17
# speedup vs baseline: 1.0004x; 1.0004x over previous
import contextlib
import math
import os
import numpy as np
import ml_dtypes
import concourse.bass as bass
import concourse.mybir as mybir
from concourse.ap import AP
from concourse.bass_utils import run_bass_kernel_spmd

F32 = mybir.dt.float32
BF16 = mybir.dt.bfloat16
AF = mybir.ActivationFunctionType
ALU = mybir.AluOpType
AX = mybir.AxisListType

B, S, D = 2, 8192, 1024
NH, HD = 16, 64
RW = 128
R = 2 * RW + 1
E, HF = 8, 2048
TPC = S // 4
HPG, NG = 4, 4
W_PAD, KC = 1536, 768
EPS = 1e-5

DEBUG = False
INTERLEAVE_PREP = True
RUN_GROUPS = NG
RUN_HEADS = HPG
RUN_ATTN = True
PREP_STAGES = 9



class Sched:

    ENGS = ("sync", "gpsimd", "tensor", "scalar", "vector")

    def __init__(self, nc, es):
        self.nc = nc
        self.events = []
        self.sems = {}
        self.ctr = {}
        self.es = es

    def sem(self, name):
        self.sems[name] = self.es.enter_context(self.nc.semaphore(name))
        self.ctr[name] = 0

    def ev(self, eng, fn, waits=(), inc=None, k=1):
        waits = [(s, v) for (s, v) in waits if v > 0]
        if inc is not None:
            self.ctr[inc] += k
        self.events.append((eng, fn, waits, inc, k))
        return self.ctr[inc] if inc is not None else None

    def build(self, block):
        n_ev = len(self.events)
        for eng_name in self.ENGS:
            def body(eng, eng_name=eng_name):
                waited = {}
                for (e, fn, waits, inc, k) in self.events:
                    if e != eng_name:
                        continue
                    for (s, v) in waits:
                        if waited.get(s, 0) < v:
                            eng.wait_ge(self.sems[s], v)
                            waited[s] = v
                    inst = fn(eng)
                    if inc is not None:
                        inst.then_inc(self.sems[inc], k)
                assert len(self.events) == n_ev, "events created during replay!"
            getattr(block, eng_name)(body)


def classify(p, qic):
    lo = 128 * p - 4 * (128 * qic + 127) - 3
    hi = 128 * p + 127 - 4 * 128 * qic
    if lo >= 129:
        return 'R'
    if hi <= -129:
        return 'L'
    return 'S'


def classify_qb(p, qb):
    cs = {classify(p, qic) for qic in range(4 * qb, 4 * qb + 4)}
    if cs == {'R'}:
        return 'R'
    if cs == {'L'}:
        return 'L'
    return 'M'



def ap3(t, off, dims):
    return AP(t.tensor if isinstance(t, AP) else t, off, dims)


def build_program():
    nc = bass.Bass("TRN2", target_bir_lowering=False, debug=False,
                   detect_race_conditions=bool(os.environ.get("KRACE")))

    def inp(name, shape):
        return nc.dram_tensor(name, shape, F32, kind="ExternalInput").ap()

    x_rows = inp("x_rows", [S, D])
    x_own_rows = inp("x_own_rows", [TPC, D])
    xT_own = inp("xT_own", [D, TPC])
    WQ_T = inp("WQ_T", [D, D])
    WK_T = inp("WK_T", [D, D])
    WV_T = inp("WV_T", [D, D])
    bQKV = inp("bQKV", [128, 24])
    WrBc = inp("WrBc", [65, 261])
    Wr0 = inp("Wr0", [65, 1])
    Wr256 = inp("Wr256", [65, 1])
    gate_w_in = inp("gate_w", [D, E])
    ew1 = inp("ew1", [E, D, HF])
    ew2 = inp("ew2", [E, D, HF])
    ew3 = inp("ew3", [E, HF, D])
    sw1 = inp("sw1", [D, HF])
    sw2 = inp("sw2", [D, HF])
    sw3 = inp("sw3", [HF, D])

    OUT = nc.dram_tensor("OUT", [D, TPC], F32, kind="ExternalOutput").ap()

    ln1_dram = nc.dram_tensor("ln1_dram", [S, D], BF16, kind="Internal")
    ln1own_dram = nc.dram_tensor("ln1own_dram", [TPC, D], BF16, kind="Internal")
    rp_dram = [nc.dram_tensor(f"rp_dram{h}", [TPC, W_PAD], BF16, kind="Internal")
               for h in range(HPG)]

    ident_f32 = nc.inline_tensor(np.eye(128, dtype=np.float32), "ident_f32")
    ident_b16 = nc.inline_tensor(np.eye(128, dtype=ml_dtypes.bfloat16), "ident_b16")
    ones_col_f32 = nc.inline_tensor(np.ones((128, 1), dtype=np.float32), "ones_col")

    es = contextlib.ExitStack()
    sch = Sched(nc, es)
    for s in ("sLD", "sGP", "sSC", "sPE", "sACT", "sDVE", "sST",
              "sLN0", "sLN1", "sLNX0", "sLNX1", "sLNS0", "sLNS1",
              "sLNX2", "sLNX3", "sLNS2", "sLNS3",
              "sPOS0", "sPOS1", "sQAUG", "sQG1", "sQG2", "sQG3", "sCW",
              "sW120", "sW121", "sW30", "sW31", "sWB0", "sWB1"):
        sch.sem(s)
    QSEMS = ("sQAUG", "sQG1", "sQG2", "sQG3")
    RPSEMS = ("sLNX0", "sLNX1", "sLNS0", "sLNS1")

    sb = nc.alloc_sbuf_tensor
    ident_sb = sb("ident_sb", [128, 128], F32).ap()
    identb_sb = sb("identb_sb", [128, 128], BF16).ap()
    ones_sb = sb("ones_sb", [128, 1], F32).ap()
    bqkv_sb = sb("bqkv_sb", [128, 24], F32).ap()
    bq8_sb = sb("bq8_sb", [128, 8], F32).ap()
    wrbc8 = sb("wrbc8", [65, 261], BF16).ap()
    wr0_8 = sb("wr0_8", [65, 1], BF16).ap()
    wr256_8 = sb("wr256_8", [65, 1], BF16).ap()

    nrmWr = sb("nrmWr", [1, 2], F32).ap()
    gate_sb = sb("gate_sb", [128, 8 * 8], BF16).ap()
    constcols = sb("constcols", [128, 4], F32).ap()
    ones_bf = sb("ones_bf", [128, 1], BF16).ap()
    onesrow65 = sb("onesrow65", [65, 128], BF16).ap()
    wr01_8 = sb("wr01_8", [65, 2], BF16).ap()
    wr01_f = sb("wr01_f", [65, 2], F32).ap()
    scal_sb = sb("scal_sb", [1, 16], F32).ap()
    maxbr64 = sb("maxbr64", [65, 1], F32).ap()

    maxk_row = sb("maxk_row", [1, NH], F32).ap()
    o_dram = nc.dram_tensor("o_dram", [D, TPC], BF16, kind="Internal")

    psA = nc.alloc_psum_tensor("psA", [128, 1024], F32).ap()
    psB = nc.alloc_psum_tensor("psB", [128, 1024], F32).ap()
    psC = nc.alloc_psum_tensor("psC", [128, 1024], F32).ap()
    psD = nc.alloc_psum_tensor("psD", [128, 1024], F32).ap()

    CONST_COLS = {}

    def constcol(val):
        if val not in CONST_COLS:
            i = len(CONST_COLS)
            CONST_COLS[val] = constcols[:, i:i + 1]
            sch.ev("vector", lambda e, i=i, val=val: e.memset(constcols[:, i:i + 1], val), inc="sDVE")
        return CONST_COLS[val]

    def emit_const_loads():
        sch.ev("sync", lambda e: e.dma_start(out=ident_sb, in_=ident_f32.ap()), inc="sLD", k=16)
        sch.ev("sync", lambda e: e.dma_start(out=identb_sb, in_=ident_b16.ap()), inc="sLD", k=16)
        sch.ev("sync", lambda e: e.dma_start(out=ones_sb, in_=ones_col_f32.ap()), inc="sLD", k=16)
        sch.ev("sync", lambda e: e.dma_start(out=bqkv_sb, in_=bQKV), inc="sLD", k=16)
        sch.ev("sync", lambda e: e.dma_start(out=wr_stat[:, 0:261], in_=WrBc), inc="sLD", k=16)
        sch.ev("sync", lambda e: e.dma_start(out=wr01_f[:, 0:1], in_=Wr0), inc="sLD", k=16)
        sch.ev("sync", lambda e: e.dma_start(out=wr01_f[:, 1:2], in_=Wr256), inc="sLD", k=16)
        ld0 = sch.ctr["sLD"]
        gv = ap3(gate_w_in, 0, [[8, 128], [128 * 8, 8], [1, 8]])
        go = ap3(gate_sb, 0, [[64, 128], [8, 8], [1, 8]])
        sch.ev("gpsimd", lambda e: e.dma_start(out=go, in_=gv), inc="sCW", k=16)
        sch.ev("vector", lambda e: e.tensor_scalar(wrbc8[0:64, :], wr_stat[0:64, :], 8.0, None, ALU.mult),
               waits=[("sLD", ld0)], inc="sDVE")
        sch.ev("vector", lambda e: e.memset(scal_sb, 0.0), inc="sDVE")
        sch.ev("vector", lambda e: e.memset(maxk_row, 0.0), inc="sDVE")
        d_wr = sch.ev("vector", lambda e: e.tensor_copy(wrbc8[64:65, :], wr_stat[64:65, :]), inc="sDVE")
        constcol(EPS)
        sch.ev("vector", lambda e: e.memset(ones_bf, 1.0), inc="sDVE")
        sch.ev("vector", lambda e: e.memset(onesrow65[64:65, :], 1.0), inc="sDVE")
        ld1 = sch.ctr["sLD"]
        sch.ev("vector", lambda e: e.tensor_scalar(wr01_8[0:64, :], wr01_f[0:64, :], 8.0, None, ALU.mult),
               waits=[("sLD", ld1)], inc="sDVE")
        sch.ev("vector", lambda e: e.tensor_copy(wr01_8[64:65, :], wr01_f[64:65, :]), inc="sDVE")
        sch.ev("vector", lambda e: e.tensor_scalar(bq8_sb, bqkv_sb[:, 0:8], 0.125, None, ALU.mult),
               waits=[("sLD", ld0)], inc="sDVE")
        d_br = sch.ev("vector", lambda e: e.reduce_max(maxbr64[64:65, 0:1], wrbc8[64:65, 0:261], axis=AX.X),
                      waits=[("sDVE", d_wr)], inc="sDVE")
        sch.ev("gpsimd", lambda e: e.dma_start(out=ap3(scal_sb, 1, [[16, 1], [1, 1]]),
                                               in_=ap3(maxbr64, 64, [[1, 1], [1, 1]])),
               waits=[("sDVE", d_br)], inc="sGP", k=16)
        a_sq = sch.ev("scalar", lambda e: e.activation(wr_sq, wrbc8[0:64, :], AF.Square),
                      waits=[("sDVE", d_wr)], inc="sACT")
        p_nw = sch.ev("tensor", lambda e: e.matmul(psA[0:1, 0:261], ones_bf[0:64, :], wr_sq,
                                                   start=True, stop=True),
                      waits=[("sACT", a_sq), ("sDVE", sch.ctr["sDVE"])], inc="sPE")
        a_nw = sch.ev("scalar", lambda e: e.activation(row261, psA[0:1, 0:261], AF.Sqrt),
                      waits=[("sPE", p_nw)], inc="sACT")
        sch.ev("vector", lambda e: e.reduce_max(scal_sb[0:1, 0:1], row261, axis=AX.X),
               waits=[("sACT", a_nw)], inc="sDVE")
        return ld0

    es_p1 = contextlib.ExitStack()
    sbp = lambda name, shape, dt: es_p1.enter_context(nc.sbuf_tensor(name, shape, dt)).ap()
    wr_stat = sbp("wr_stat", [65, 261], F32)
    row261 = sbp("row261", [1, 261], F32)
    wr_sq = sbp("wr_sq", [64, 261], BF16)
    P1D = 4
    p1_bufx = [sbp(f"p1x{i}", [128, 4096], F32) for i in range(P1D)]
    p1_bufl = [sbp(f"p1l{i}", [128, 4096], BF16) for i in range(P1D)]
    p1_sq = [sbp(f"p1sq{i}", [128, 1024], F32) for i in range(P1D)]
    p1_stat = [sbp(f"p1st{i}", [128, 24], F32) for i in range(P1D)]

    def emit_ln_phase(src_ap, dst_dram, nblk, ld0, dbg_dram=None):
        prev_dve = sch.ctr["sDVE"]
        prev_act = sch.ctr["sACT"]
        prev_st = {i: sch.ctr[f"sLNS{i}"] for i in range(P1D)}
        entry = [("sDVE", prev_dve), ("sACT", prev_act)]
        ap_done = {}
        st_done = {}
        dbg_guard = {}
        for b in range(nblk):
            bx, bl = p1_bufx[b % P1D], p1_bufl[b % P1D]
            sq, st = p1_sq[b % P1D], p1_stat[b % P1D]
            ldsem = f"sLNX{b % P1D}"
            stsem = f"sLNS{b % P1D}"
            waits = [("sLD", ld0)]
            if b >= P1D:
                waits += ap_done[b - P1D]
                if (b - P1D) in dbg_guard:
                    waits.append(("sST", dbg_guard[b - P1D]))
            else:
                waits += entry
            for hb, qn in ((0, "sync"), (1, "scalar")):
                src = ap3(src_ap, (b * 512 + hb * 256) * D, [[D, 128], [128 * D, 2], [1, D]])
                dst_sb = ap3(bx, hb * 2048, [[4096, 128], [1024, 2], [1, 1024]])
                sch.ev(qn, lambda e, src=src, dst_sb=dst_sb: e.dma_start(out=dst_sb, in_=src),
                       waits=waits, inc=ldsem, k=16)
            ldb = sch.ctr[ldsem]
            w_prev = ap_done[b - P1D] if b >= P1D else []
            for a in range(4):
                xt = bx[:, a * 1024:(a + 1) * 1024]
                sch.ev("vector", lambda e, xt=xt, st=st, a=a: e.reduce_sum(st[:, a:a + 1], xt, axis=AX.X),
                       waits=[(ldsem, ldb)] + w_prev, inc="sDVE")
                sch.ev("scalar", lambda e, xt=xt, sq=sq, st=st, a=a: e.activation(
                    sq, xt, AF.Square, accum_out=st[:, 4 + a:5 + a]),
                       waits=[(ldsem, ldb)] + w_prev, inc="sACT")
            d_sums = sch.ctr["sDVE"]
            a_sqs = sch.ctr["sACT"]
            m1 = sch.ev("vector", lambda e, st=st: e.tensor_scalar(st[:, 8:12], st[:, 0:4], 1.0 / D, None, ALU.mult),
                        waits=[("sDVE", d_sums)], inc="sDVE")
            m2 = sch.ev("vector", lambda e, st=st: e.tensor_scalar(st[:, 12:16], st[:, 4:8], 1.0 / D, None, ALU.mult),
                        waits=[("sACT", a_sqs)], inc="sDVE")
            m3 = sch.ev("vector", lambda e, st=st: e.tensor_mul(st[:, 16:20], st[:, 8:12], st[:, 8:12]),
                        waits=[("sDVE", m1)], inc="sDVE")
            m4 = sch.ev("vector", lambda e, st=st: e.tensor_sub(st[:, 12:16], st[:, 12:16], st[:, 16:20]),
                        waits=[("sDVE", m3)], inc="sDVE")
            epsc = constcol(EPS)
            sch.ev("scalar", lambda e, st=st, epsc=epsc: e.activation(st[:, 16:20], st[:, 12:16], AF.Sqrt, bias=epsc),
                   waits=[("sDVE", m4)], inc="sACT")
            a_rt = sch.ctr["sACT"]
            m5 = sch.ev("vector", lambda e, st=st: e.reciprocal(st[:, 20:24], st[:, 16:20]),
                        waits=[("sACT", a_rt)], inc="sDVE")
            m6 = sch.ev("vector", lambda e, st=st: e.scalar_tensor_tensor(
                st[:, 0:4], st[:, 8:12], -1.0, st[:, 20:24], ALU.mult, ALU.mult),
                        waits=[("sDVE", m5)], inc="sDVE")
            w_st = ([(stsem, st_done[b - P1D])] if b >= P1D
                    else [(stsem, prev_st[b % P1D])])
            for a in range(4):
                xt = bx[:, a * 1024:(a + 1) * 1024]
                lt = bl[:, a * 1024:(a + 1) * 1024]
                if a < 2:
                    sch.ev("scalar", lambda e, xt=xt, lt=lt, st=st, a=a: e.activation(
                        lt, xt, AF.Identity, bias=st[:, 0 + a:1 + a], scale=st[:, 20 + a:21 + a]),
                           waits=w_st + [("sDVE", m6)], inc="sACT")
                else:
                    sch.ev("vector", lambda e, xt=xt, lt=lt, st=st, a=a: e.tensor_scalar(
                        lt, xt, st[:, 8 + a:9 + a], st[:, 20 + a:21 + a], ALU.subtract, ALU.mult),
                           waits=w_st + [("sDVE", m6)], inc="sDVE")
            ap_done[b] = [("sACT", sch.ctr["sACT"]), ("sDVE", sch.ctr["sDVE"])]
            dstv = ap3(dst_dram, b * 512 * D, [[D, 128], [128 * D, 4], [1, D]])
            srcv = ap3(bl, 0, [[4096, 128], [1024, 4], [1, 1024]])
            sch.ev("gpsimd", lambda e, dstv=dstv, srcv=srcv: e.dma_start(out=dstv, in_=srcv),
                   waits=ap_done[b], inc=stsem, k=16)
            if dbg_dram is not None:
                dbv = ap3(dbg_dram, b * 512 * D, [[D, 128], [128 * D, 4], [1, D]])
                sch.ev("scalar", lambda e, dbv=dbv, srcv=srcv: e.dma_start(out=dbv, in_=srcv),
                       waits=ap_done[b], inc="sST", k=16)
            if dbg_dram is not None and b == 0:
                D0S = nc.dram_tensor("DBG0_ST", [128, 24], F32, kind="ExternalOutput").ap()
                D0X = nc.dram_tensor("DBG0_X", [128, 4096], F32, kind="ExternalOutput").ap()
                D0L = nc.dram_tensor("DBG0_L", [128, 4096], BF16, kind="ExternalOutput").ap()
                sch.ev("scalar", lambda e, st=st: e.dma_start(out=D0S, in_=st),
                       waits=ap_done[b], inc="sST", k=16)
                sch.ev("scalar", lambda e, bx=bx: e.dma_start(out=D0X, in_=bx),
                       waits=ap_done[b], inc="sST", k=16)
                sch.ev("scalar", lambda e, bl=bl: e.dma_start(out=D0L, in_=bl),
                       waits=ap_done[b], inc="sST", k=16)
                dbg_guard[b] = sch.ctr["sST"]
            st_done[b] = sch.ctr[stsem]
        return st_done[nblk - 1]


    JB = S // 512
    QBn = TPC // 512

    def emit_group_proj(g, fence, psd_in):
        for (wsb, wt) in ((wk_sb, WK_T), (wv_sb, WV_T)):
            iv = ap3(wt, 256 * g, [[D, 128], [128 * D, 8], [1, 256]])
            ov = ap3(wsb, 0, [[2048, 128], [256, 8], [1, 256]])
            sch.ev("gpsimd", lambda e, iv=iv, ov=ov: e.dma_start(out=ov, in_=iv),
                   waits=list(psd_in), inc="sGP", k=16)
        w_w = sch.ctr["sGP"]
        mm_done = {}
        kb_free = {"A0": fence, "A1": fence, "B0": fence, "B1": fence, "C0": fence, "C1": fence}
        kv_mile = {}
        ln_ld = {}
        for jb in range(JB):
            buf = lnT[jb % 2]
            w_ln = list(psd_in)
            if jb >= 2:
                w_ln = [("sPE", mm_done[jb - 2])]
            lsem = f"sLN{jb % 2}"
            for dc in range(8):
                iv = ap3(ln1_dram, jb * 512 * D + dc * 128, [[D, 512], [1, 128]])
                sch.ev("sync" if dc < 4 else "scalar",
                       lambda e, buf=buf, dc=dc, iv=iv: e.dma_start_transpose(
                           out=buf[:, dc * 512:(dc + 1) * 512], in_=iv),
                       waits=w_ln, inc=lsem, k=16)
            ln_ld[jb] = sch.ctr[lsem]
            for fc in range(2):
                w_mm = [(lsem, ln_ld[jb]), ("sGP", w_w)] + kb_free[f"A{fc}"]
                for dc in range(8):
                    sch.ev("tensor", lambda e, fc=fc, dc=dc, buf=buf: e.matmul(
                        psA[:, fc * 512:(fc + 1) * 512],
                        wk_sb[:, dc * 256 + fc * 128: dc * 256 + fc * 128 + 128],
                        buf[:, dc * 512:(dc + 1) * 512],
                        start=(dc == 0), stop=(dc == 7)),
                           waits=(w_mm if dc == 0 else []), inc=("sPE" if dc == 7 else None))
                p_mm = sch.ctr["sPE"]
                hhA, hhB = 2 * fc, 2 * fc + 1
                bcol = 8 + 2 * g + fc
                aev = sch.ev("scalar", lambda e, hhA=hhA, fc=fc, jb=jb, bcol=bcol: e.activation(
                    K_aug[hhA][0:64, jb * 512:(jb + 1) * 512], psA[0:64, fc * 512:(fc + 1) * 512],
                    AF.Identity, bias=bqkv_sb[0:64, bcol:bcol + 1]),
                       waits=[("sPE", p_mm)], inc="sACT")
                dev = sch.ev("vector", lambda e, hhB=hhB, fc=fc, jb=jb, bcol=bcol: e.tensor_scalar(
                    K_aug[hhB][0:64, jb * 512:(jb + 1) * 512], psA[64:128, fc * 512:(fc + 1) * 512],
                    bqkv_sb[64:128, bcol:bcol + 1], None, ALU.add),
                       waits=[("sPE", p_mm)], inc="sDVE")
                kb_free[f"A{fc}"] = [("sACT", aev), ("sDVE", dev)]
            for sub in range(4):
                jc = 4 * jb + sub
                half = sub % 2
                w_mm = [(lsem, ln_ld[jb]), ("sGP", w_w)] + kb_free[f"B{half}"]
                for dc in range(8):
                    sch.ev("tensor", lambda e, half=half, dc=dc, sub=sub, buf=buf: e.matmul(
                        psB[:, half * 512: half * 512 + 256],
                        buf[:, dc * 512 + sub * 128: dc * 512 + sub * 128 + 128],
                        wv_sb[:, dc * 256:(dc + 1) * 256],
                        start=(dc == 0), stop=(dc == 7)),
                           waits=(w_mm if dc == 0 else []), inc=("sPE" if dc == 7 else None))
                p_mm = sch.ctr["sPE"]
                srcv = ap3(psB, half * 512, [[1024, 128], [64, 4], [1, 64]])
                dstv = ap3(V_sb, jc * 260, [[64 * 260, 128], [65, 4], [1, 64]])
                dev = sch.ev("vector", lambda e, srcv=srcv, dstv=dstv: e.tensor_copy(dstv, srcv),
                             waits=[("sPE", p_mm)], inc="sDVE")
                kb_free[f"B{half}"] = [("sDVE", dev)]
            mm_done[jb] = sch.ctr["sPE"]
        ov = ap3(V_sb, 64, [[64 * 260, 128], [260, 64], [65, 4], [1, 1]])
        v_ones = sch.ev("vector", lambda e, ov=ov: e.memset(ov, 1.0),
                        waits=fence, inc="sDVE")
        ivq = ap3(WQ_T, 256 * g, [[D, 128], [128 * D, 8], [1, 256]])
        ovq = ap3(wq_sb, 0, [[2048, 128], [256, 8], [1, 256]])
        sch.ev("gpsimd", lambda e, ivq=ivq, ovq=ovq: e.dma_start(out=ovq, in_=ivq),
               waits=[("sPE", mm_done[JB - 1])], inc="sGP", k=16)
        w_wq = sch.ctr["sGP"]
        psDb = psD.bitcast(BF16)
        q_copy = {}
        q_mm = {}
        pool_free = {0: list(psd_in) + fence, 1: list(psd_in) + fence}
        region_free = {0: list(psd_in), 1: list(psd_in)}
        for qb in range(QBn):
            buf = lnT[qb % 2]
            w_ln = ([("sPE", mm_done[JB - 2 + (qb % 2)])] if qb < 2
                    else [("sPE", q_mm[qb - 2])])
            for j in range(4):
                k = qb * 4 + j
                lsem = f"sLN{k % 2}"
                tok_buf = P_pool[k % 2]
                iv = ap3(ln1own_dram, (qb * 512 + j * 128) * D, [[D, 128], [1, D]])
                sch.ev("sync" if k % 2 == 0 else "scalar",
                       lambda e, tok_buf=tok_buf, iv=iv: e.dma_start(out=tok_buf, in_=iv),
                       waits=pool_free[k % 2], inc=lsem, k=16)
                ld_k = sch.ctr[lsem]
                p_tr = None
                for dc in range(8):
                    r = (k * 8 + dc) % 2
                    p_tr = sch.ev("tensor", lambda e, tok_buf=tok_buf, dc=dc, r=r: e.transpose(
                        psDb[:, r * 1024:r * 1024 + 128],
                        tok_buf[:, dc * 128:(dc + 1) * 128], identb_sb),
                                  waits=[(lsem, ld_k)] + region_free[r], inc="sPE")
                    d_c = sch.ev("vector", lambda e, buf=buf, dc=dc, j=j, r=r: e.tensor_copy(
                        buf[:, dc * 512 + j * 128:dc * 512 + j * 128 + 128],
                        psDb[:, r * 1024:r * 1024 + 128]),
                                 waits=[("sPE", p_tr)] + w_ln, inc="sDVE")
                    region_free[r] = [("sDVE", d_c)]
                pool_free[k % 2] = [("sPE", p_tr)]
            q_copy[qb] = sch.ctr["sDVE"]
            for fc in range(2):
                w_mm = [("sDVE", q_copy[qb]), ("sGP", w_wq)] + kb_free[f"C{fc}"]
                for dc in range(8):
                    sch.ev("tensor", lambda e, fc=fc, dc=dc, buf=buf: e.matmul(
                        psC[:, fc * 512:(fc + 1) * 512],
                        wq_sb[:, dc * 256 + fc * 128: dc * 256 + fc * 128 + 128],
                        buf[:, dc * 512:(dc + 1) * 512],
                        start=(dc == 0), stop=(dc == 7)),
                           waits=(w_mm if dc == 0 else []), inc=("sPE" if dc == 7 else None))
                p_mm = sch.ctr["sPE"]
                aev = sch.ev("scalar", lambda e, fc=fc, qb=qb, g=g: e.activation(
                    Q_sb[:, fc * TPC + qb * 512: fc * TPC + qb * 512 + 512],
                    psC[:, fc * 512:(fc + 1) * 512],
                    AF.Identity, scale=0.125, bias=bq8_sb[:, 2 * g + fc:2 * g + fc + 1]),
                       waits=[("sPE", p_mm)], inc="sACT")
                kb_free[f"C{fc}"] = [("sACT", aev)]
            q_mm[qb] = sch.ctr["sPE"]
        kv_ready = {"k": [("sACT", sch.ctr["sACT"]), ("sDVE", sch.ctr["sDVE"])],
                    "q": [("sACT", sch.ctr["sACT"])],
                    "v": [("sDVE", max(v_ones, sch.ctr["sDVE"]))],
                    "psd": [("sPE", sch.ctr["sPE"]), ("sDVE", sch.ctr["sDVE"])]}
        return kv_ready

    def emit_head_prep(g, hh, kv_ready, psfree, guards, out):
        h = 4 * g + hh
        qaug = qaugs[hh % 2]
        QSEM, RPSEM = QSEMS[hh], RPSEMS[hh]
        fc, rhalf = hh // 2, 64 * (hh % 2)
        wq = kv_ready["q"] + kv_ready["k"] + list(guards)
        d_q = sch.ev("vector", lambda e, fc=fc, rhalf=rhalf: e.tensor_copy(
            qh1[0:64, :], Q_sb[rhalf:rhalf + 64, fc * TPC:(fc + 1) * TPC]),
                     waits=wq, inc="sDVE")
        d_q = sch.ev("vector", lambda e: e.memset(qh1[64:65, :], 1.0), waits=wq, inc="sDVE")
        for t in "LBR":
            sch.ev("vector", lambda e, t=t, fc=fc, rhalf=rhalf: e.tensor_copy(
                qaug[t][0:64, :], Q_sb[rhalf:rhalf + 64, fc * TPC:(fc + 1) * TPC]),
                   waits=wq, inc="sDVE")
        d_base = sch.ctr["sDVE"]
        out["qaug"] = [("sDVE", d_base)]
        yield
        bank0_free = list(psfree)
        d_ks = sch.ev("vector", lambda e, hh=hh: e.tensor_copy(
            ksub[:, 0:256], ap3(K_aug[hh], 0, [[S, 64], [32, 256]])),
                      waits=kv_ready["k"] + list(guards), inc="sDVE")
        mg = 192 * h + 64
        d_ks = sch.ev("vector", lambda e, hh=hh, mg=mg: e.tensor_copy(
            ksub[:, 256:320], K_aug[hh][0:64, mg:mg + 64]),
                      waits=kv_ready["k"] + list(guards) + [("sDVE", d_ks)], inc="sDVE")
        tr_free = list(psfree)
        mm_free = list(bank0_free)
        for qc in range(16):
            p_m = sch.ev("tensor", lambda e, qc=qc: e.matmul(
                psD[:, 0:320], qh1[0:64, qc * 128:(qc + 1) * 128], ksub,
                start=True, stop=True),
                         waits=mm_free + [("sDVE", d_ks), ("sDVE", d_q)], inc="sPE")
            d_m = sch.ev("vector", lambda e: e.reduce_max(maxk_col2, psD[:, 0:320], axis=AX.X),
                         waits=[("sPE", p_m)], inc="sDVE")
            mm_free = [("sDVE", d_m)]
            yield
            p_tt = sch.ev("tensor", lambda e: e.transpose(psD[0:1, 640:768], maxk_col2,
                                                          ident_sb),
                          waits=[("sDVE", d_m)] + tr_free, inc="sPE")
            d_mc = sch.ev("vector", lambda e, qc=qc: e.tensor_copy(
                tmp_row[0:1, qc * 128:(qc + 1) * 128], psD[0:1, 640:768]),
                          waits=[("sPE", p_tt)], inc="sDVE")
            tr_free = [("sDVE", d_mc)]
            yield
        rfree = mm_free + tr_free
        ctr_free = list(guards)
        r01_free = list(guards)
        r01_psum = list(psfree)
        pe_w = rfree
        mp_free = tr_free
        for qic in range(16):
            p_rp = sch.ev("tensor", lambda e, qic=qic: e.matmul(
                psD[:, 0:261], qh1[:, qic * 128:(qic + 1) * 128], wrbc8[:, 0:261],
                start=True, stop=True),
                          waits=pe_w + [("sDVE", d_q)], inc="sPE")
            p_r01 = sch.ev("tensor", lambda e, qic=qic: e.matmul(
                psD[:, 512:514], qh1[:, qic * 128:(qic + 1) * 128], wr01_8[:, 0:2],
                start=True, stop=True), waits=r01_psum, inc="sPE")
            a_c = sch.ev("scalar", lambda e: e.activation(
                rp_pad_sb[:, 640:901], psD[:, 0:261], AF.Copy),
                         waits=[("sPE", p_r01)] + ctr_free, inc="sACT")
            pe_w = [("sACT", a_c)]
            d_mp = sch.ev("vector", lambda e: e.reduce_max(maxk_col2, psD[:, 0:261], axis=AX.X),
                          waits=[("sPE", p_r01), ("sACT", a_c)], inc="sDVE")
            yield
            p_tp = sch.ev("tensor", lambda e: e.transpose(psD[0:1, 640:768], maxk_col2,
                                                          ident_sb),
                          waits=[("sDVE", d_mp)] + mp_free, inc="sPE")
            d_mq = sch.ev("vector", lambda e, qic=qic: e.tensor_copy(
                qn_row[0:1, qic * 128:(qic + 1) * 128], psD[0:1, 640:768]),
                          waits=[("sPE", p_tp)], inc="sDVE")
            mp_free = [("sDVE", d_mq)]
            pe_w.append(("sPE", p_tp))
            d_c2 = sch.ev("vector", lambda e: e.tensor_copy(rp01[:, 0:2], psD[:, 512:514]),
                          waits=[("sPE", p_tp)] + r01_free, inc="sDVE")
            d_l = sch.ev("vector", lambda e: e.tensor_copy(
                rp_pad_sb[:, 0:640], ap3(rp01, 0, [[2, 128], [0, 640]])),
                         waits=[("sDVE", d_c2)] + ctr_free, inc="sDVE")
            d_r2 = sch.ev("vector", lambda e: e.tensor_copy(
                rp_pad_sb[:, 901:W_PAD], ap3(rp01, 1, [[2, 128], [0, W_PAD - 901]])),
                         waits=[("sDVE", d_l)], inc="sDVE")
            pe_w.append(("sDVE", d_r2))
            ov = ap3(rp_dram[hh], qic * 128 * W_PAD, [[W_PAD, 128], [1, W_PAD]])
            gc = sch.ev("sync", lambda e, ov=ov: e.dma_start(out=ov, in_=rp_pad_sb),
                        waits=[("sACT", a_c), ("sDVE", d_r2)], inc=RPSEM, k=16)
            ctr_free = [(RPSEM, gc)]
            r01_free = [(RPSEM, gc)]
            r01_psum = [("sDVE", d_c2), ("sDVE", d_mq)]
            yield
        out["rp_stored"] = [(RPSEM, sch.ctr[RPSEM])]
        d_c = sch.ev("vector", lambda e: e.scalar_tensor_tensor(
            c_row, tmp_row, 8.0, qn_row, ALU.add, ALU.add),
                     waits=mp_free + [("sDVE", sch.ctr["sDVE"])], inc="sDVE")
        rfree2 = pe_w
        for qb in range(QBn):
            for col in range(2):
                row = rp0_row if col == 0 else rp256_row
                p_r = sch.ev("tensor", lambda e, qb=qb, col=col: e.matmul(
                    psD[0:1, 0:512], wr01_8[:, col:col + 1], qh1[:, qb * 512:(qb + 1) * 512],
                    start=True, stop=True),
                             waits=rfree2 + [("sDVE", d_q)], inc="sPE")
                d_r = sch.ev("vector", lambda e, qb=qb, row=row: e.tensor_copy(
                    row[0:1, qb * 512:(qb + 1) * 512], psD[0:1, 0:512]),
                             waits=[("sPE", p_r)], inc="sDVE")
                rfree2 = [("sDVE", d_r)]
                yield
        rfree = rfree2
        t_marks = []
        qn_free = [("sDVE", d_c)]
        for t, mk, fine_row in (("L", rp0_row, rp0_row), ("R", rp256_row, rp256_row),
                                ("B", None, rp0_row)):
            if mk is None:
                d_t = sch.ev("vector", lambda e: e.tensor_scalar(
                    tmp_row, c_row, -1.0, None, ALU.mult),
                             waits=[("sDVE", d_c)], inc="sDVE")
            else:
                d_t = sch.ev("vector", lambda e, mk=mk: e.tensor_sub(tmp_row, mk, c_row),
                             waits=rfree + [("sDVE", d_c)], inc="sDVE")
            d_co = sch.ev("vector", lambda e: e.tensor_copy(qn2_row, tmp_row),
                          waits=[("sDVE", d_t)] + qn_free, inc="sDVE")
            g_co = sch.ev("gpsimd", lambda e, t=t: e.dma_start(
                out=qaug[t][64:65, :], in_=qn2_row), waits=[("sDVE", d_co)], inc=QSEM, k=16)
            d_f = sch.ev("vector", lambda e, fine_row=fine_row: e.tensor_sub(
                fine_row, tmp_row, qn2_row), waits=[("sDVE", d_co)], inc="sDVE")
            g_f = sch.ev("gpsimd", lambda e, t=t, fine_row=fine_row: e.dma_start(
                out=qaug[t][65:66, :], in_=fine_row),
                         waits=[("sDVE", d_f)], inc=QSEM, k=16)
            qn_free = [(QSEM, g_f)]
            t_marks.append((QSEM, g_f))
            yield
        out["qaug"] = t_marks + [("sDVE", d_base)]
        out["rp_stored"] = [(RPSEM, sch.ctr[RPSEM])]
        out["psD_user"] = pe_w + rfree + mp_free
        out["tail"] = [("sDVE", sch.ctr["sDVE"]), ("sACT", sch.ctr["sACT"]),
                       ("sPE", sch.ctr["sPE"]), (RPSEM, sch.ctr[RPSEM]),
                       (QSEM, sch.ctr[QSEM])]

    ot_chain = {}

    def emit_head_attn(g, hh, prep, sfree_in, ot_free, pos_entry, bg=None):
        h = 4 * g + hh
        qaug = qaugs[hh % 2]
        sbanks = [psA, psB]
        ot_dma_free = ot_chain
        s_free = {0: list(sfree_in[0]), 1: list(sfree_in[1])}
        exp_mile = {}
        pv_mile = {}
        pos_ld = {}
        pos_free = {}
        s_stop = {}
        O_T = psC
        ot_in = list(ot_free)
        for qbp in range(2):

            def load_pos(qic):
                p_first = max(0, 4 * qic - 1)
                n_p = min(63, 4 * qic + 4) - p_first + 1
                width = 128 * n_p
                off = qic * 128 * W_PAD + KC + (128 * p_first - 512 * qic)
                iv = ap3(rp_dram[hh], off, [[W_PAD - 4, 128], [1, width]])
                buf = pos_pool[qic % 2]
                psem = f"sPOS{qic % 2}"
                w = prep["rp_stored"] + pos_free.get(qic - 2, list(pos_entry))
                sch.ev("gpsimd", lambda e, buf=buf, iv=iv, width=width: e.dma_start(
                    out=buf[:, 0:width], in_=iv), waits=w, inc=psem, k=16)
                pos_ld[qic] = (psem, sch.ctr[psem])

            qic_lo, qic_hi = 8 * qbp, 8 * qbp + 8

            def emit_S(p, qbp=qbp, qic_lo=qic_lo, qic_hi=qic_hi):
                if qbp == 0 and p == 0:
                    load_pos(0)
                for qic in range(qic_lo, qic_hi):
                    if qic >= 1 and max(1, 4 * qic - 3) == p and qic not in pos_ld:
                        load_pos(qic)
                bank = sbanks[p % 2]
                ops = []
                for qb2 in range(2):
                    qb = 2 * qbp + qb2
                    cls = classify_qb(p, qb)
                    if cls in "LR":
                        ops.append(("far512", qb2, cls, None))
                    else:
                        for qic in range(4 * qb, 4 * qb + 4):
                            c = classify(p, qic)
                            if c == 'S':
                                ops.append(("band", qb2, c, qic))
                            else:
                                ops.append(("far128", qb2, c, qic))
                started = {0: False, 1: False}
                n_emitted = 0
                w_first = s_free[p % 2] + prep["qaug"]
                pos_w = sorted({o[3] for o in ops if o[0] == "band"})
                for qic in pos_w:
                    w_first = w_first + [pos_ld[qic]]
                last_by_half = {}
                for idx, (kind, qb2, c, qic) in enumerate(ops):
                    if kind == "band":
                        last_by_half[qb2] = idx
                    else:
                        last_by_half[qb2] = idx
                seen = {0: 0, 1: 0}
                tot = {0: sum(1 for o in ops if o[1] == 0) + sum(1 for o in ops if o[1] == 0 and o[0] == "band"),
                       1: sum(1 for o in ops if o[1] == 1) + sum(1 for o in ops if o[1] == 1 and o[0] == "band")}
                for (kind, qb2, c, qic) in ops:
                    qb = 2 * qbp + qb2
                    first = not started[qb2]
                    started[qb2] = True
                    if kind == "far512":
                        seen[qb2] += 1
                        stop = seen[qb2] == tot[qb2]
                        sch.ev("tensor", lambda e, bank=bank, qb2=qb2, c=c, qb=qb, p=p, hh=hh, first=first, stop=stop: e.matmul(
                            bank[:, qb2 * 512:(qb2 + 1) * 512],
                            K_aug[hh][:, p * 128:(p + 1) * 128],
                            qaug[c][:, qb * 512:(qb + 1) * 512],
                            start=first, stop=stop),
                               waits=(w_first if n_emitted == 0 else []),
                               inc=None)
                        n_emitted += 1
                    elif kind == "far128":
                        seen[qb2] += 1
                        stop = seen[qb2] == tot[qb2]
                        col = qb2 * 512 + (qic - 4 * qb) * 128
                        sch.ev("tensor", lambda e, bank=bank, col=col, c=c, qic=qic, p=p, hh=hh, first=first, stop=stop: e.matmul(
                            bank[:, col:col + 128],
                            K_aug[hh][:, p * 128:(p + 1) * 128],
                            qaug[c][:, qic * 128:(qic + 1) * 128],
                            start=first, stop=stop),
                               waits=(w_first if n_emitted == 0 else []),
                               inc=None)
                        n_emitted += 1
                    else:
                        col = qb2 * 512 + (qic - 4 * qb) * 128
                        p_first_q = max(0, 4 * qic - 1)
                        pcol = (p - p_first_q) * 128
                        buf = pos_pool[qic % 2]
                        seen[qb2] += 2
                        stop = seen[qb2] == tot[qb2]
                        sch.ev("tensor", lambda e, bank=bank, col=col, buf=buf, pcol=pcol, first=first: e.matmul(
                            bank[:, col:col + 128], buf[:, pcol:pcol + 128], ident_sb,
                            start=first, stop=False, is_transpose=True),
                               waits=(w_first if n_emitted == 0 else []),
                               inc=None)
                        n_emitted += 1
                        sch.ev("tensor", lambda e, bank=bank, col=col, qic=qic, p=p, hh=hh, stop=stop: e.matmul(
                            bank[:, col:col + 128],
                            K_aug[hh][:, p * 128:(p + 1) * 128],
                            qaug["B"][:, qic * 128:(qic + 1) * 128],
                            start=False, stop=stop),
                               inc=None)
                        n_emitted += 1
                eng, fn, wts, inc0, k0 = sch.events[-1]
                assert eng == "tensor" and inc0 is None
                sch.ctr["sPE"] += 1
                sch.events[-1] = (eng, fn, wts, "sPE", 1)
                p_s = sch.ctr["sPE"]
                for qic in pos_w:
                    pos_free[qic] = [("sPE", p_s)]
                s_stop[p] = p_s

            def emit_exp(p):
                w_exp = [("sPE", s_stop[p])] + pv_mile.get(p - 2, [])
                a_e = sch.ev("scalar", lambda e, bank=sbanks[p % 2], p=p: e.activation(
                    P_pool[p % 2], bank, AF.Exp), waits=w_exp, inc="sACT")
                exp_mile[p] = a_e
                s_free[p % 2] = [("sACT", a_e)]

            def emit_PV(p):
                for qb2 in range(2):
                    sch.ev("tensor", lambda e, qb2=qb2, p=p, hh=hh: e.matmul(
                        O_T[0:65, qb2 * 512:(qb2 + 1) * 512],
                        V_sb[:, p * 260 + hh * 65: p * 260 + hh * 65 + 65],
                        P_pool[p % 2][:, qb2 * 512:(qb2 + 1) * 512],
                        start=(p == 0), stop=(p == 63)),
                           waits=([("sACT", exp_mile[p])] + (ot_in if p == 0 else []))
                                 if qb2 == 0 else [],
                           inc=("sPE" if qb2 == 1 else None))
                pv_mile[p] = [("sPE", sch.ctr["sPE"])]

            for it in range(66):
                if it <= 63:
                    emit_S(it)
                if 0 <= it - 1 <= 63:
                    emit_exp(it - 1)
                if 0 <= it - 2 <= 63:
                    emit_PV(it - 2)
                if bg is not None:
                    next(bg, None)
            pv_last = sch.ctr["sPE"]
            ev_marks = []
            bcast_free = []
            for qb2 in range(2):
                qb = 2 * qbp + qb2
                def _recip(e, qb2=qb2):
                    with nc.allow_low_precision("bf16 recip of softmax denominators is within tolerance"):
                        return e.reciprocal(rcp_sb[64:65, 0:512],
                                            O_T[64:65, qb2 * 512:(qb2 + 1) * 512])
                d_r = sch.ev("vector", _recip, waits=[("sPE", pv_last)] + bcast_free,
                             inc="sDVE")
                p_b = sch.ev("tensor", lambda e, qb2=qb2: e.matmul(
                    O_T[64:128, qb2 * 512:(qb2 + 1) * 512],
                    onesrow65[64:65, 0:64],
                    rcp_sb[64:65, 0:512],
                    start=True, stop=True),
                             waits=[("sDVE", d_r)] + bcast_free, inc="sPE")
                d_b = sch.ev("vector", lambda e, qb2=qb2: e.tensor_copy(
                    rb_sb, O_T[64:128, qb2 * 512:(qb2 + 1) * 512]),
                             waits=[("sPE", p_b)] + ot_dma_free.get("rb", []), inc="sDVE")
                bcast_free = [("sDVE", d_b)]
                d_o = sch.ev("vector", lambda e, qb2=qb2: e.scalar_tensor_tensor(
                    o_tile,
                    O_T[0:64, qb2 * 512:(qb2 + 1) * 512],
                    1.0, rb_sb, ALU.mult, ALU.mult),
                             waits=[("sDVE", d_b)] + ot_dma_free.get("o", []), inc="sDVE")
                ot_dma_free["rb"] = [("sDVE", d_o)]
                ov = ap3(o_dram, (64 * h) * TPC + qb * 512, [[TPC, 64], [1, 512]])
                g_o = sch.ev("sync", lambda e, ov=ov: e.dma_start(
                    out=ov, in_=o_tile),
                             waits=[("sDVE", d_o)], inc="sSC", k=16)
                ot_dma_free["o"] = [("sSC", g_o)]
                ev_marks.append(("sDVE", d_o))
                ev_marks.append(("sPE", p_b))
            ot_in = list(ev_marks)
        if bg is not None:
            for _ in bg:
                pass
        return {"s_free": [s_free[0], s_free[1]], "ot_done": list(ev_marks),
                "pv_last": [("sPE", sch.ctr["sPE"])],
                "exp_done": [("sACT", sch.ctr["sACT"])],
                "o_stored": [("sSC", sch.ctr["sSC"])]}


    def emit_pc_pd(attn_done):
        es_pc = contextlib.ExitStack()
        sbc = lambda name, shape, dt: es_pc.enter_context(nc.sbuf_tensor(name, shape, dt)).ap()
        x2 = [sbc(f"x2_{dc}", [128, TPC], F32) for dc in range(8)]
        ln2 = [sbc(f"ln2_{dc}", [128, 1024], BF16) for dc in range(8)]
        g_sb = sbc("g_sb", [128, 16 * 1024], BF16)
        w3c = [sbc(f"w3c{i}", [128, 4 * 1024], BF16) for i in range(2)]
        w1c4 = [sbc(f"w1c4_{i}", [128, 8 * 512], BF16) for i in range(2)]
        w2c4 = [sbc(f"w2c4_{i}", [128, 8 * 512], BF16) for i in range(2)]
        wb2 = [sbc(f"wb2_{i}", [128, 1024], BF16) for i in range(2)]
        s_tmp = sbc("s_tmp", [128, 1024], BF16)
        sq_scr = sbc("sq_scr", [128, 1024], F32)
        gl = sbc("gl", [128, 8], F32)
        gl2 = sbc("gl2", [128, 8], F32)
        gl3 = sbc("gl3", [128, 8], F32)
        gcol = sbc("gcol", [128, 6], F32)
        wtok = sbc("wtok", [128, 8], BF16)
        w8_rows = sbc("w8_rows", [8, 1024], BF16)
        ident_bf = sbc("ident_bf", [128, 128], BF16)
        onesrow_f = sbc("onesrow_f", [1, 128], F32)
        mean_row = sbc("mean_row", [1, TPC], F32)
        rstd_row = sbc("rstd_row", [1, TPC], F32)
        tmp2_row = sbc("tmp2_row", [1, TPC], F32)
        we_dram = nc.dram_tensor("we_dram", [E, TPC], BF16, kind="Internal")

        d0 = sch.ev("vector", lambda e: e.tensor_copy(ident_bf, ident_sb),
                    waits=attn_done, inc="sDVE")
        sch.ev("vector", lambda e: e.memset(onesrow_f, 1.0), inc="sDVE")
        consts_d = sch.ctr["sDVE"]

        x_mark = {}
        for dc in range(8):
            iv = ap3(xT_own, dc * 128 * TPC, [[TPC, 128], [1, TPC]])
            sch.ev("sync" if dc % 2 == 0 else "scalar",
                   lambda e, iv=iv, dc=dc: e.dma_start(out=x2[dc], in_=iv),
                   waits=attn_done, inc="sLD", k=16)
            x_mark[dc] = ("sLD", sch.ctr["sLD"])
        o_stage = [s_tmp, sq_scr]
        stt_mark = {}
        for k in range(16):
            dc, hf = k // 2, k % 2
            ov = ap3(o_dram, dc * 128 * TPC + hf * 1024, [[TPC, 128], [1, 1024]])
            w = ([stt_mark[k - 2]] if k >= 2 else list(attn_done))
            sch.ev("gpsimd", lambda e, ov=ov, k=k: e.dma_start(
                out=o_stage[k % 2][:, 0:1024], in_=ov),
                   waits=w, inc="sGP", k=16)
            o_mark = ("sGP", sch.ctr["sGP"])
            d = sch.ev("vector", lambda e, dc=dc, hf=hf, k=k: e.scalar_tensor_tensor(
                x2[dc][:, hf * 1024:(hf + 1) * 1024],
                o_stage[k % 2][:, 0:1024],
                bqkv_sb[:, 16 + dc:17 + dc],
                x2[dc][:, hf * 1024:(hf + 1) * 1024], ALU.add, ALU.add),
                       waits=[o_mark, x_mark[dc], ("sDVE", consts_d)], inc="sDVE")
            stt_mark[k] = ("sDVE", d)
        x2_all = [("sDVE", sch.ctr["sDVE"])]

        a_prev = []
        sqh_prev = {0: x2_all, 1: x2_all}
        for qb in range(QBn):
            for dc in range(8):
                sch.ev("tensor", lambda e, qb=qb, dc=dc: e.matmul(
                    psA[0:1, 0:512], ones_sb, x2[dc][:, qb * 512:(qb + 1) * 512],
                    start=(dc == 0), stop=(dc == 7)),
                       waits=(x2_all + a_prev if dc == 0 else []),
                       inc=("sPE" if dc == 7 else None))
            p_sum = sch.ctr["sPE"]
            for dc in range(8):
                h = dc % 2
                a_s = sch.ev("scalar", lambda e, qb=qb, dc=dc, h=h: e.activation(
                    sq_scr[:, h * 512:(h + 1) * 512],
                    x2[dc][:, qb * 512:(qb + 1) * 512], AF.Square),
                             waits=sqh_prev[h] + (a_prev if dc == 0 else []), inc="sACT")
                m_s = sch.ev("tensor", lambda e, dc=dc, h=h: e.matmul(
                    psB[0:1, 0:512], ones_sb, sq_scr[:, h * 512:(h + 1) * 512],
                    start=(dc == 0), stop=(dc == 7)),
                       waits=[("sACT", a_s)], inc="sPE")
                sqh_prev[h] = [("sPE", m_s)]
            p_sq = sch.ctr["sPE"]
            d_m = sch.ev("vector", lambda e, qb=qb: e.tensor_scalar(
                mean_row[0:1, qb * 512:(qb + 1) * 512], psA[0:1, 0:512],
                1.0 / D, None, ALU.mult),
                         waits=[("sPE", p_sum)], inc="sDVE")
            d_v = sch.ev("vector", lambda e, qb=qb: e.tensor_scalar(
                tmp2_row[0:1, qb * 512:(qb + 1) * 512], psB[0:1, 0:512],
                1.0 / D, None, ALU.mult),
                         waits=[("sPE", p_sq), ("sDVE", d_m)], inc="sDVE")
            a_prev = [("sDVE", d_v)]
        d_m2r = sch.ev("vector", lambda e: e.tensor_mul(rstd_row, mean_row, mean_row),
                       waits=a_prev, inc="sDVE")
        d_var = sch.ev("vector", lambda e: e.tensor_sub(tmp2_row, tmp2_row, rstd_row),
                       waits=[("sDVE", d_m2r)], inc="sDVE")
        a_sr = sch.ev("scalar", lambda e: e.activation(rstd_row, tmp2_row, AF.Sqrt,
                                                       bias=constcol(EPS)[0:1, :]),
                      waits=[("sDVE", d_var)], inc="sACT")
        d_rs = sch.ev("vector", lambda e: e.reciprocal(tmp2_row, rstd_row),
                      waits=[("sACT", a_sr)], inc="sDVE")

        psA_free = [("sDVE", sch.ctr["sDVE"]), ("sPE", sch.ctr["sPE"])]
        psB_free = list(psA_free)
        regs = [(psC, 0), (psC, 512), (psD, 0), (psD, 512)]
        regs_free = [list(psA_free) for _ in range(4)]
        gl_free = []
        tr_free = []
        sq_free = [("sDVE", sch.ctr["sDVE"])]
        stmp_free = []
        x2_add_done = []
        we_done = {}
        ln2_done = []
        bc_free = psA_free
        ln2_used = []

        NQH = (E + 1) * 4
        NKH = (E + 1) * 4
        w12_ld, w12_freed = {}, {}
        w3_ld, w3_freed = {}, {}
        wb_ld, wb_freed = {}, {}
        gch_free = {0: list(attn_done), 1: list(attn_done)}

        def emit_w12_load(q):
            halfq, r = divmod(q, NQH)
            e_, hcg = divmod(r, 4)
            if e_ == E:
                i1 = ap3(sw1, hcg * 512, [[HF, 128], [128 * HF, 8], [1, 512]])
                i2 = ap3(sw2, hcg * 512, [[HF, 128], [128 * HF, 8], [1, 512]])
            else:
                off = e_ * D * HF + hcg * 512
                i1 = ap3(ew1, off, [[HF, 128], [128 * HF, 8], [1, 512]])
                i2 = ap3(ew2, off, [[HF, 128], [128 * HF, 8], [1, 512]])
            par = q % 2
            wsem = f"sW12{par}"
            od = [[4096, 128], [512, 8], [1, 512]]
            w = w12_freed.get(q - 2, list(attn_done))
            sch.ev("gpsimd", lambda e, i1=i1, par=par, od=od: e.dma_start(
                out=ap3(w1c4[par], 0, od), in_=i1), waits=w, inc=wsem, k=16)
            sch.ev("gpsimd", lambda e, i2=i2, par=par, od=od: e.dma_start(
                out=ap3(w2c4[par], 0, od), in_=i2), inc=wsem, k=16)
            w12_ld[q] = (wsem, sch.ctr[wsem])

        def emit_w3_load(k):
            halfk, r = divmod(k, NKH)
            e_, ch = divmod(r, 4)
            if e_ == E:
                iv = ap3(sw3, ch * 4 * 128 * D, [[D, 128], [128 * D, 4], [1, D]])
            else:
                iv = ap3(ew3, e_ * HF * D + ch * 4 * 128 * D,
                         [[D, 128], [128 * D, 4], [1, D]])
            par = k % 2
            ksem = f"sW3{par}"
            w = w3_freed.get(k - 2, list(attn_done))
            sch.ev("gpsimd", lambda e, iv=iv, par=par: e.dma_start(
                out=w3c[par], in_=iv), waits=w, inc=ksem, k=16)
            w3_ld[k] = (ksem, sch.ctr[ksem])

        def emit_wb_load(j, half):
            e_ = j - half * E
            par = j % 2
            bsem = f"sWB{par}"
            wv = ap3(we_dram, e_ * TPC + half * 1024, [[0, 128], [1, 1024]])
            w = wb_freed.get(j - 2, []) + we_done[half]
            sch.ev("gpsimd", lambda e, wv=wv, par=par: e.dma_start(
                out=wb2[par], in_=wv), waits=w, inc=bsem, k=16)
            wb_ld[j] = (bsem, sch.ctr[bsem])

        emit_w12_load(0)
        emit_w12_load(1)
        emit_w3_load(0)
        emit_w3_load(1)

        for half in range(2):
            toff = half * 1024
            for qbl in range(2):
                qb = 2 * half + qbl
                p_bm = sch.ev("tensor", lambda e, qb=qb: e.matmul(
                    psA[:, 0:512], onesrow_f, mean_row[0:1, qb * 512:(qb + 1) * 512],
                    start=True, stop=True),
                              waits=[("sDVE", d_rs), ("sDVE", consts_d)] + bc_free + psA_free,
                              inc="sPE")
                p_br = sch.ev("tensor", lambda e, qb=qb: e.matmul(
                    psB[:, 0:512], onesrow_f, tmp2_row[0:1, qb * 512:(qb + 1) * 512],
                    start=True, stop=True), waits=psB_free, inc="sPE")
                p_bc = sch.ctr["sPE"]
                d_prev = sq_free
                for dc in range(8):
                    d_s = sch.ev("vector", lambda e, dc=dc, qb=qb: e.tensor_sub(
                        sq_scr[:, 0:512], x2[dc][:, qb * 512:(qb + 1) * 512], psA[:, 0:512]),
                                 waits=[("sPE", p_bc)] + d_prev + ln2_used, inc="sDVE")
                    d_l = sch.ev("vector", lambda e, dc=dc, qbl=qbl: e.tensor_mul(
                        ln2[dc][:, qbl * 512:(qbl + 1) * 512], sq_scr[:, 0:512], psB[:, 0:512]),
                                 waits=[("sDVE", d_s)], inc="sDVE")
                    d_prev = [("sDVE", d_l)]
                sq_free = d_prev
                psA_free = d_prev
                psB_free = d_prev
                bc_free = d_prev
            ln2_done = [("sDVE", sch.ctr["sDVE"])]

            for tcl in range(8):
                for dc in range(8):
                    sch.ev("tensor", lambda e, tcl=tcl, dc=dc: e.matmul(
                        psC[:, 0:8], ln2[dc][:, tcl * 128:(tcl + 1) * 128],
                        gate_sb[:, dc * 8:(dc + 1) * 8],
                        start=(dc == 0), stop=(dc == 7)),
                           waits=(ln2_done + gl_free + regs_free[0] + [("sCW", 16)]
                                  if dc == 0 else []),
                           inc=("sPE" if dc == 7 else None))
                p_g = sch.ctr["sPE"]
                c1 = sch.ev("vector", lambda e: e.tensor_copy(gl, psC[:, 0:8]),
                            waits=[("sPE", p_g)], inc="sDVE")
                regs_free[0] = [("sDVE", c1)]
                gl_free = [("sDVE", c1)]
                c2 = sch.ev("vector", lambda e: e.reduce_max(gcol[:, 0:1], gl, axis=AX.X),
                            waits=[("sDVE", c1)], inc="sDVE")
                c3 = sch.ev("vector", lambda e: e.tensor_scalar(
                    gl2, gl, gcol[:, 0:1], None, ALU.is_ge), waits=[("sDVE", c2)], inc="sDVE")
                c4 = sch.ev("vector", lambda e: e.scalar_tensor_tensor(
                    gl2, gl2, -1e30, gl, ALU.mult, ALU.add), waits=[("sDVE", c3)], inc="sDVE")
                c5 = sch.ev("vector", lambda e: e.reduce_max(gcol[:, 1:2], gl2, axis=AX.X),
                            waits=[("sDVE", c4)], inc="sDVE")
                c6 = sch.ev("vector", lambda e: e.tensor_scalar(
                    gl2, gl, gcol[:, 1:2], None, ALU.is_ge), waits=[("sDVE", c5)], inc="sDVE")
                c7 = sch.ev("vector", lambda e: e.tensor_scalar(
                    gcol[:, 2:3], gcol[:, 0:1], -1.0, None, ALU.mult),
                            waits=[("sDVE", c2)], inc="sDVE")
                c8 = sch.ev("scalar", lambda e: e.activation(
                    gl3, gl, AF.Exp, bias=gcol[:, 2:3]),
                            waits=[("sDVE", c7), ("sDVE", c6)], inc="sACT")
                c9 = sch.ev("vector", lambda e: e.tensor_mul(gl3, gl3, gl2),
                            waits=[("sACT", c8)], inc="sDVE")
                c10 = sch.ev("vector", lambda e: e.reduce_sum(gcol[:, 3:4], gl3, axis=AX.X),
                             waits=[("sDVE", c9)], inc="sDVE")
                c11 = sch.ev("vector", lambda e: e.reciprocal(gcol[:, 4:5], gcol[:, 3:4]),
                             waits=[("sDVE", c10)], inc="sDVE")
                c12 = sch.ev("vector", lambda e: e.tensor_scalar(
                    wtok, gl3, gcol[:, 4:5], None, ALU.mult), waits=[("sDVE", c11)], inc="sDVE")
                p_t = sch.ev("tensor", lambda e: e.transpose(
                    psD.bitcast(BF16)[0:8, 0:128], wtok, ident_bf),
                             waits=[("sDVE", c12), ("sDVE", consts_d)] + tr_free + regs_free[2],
                             inc="sPE")
                c13 = sch.ev("vector", lambda e, tcl=tcl: e.tensor_copy(
                    w8_rows[:, tcl * 128:(tcl + 1) * 128], psD.bitcast(BF16)[0:8, 0:128]),
                             waits=[("sPE", p_t)], inc="sDVE")
                tr_free = [("sDVE", c13)]
                regs_free[2] = [("sDVE", c13)]
            w8_done = sch.ctr["sDVE"]
            ww = ap3(we_dram, toff, [[TPC, 8], [1, 1024]])
            m = sch.ev("gpsimd", lambda e, ww=ww: e.dma_start(out=ww, in_=w8_rows),
                       waits=[("sDVE", w8_done)], inc="sGP", k=16)
            we_done[half] = [("sGP", m)]

            for e_ in range(E + 1):
                shared = e_ == E
                j = half * E + e_
                if not shared:
                    emit_wb_load(j, half)
                g_ready = {}
                for hcg in range(4):
                    q = (half * (E + 1) + e_) * 4 + hcg
                    ld = w12_ld[q]
                    par = q % 2
                    for hh2 in range(4):
                        hc = hcg * 4 + hh2
                        co = hh2 * 128
                        for nb in range(2):
                            for dc in range(8):
                                sch.ev("tensor", lambda e, dc=dc, nb=nb, par=par, co=co: e.matmul(
                                    psA[:, nb * 512:(nb + 1) * 512],
                                    w1c4[par][:, dc * 512 + co:dc * 512 + co + 128],
                                    ln2[dc][:, nb * 512:(nb + 1) * 512],
                                    start=(dc == 0), stop=(dc == 7)),
                                       waits=(psA_free + [ld] + ln2_done
                                              if (nb == 0 and dc == 0) else []),
                                       inc=("sPE" if dc == 7 else None))
                        p_h1 = sch.ctr["sPE"]
                        for nb in range(2):
                            for dc in range(8):
                                sch.ev("tensor", lambda e, dc=dc, nb=nb, par=par, co=co: e.matmul(
                                    psB[:, nb * 512:(nb + 1) * 512],
                                    w2c4[par][:, dc * 512 + co:dc * 512 + co + 128],
                                    ln2[dc][:, nb * 512:(nb + 1) * 512],
                                    start=(dc == 0), stop=(dc == 7)),
                                       waits=(psB_free + [ld]
                                              if (nb == 0 and dc == 0) else []),
                                       inc=("sPE" if dc == 7 else None))
                        p_h2 = sch.ctr["sPE"]
                        a_si = sch.ev("scalar", lambda e: e.activation(s_tmp, psA, AF.Silu),
                                      waits=[("sPE", p_h1)] + stmp_free, inc="sACT")
                        d_g = sch.ev("vector", lambda e, hc=hc: e.tensor_mul(
                            g_sb[:, hc * 1024:(hc + 1) * 1024], s_tmp, psB),
                                     waits=[("sPE", p_h2), ("sACT", a_si)] + gch_free[hc // 8],
                                     inc="sDVE")
                        psA_free = [("sACT", a_si)]
                        psB_free = [("sDVE", d_g)]
                        stmp_free = [("sDVE", d_g)]
                        if not shared:
                            d_g = sch.ev("vector", lambda e, hc=hc, wpar=j % 2: e.tensor_mul(
                                g_sb[:, hc * 1024:(hc + 1) * 1024],
                                g_sb[:, hc * 1024:(hc + 1) * 1024], wb2[wpar]),
                                         waits=[wb_ld[j], ("sDVE", d_g)], inc="sDVE")
                        g_ready[hc] = d_g
                    w12_freed[q] = [("sPE", p_h2)]
                    if q + 2 < 2 * NQH:
                        emit_w12_load(q + 2)
                if not shared:
                    wb_freed[j] = [("sDVE", g_ready[15])]
                for ch in range(4):
                    k = (half * (E + 1) + e_) * 4 + ch
                    kpar = k % 2
                    gw = [("sDVE", g_ready[ch * 4 + 3]), w3_ld[k]]
                    for nb in range(2):
                        for dout in range(8):
                            r = (nb * 8 + dout) % 4
                            bank, cb = regs[r]
                            for i in range(4):
                                hc = ch * 4 + i
                                sch.ev("tensor", lambda e, i=i, hc=hc, dout=dout, nb=nb, kpar=kpar, bank=bank, cb=cb: e.matmul(
                                    bank[:, cb:cb + 512],
                                    w3c[kpar][:, i * 1024 + dout * 128:i * 1024 + dout * 128 + 128],
                                    g_sb[:, hc * 1024 + nb * 512:hc * 1024 + nb * 512 + 512],
                                    start=(i == 0), stop=(i == 3)),
                                       waits=(gw + regs_free[r] if i == 0 else []),
                                       inc=("sPE" if i == 3 else None))
                            p_w3 = sch.ctr["sPE"]
                            d_x = sch.ev("vector", lambda e, dout=dout, nb=nb, toff=toff, bank=bank, cb=cb: e.tensor_add(
                                x2[dout][:, toff + nb * 512:toff + (nb + 1) * 512],
                                x2[dout][:, toff + nb * 512:toff + (nb + 1) * 512],
                                bank[:, cb:cb + 512]),
                                         waits=[("sPE", p_w3)], inc="sDVE")
                            regs_free[r] = [("sDVE", d_x)]
                    w3_freed[k] = [("sPE", sch.ctr["sPE"])]
                    if ch % 2 == 1:
                        gch_free[ch // 2] = list(w3_freed[k])
                    if k + 2 < 2 * NKH:
                        emit_w3_load(k + 2)
            ln2_used = [("sPE", sch.ctr["sPE"])]
            x2_add_done = [("sDVE", sch.ctr["sDVE"])]

        for dc in range(8):
            ovv = ap3(OUT, dc * 128 * TPC, [[TPC, 128], [1, TPC]])
            sch.ev("scalar", lambda e, ovv=ovv, dc=dc: e.dma_start(out=ovv, in_=x2[dc]),
                   waits=x2_add_done, inc="sST", k=16)
        return es_pc

    ld0 = emit_const_loads()
    DBGL = nc.dram_tensor("DBG_LN1", [S, D], BF16, kind="ExternalOutput").ap() if DEBUG else None
    ln1_done = emit_ln_phase(x_rows, ln1_dram, S // 512, ld0, dbg_dram=DBGL)
    ln1own_done = emit_ln_phase(x_own_rows, ln1own_dram, TPC // 512, ld0)

    es_p1.close()

    es_attn = contextlib.ExitStack()
    sba = lambda name, shape, dt: es_attn.enter_context(nc.sbuf_tensor(name, shape, dt)).ap()
    wk_sb = sba("wk_sb", [128, 8 * 256], BF16)
    wv_sb = sba("wv_sb", [128, 8 * 256], BF16)
    wq_sb = wk_sb
    lnT = [sba(f"lnT{i}", [128, 8 * 512], BF16) for i in range(2)]
    K_aug = [sba(f"K_aug{hh}", [66, S], BF16) for hh in range(HPG)]
    V_sb = sba("V_sb", [128, 64 * 260], BF16)
    Q_sb = sba("Q_sb", [128, 2 * TPC], BF16)
    qh1 = sba("qh1", [65, TPC], BF16)
    qaugs = [{t: sba(f"qaug{t}{i}", [66, TPC], BF16) for t in "LBR"}
             for i in range(2)]
    rp_pad_sb = sba("rp_pad_sb", [128, W_PAD], BF16)
    rp01 = sba("rp01", [128, 2], BF16)
    pos_pool = [sba(f"pos{i}", [128, 768], F32) for i in range(2)]
    P_pool = [sba(f"P{i}", [128, 1024], BF16) for i in range(2)]
    rcp_sb = sba("rcp_sb", [65, 512], BF16)
    rb_sb = sba("rb_sb", [64, 512], BF16)
    o_tile = sba("o_tile", [64, 512], BF16)
    maxk_col2 = sba("maxk_col2", [128, 1], F32)
    ksub = sba("ksub", [64, 320], BF16)
    qn_row = sba("qn_row", [1, TPC], BF16)
    qn2_row = sba("qn2_row", [1, TPC], BF16)
    c_row = sba("c_row", [1, TPC], F32)
    tmp_row = sba("tmp_row", [1, TPC], F32)
    rp0_row = sba("rp0_row", [1, TPC], BF16)
    rp256_row = sba("rp256_row", [1, TPC], BF16)

    fence0 = [(s, v) for s, v in sch.ctr.items()]
    for hh in range(HPG):
        sch.ev("vector", lambda e, hh=hh: e.memset(K_aug[hh][64:66, :], 1.0),
               waits=fence0, inc="sDVE")
    fence0 = fence0 + [("sDVE", sch.ctr["sDVE"])]
    fence = fence0
    sfree_in = [fence0, fence0]
    ot_free = fence0
    psD_free = fence0
    prep_guards = []
    dbg_marks = []
    pos_prev = fence0
    for g in range(RUN_GROUPS):
        kv = emit_group_proj(g, fence, psD_free)
        out0 = {}
        for _ in emit_head_prep(g, 0, kv, psfree=kv["psd"], guards=fence, out=out0):
            pass
        preps = {0: out0}
        for hh in range(RUN_HEADS):
            bg = None
            if hh + 1 < RUN_HEADS:
                outn = {}
                bg = emit_head_prep(g, hh + 1, kv, psfree=preps[hh]["psD_user"],
                                    guards=preps[hh]["tail"] + list(prep_guards or fence),
                                    out=outn)
                preps[hh + 1] = outn
                if not INTERLEAVE_PREP:
                    for _ in bg:
                        pass
                    bg = None
            res = emit_head_attn(g, hh, preps[hh], sfree_in, ot_free, pos_prev, bg=bg)
            sfree_in = res["s_free"]
            ot_free = res["ot_done"]
            pos_prev = res["pv_last"]
            prep_guards = res["pv_last"] + res["exp_done"] + res["o_stored"] + [
                ("sGP", sch.ctr["sGP"]), ("sDVE", sch.ctr["sDVE"]),
                ("sPOS0", sch.ctr["sPOS0"]), ("sPOS1", sch.ctr["sPOS1"])] + [
                (s, sch.ctr[s]) for s in QSEMS + RPSEMS]
        psD_free = preps[RUN_HEADS - 1]["psD_user"]
        fence = prep_guards + [("sACT", sch.ctr["sACT"])]

    attn_done = prep_guards + [(s, v) for s, v in sch.ctr.items()]
    if DEBUG:
        DBG_OACC = nc.dram_tensor("DBG_OACC", [D, TPC], BF16, kind="ExternalOutput").ap()
        mo = sch.ev("scalar", lambda e: e.dma_start(out=DBG_OACC, in_=o_dram.ap()),
                    waits=attn_done, inc="sST", k=16)
        attn_done = attn_done + [("sST", mo)]

    es_attn.close()
    es_pc = emit_pc_pd(attn_done)

    st_total = sch.ctr["sST"]
    sch.ev("sync", lambda e: e.wait_ge(sch.sems["sST"], st_total))

    block = es.enter_context(nc.Block())
    sch.build(block)
    es.close()
    return nc


_prog_cache = {}


def _get_prog():
    if "nc" not in _prog_cache:
        _prog_cache["nc"] = build_program()
    return _prog_cache["nc"]



def stage_inputs(core, **inputs):
    b, cp = core // 4, core % 4
    x = np.asarray(inputs["L0_attn_x"])
    Wqkv = np.asarray(inputs["L0_attn_Wqkv"])
    bqkv = np.asarray(inputs["L0_attn_bqkv"])
    Wr = np.asarray(inputs["L0_attn_Wr"])
    br = np.asarray(inputs["L0_attn_br"])

    d = {}
    d["x_rows"] = np.ascontiguousarray(x[b])
    d["x_own_rows"] = np.ascontiguousarray(x[b, cp::4])
    d["xT_own"] = np.ascontiguousarray(x[b, cp::4].T)

    Wh = Wqkv.reshape(NH, 3, HD, D)
    d["WQ_T"] = np.ascontiguousarray(Wh[:, 0].reshape(D, D).T)
    d["WK_T"] = np.ascontiguousarray(Wh[:, 1].reshape(D, D).T)
    d["WV_T"] = np.ascontiguousarray(Wh[:, 2].reshape(D, D).T)
    bh = bqkv.reshape(NH, 3, HD)
    bq = bh[:, 0].reshape(D)
    bk = bh[:, 1].reshape(D)
    bv = bh[:, 2].reshape(D)
    d["bQKV"] = np.ascontiguousarray(
        np.concatenate([bq.reshape(8, 128).T, bk.reshape(8, 128).T, bv.reshape(8, 128).T], axis=1))
    WrB = np.concatenate([Wr.T, br[None, :]], axis=0)
    idx = np.clip(np.arange(261) - cp, 0, 256)
    d["WrBc"] = np.ascontiguousarray(WrB[:, idx])
    d["Wr0"] = np.ascontiguousarray(WrB[:, 0:1])
    d["Wr256"] = np.ascontiguousarray(WrB[:, 256:257])
    d["gate_w"] = np.ascontiguousarray(inputs["L0_ffn_gate_w"])
    for k in ("ew1", "ew2", "ew3", "sw1", "sw2", "sw3"):
        d[k] = np.ascontiguousarray(inputs["L0_ffn_" + k])
    return d


def kernel(**inputs):
    nc = _get_prog()
    in_maps = [stage_inputs(c, **inputs) for c in range(8)]
    res = run_bass_kernel_spmd(nc, in_maps, core_ids=list(range(8)))
    out = np.empty((B, S, D), dtype=np.float32)
    for c in range(8):
        b, cp = c // 4, c % 4
        out[b, cp::4, :] = res.results[c]["OUT"].T
    return out


# version 18
# speedup vs baseline: 1.0168x; 1.0164x over previous
import contextlib
import math
import os
import numpy as np
import ml_dtypes
import concourse.bass as bass
import concourse.mybir as mybir
from concourse.ap import AP
from concourse.bass_utils import run_bass_kernel_spmd

F32 = mybir.dt.float32
BF16 = mybir.dt.bfloat16
AF = mybir.ActivationFunctionType
ALU = mybir.AluOpType
AX = mybir.AxisListType

B, S, D = 2, 8192, 1024
NH, HD = 16, 64
RW = 128
R = 2 * RW + 1
E, HF = 8, 2048
TPC = S // 4
HPG, NG = 4, 4
W_PAD, KC = 1536, 768
EPS = 1e-5

DEBUG = False
INTERLEAVE_PREP = True
RUN_GROUPS = NG
RUN_HEADS = HPG
RUN_ATTN = True
PREP_STAGES = 9



class Sched:

    ENGS = ("sync", "gpsimd", "tensor", "scalar", "vector")

    def __init__(self, nc, es):
        self.nc = nc
        self.events = []
        self.sems = {}
        self.ctr = {}
        self.es = es

    def sem(self, name):
        self.sems[name] = self.es.enter_context(self.nc.semaphore(name))
        self.ctr[name] = 0

    def ev(self, eng, fn, waits=(), inc=None, k=1):
        waits = [(s, v) for (s, v) in waits if v > 0]
        if inc is not None:
            self.ctr[inc] += k
        self.events.append((eng, fn, waits, inc, k))
        return self.ctr[inc] if inc is not None else None

    def build(self, block):
        n_ev = len(self.events)
        for eng_name in self.ENGS:
            def body(eng, eng_name=eng_name):
                waited = {}
                for (e, fn, waits, inc, k) in self.events:
                    if e != eng_name:
                        continue
                    for (s, v) in waits:
                        if waited.get(s, 0) < v:
                            eng.wait_ge(self.sems[s], v)
                            waited[s] = v
                    inst = fn(eng)
                    if inc is not None:
                        inst.then_inc(self.sems[inc], k)
                assert len(self.events) == n_ev, "events created during replay!"
            getattr(block, eng_name)(body)


def classify(p, qic):
    lo = 128 * p - 4 * (128 * qic + 127) - 3
    hi = 128 * p + 127 - 4 * 128 * qic
    if lo >= 129:
        return 'R'
    if hi <= -129:
        return 'L'
    return 'S'


def classify_qb(p, qb):
    cs = {classify(p, qic) for qic in range(4 * qb, 4 * qb + 4)}
    if cs == {'R'}:
        return 'R'
    if cs == {'L'}:
        return 'L'
    return 'M'



def ap3(t, off, dims):
    return AP(t.tensor if isinstance(t, AP) else t, off, dims)


def build_program():
    nc = bass.Bass("TRN2", target_bir_lowering=False, debug=False,
                   detect_race_conditions=bool(os.environ.get("KRACE")))

    def inp(name, shape):
        return nc.dram_tensor(name, shape, F32, kind="ExternalInput").ap()

    x_rows = inp("x_rows", [S, D])
    x_own_rows = inp("x_own_rows", [TPC, D])
    xT_own = inp("xT_own", [D, TPC])
    WQ_T = inp("WQ_T", [D, D])
    WK_T = inp("WK_T", [D, D])
    WV_T = inp("WV_T", [D, D])
    bQKV = inp("bQKV", [128, 24])
    WrBc = inp("WrBc", [65, 261])
    Wr0 = inp("Wr0", [65, 1])
    Wr256 = inp("Wr256", [65, 1])
    gate_w_in = inp("gate_w", [D, E])
    ew1 = inp("ew1", [E, D, HF])
    ew2 = inp("ew2", [E, D, HF])
    ew3 = inp("ew3", [E, HF, D])
    sw1 = inp("sw1", [D, HF])
    sw2 = inp("sw2", [D, HF])
    sw3 = inp("sw3", [HF, D])

    OUT = nc.dram_tensor("OUT", [D, TPC], F32, kind="ExternalOutput").ap()

    ln1_dram = nc.dram_tensor("ln1_dram", [S, D], BF16, kind="Internal")
    ln1own_dram = nc.dram_tensor("ln1own_dram", [TPC, D], BF16, kind="Internal")
    rp_dram = [nc.dram_tensor(f"rp_dram{h}", [TPC, W_PAD], BF16, kind="Internal")
               for h in range(HPG)]

    ident_f32 = nc.inline_tensor(np.eye(128, dtype=np.float32), "ident_f32")
    ident_b16 = nc.inline_tensor(np.eye(128, dtype=ml_dtypes.bfloat16), "ident_b16")
    ones_col_f32 = nc.inline_tensor(np.ones((128, 1), dtype=np.float32), "ones_col")

    es = contextlib.ExitStack()
    sch = Sched(nc, es)
    for s in ("sLD", "sGP", "sSC", "sPE", "sACT", "sDVE", "sST",
              "sLN0", "sLN1", "sLNX0", "sLNX1", "sLNS0", "sLNS1",
              "sPOS0", "sPOS1", "sQAUG", "sQG1", "sQG2", "sQG3", "sCW",
              "sW120", "sW121", "sW30", "sW31", "sWB0", "sWB1"):
        sch.sem(s)
    QSEMS = ("sQAUG", "sQG1", "sQG2", "sQG3")
    RPSEMS = ("sLNX0", "sLNX1", "sLNS0", "sLNS1")

    sb = nc.alloc_sbuf_tensor
    ident_sb = sb("ident_sb", [128, 128], F32).ap()
    identb_sb = sb("identb_sb", [128, 128], BF16).ap()
    ones_sb = sb("ones_sb", [128, 1], F32).ap()
    bqkv_sb = sb("bqkv_sb", [128, 24], F32).ap()
    bq8_sb = sb("bq8_sb", [128, 8], F32).ap()
    wrbc8 = sb("wrbc8", [65, 261], BF16).ap()
    wr0_8 = sb("wr0_8", [65, 1], BF16).ap()
    wr256_8 = sb("wr256_8", [65, 1], BF16).ap()

    nrmWr = sb("nrmWr", [1, 2], F32).ap()
    gate_sb = sb("gate_sb", [128, 8 * 8], BF16).ap()
    constcols = sb("constcols", [128, 4], F32).ap()
    ones_bf = sb("ones_bf", [128, 1], BF16).ap()
    onesrow65 = sb("onesrow65", [65, 128], BF16).ap()
    wr01_8 = sb("wr01_8", [65, 2], BF16).ap()
    wr01_f = sb("wr01_f", [65, 2], F32).ap()
    scal_sb = sb("scal_sb", [1, 16], F32).ap()
    maxbr64 = sb("maxbr64", [65, 1], F32).ap()

    maxk_row = sb("maxk_row", [1, NH], F32).ap()
    o_dram = nc.dram_tensor("o_dram", [D, TPC], BF16, kind="Internal")

    psA = nc.alloc_psum_tensor("psA", [128, 1024], F32).ap()
    psB = nc.alloc_psum_tensor("psB", [128, 1024], F32).ap()
    psC = nc.alloc_psum_tensor("psC", [128, 1024], F32).ap()
    psD = nc.alloc_psum_tensor("psD", [128, 1024], F32).ap()

    CONST_COLS = {}

    def constcol(val):
        if val not in CONST_COLS:
            i = len(CONST_COLS)
            CONST_COLS[val] = constcols[:, i:i + 1]
            sch.ev("vector", lambda e, i=i, val=val: e.memset(constcols[:, i:i + 1], val), inc="sDVE")
        return CONST_COLS[val]

    def emit_const_loads():
        sch.ev("sync", lambda e: e.dma_start(out=ident_sb, in_=ident_f32.ap()), inc="sLD", k=16)
        sch.ev("sync", lambda e: e.dma_start(out=identb_sb, in_=ident_b16.ap()), inc="sLD", k=16)
        sch.ev("sync", lambda e: e.dma_start(out=ones_sb, in_=ones_col_f32.ap()), inc="sLD", k=16)
        sch.ev("sync", lambda e: e.dma_start(out=bqkv_sb, in_=bQKV), inc="sLD", k=16)
        sch.ev("sync", lambda e: e.dma_start(out=wr_stat[:, 0:261], in_=WrBc), inc="sLD", k=16)
        sch.ev("sync", lambda e: e.dma_start(out=wr01_f[:, 0:1], in_=Wr0), inc="sLD", k=16)
        sch.ev("sync", lambda e: e.dma_start(out=wr01_f[:, 1:2], in_=Wr256), inc="sLD", k=16)
        ld0 = sch.ctr["sLD"]
        gv = ap3(gate_w_in, 0, [[8, 128], [128 * 8, 8], [1, 8]])
        go = ap3(gate_sb, 0, [[64, 128], [8, 8], [1, 8]])
        sch.ev("gpsimd", lambda e: e.dma_start(out=go, in_=gv), inc="sCW", k=16)
        sch.ev("vector", lambda e: e.tensor_scalar(wrbc8[0:64, :], wr_stat[0:64, :], 8.0, None, ALU.mult),
               waits=[("sLD", ld0)], inc="sDVE")
        sch.ev("vector", lambda e: e.memset(scal_sb, 0.0), inc="sDVE")
        sch.ev("vector", lambda e: e.memset(maxk_row, 0.0), inc="sDVE")
        d_wr = sch.ev("vector", lambda e: e.tensor_copy(wrbc8[64:65, :], wr_stat[64:65, :]), inc="sDVE")
        constcol(EPS)
        sch.ev("vector", lambda e: e.memset(ones_bf, 1.0), inc="sDVE")
        sch.ev("vector", lambda e: e.memset(onesrow65[64:65, :], 1.0), inc="sDVE")
        ld1 = sch.ctr["sLD"]
        sch.ev("vector", lambda e: e.tensor_scalar(wr01_8[0:64, :], wr01_f[0:64, :], 8.0, None, ALU.mult),
               waits=[("sLD", ld1)], inc="sDVE")
        sch.ev("vector", lambda e: e.tensor_copy(wr01_8[64:65, :], wr01_f[64:65, :]), inc="sDVE")
        sch.ev("vector", lambda e: e.tensor_scalar(bq8_sb, bqkv_sb[:, 0:8], 0.125, None, ALU.mult),
               waits=[("sLD", ld0)], inc="sDVE")
        d_br = sch.ev("vector", lambda e: e.reduce_max(maxbr64[64:65, 0:1], wrbc8[64:65, 0:261], axis=AX.X),
                      waits=[("sDVE", d_wr)], inc="sDVE")
        sch.ev("gpsimd", lambda e: e.dma_start(out=ap3(scal_sb, 1, [[16, 1], [1, 1]]),
                                               in_=ap3(maxbr64, 64, [[1, 1], [1, 1]])),
               waits=[("sDVE", d_br)], inc="sGP", k=16)
        a_sq = sch.ev("scalar", lambda e: e.activation(wr_sq, wrbc8[0:64, :], AF.Square),
                      waits=[("sDVE", d_wr)], inc="sACT")
        p_nw = sch.ev("tensor", lambda e: e.matmul(psA[0:1, 0:261], ones_bf[0:64, :], wr_sq,
                                                   start=True, stop=True),
                      waits=[("sACT", a_sq), ("sDVE", sch.ctr["sDVE"])], inc="sPE")
        a_nw = sch.ev("scalar", lambda e: e.activation(row261, psA[0:1, 0:261], AF.Sqrt),
                      waits=[("sPE", p_nw)], inc="sACT")
        sch.ev("vector", lambda e: e.reduce_max(scal_sb[0:1, 0:1], row261, axis=AX.X),
               waits=[("sACT", a_nw)], inc="sDVE")
        return ld0

    es_p1 = contextlib.ExitStack()
    sbp = lambda name, shape, dt: es_p1.enter_context(nc.sbuf_tensor(name, shape, dt)).ap()
    wr_stat = sbp("wr_stat", [65, 261], F32)
    row261 = sbp("row261", [1, 261], F32)
    wr_sq = sbp("wr_sq", [64, 261], BF16)
    p1_bufx = [sbp(f"p1x{i}", [128, 4096], F32) for i in range(2)]
    p1_bufl = [sbp(f"p1l{i}", [128, 4096], BF16) for i in range(2)]
    p1_sq = [sbp(f"p1sq{i}", [128, 1024], F32) for i in range(2)]
    p1_stat = [sbp(f"p1st{i}", [128, 24], F32) for i in range(2)]

    def emit_ln_phase(src_ap, dst_dram, nblk, ld0, dbg_dram=None):
        prev_dve = sch.ctr["sDVE"]
        prev_act = sch.ctr["sACT"]
        prev_st = {0: sch.ctr["sLNS0"], 1: sch.ctr["sLNS1"]}
        entry = [("sDVE", prev_dve), ("sACT", prev_act)]
        ap_done = {}
        st_done = {}
        dbg_guard = {}
        for b in range(nblk):
            bx, bl = p1_bufx[b % 2], p1_bufl[b % 2]
            sq, st = p1_sq[b % 2], p1_stat[b % 2]
            ldsem = f"sLNX{b % 2}"
            stsem = f"sLNS{b % 2}"
            waits = [("sLD", ld0)]
            if b >= 2:
                waits.append(("sACT", ap_done[b - 2]))
                if (b - 2) in dbg_guard:
                    waits.append(("sST", dbg_guard[b - 2]))
            else:
                waits += entry
            src = ap3(src_ap, b * 512 * D, [[D, 128], [128 * D, 4], [1, D]])
            dst_sb = ap3(bx, 0, [[4096, 128], [1024, 4], [1, 1024]])
            sch.ev("sync", lambda e, src=src, dst_sb=dst_sb: e.dma_start(out=dst_sb, in_=src),
                   waits=waits, inc=ldsem, k=16)
            ldb = sch.ctr[ldsem]
            w_prev = [("sACT", ap_done[b - 2])] if b >= 2 else []
            for a in range(4):
                xt = bx[:, a * 1024:(a + 1) * 1024]
                sch.ev("vector", lambda e, xt=xt, st=st, a=a: e.reduce_sum(st[:, a:a + 1], xt, axis=AX.X),
                       waits=[(ldsem, ldb)] + w_prev, inc="sDVE")
                sch.ev("scalar", lambda e, xt=xt, sq=sq, st=st, a=a: e.activation(
                    sq, xt, AF.Square, accum_out=st[:, 4 + a:5 + a]),
                       waits=[(ldsem, ldb)] + w_prev, inc="sACT")
            d_sums = sch.ctr["sDVE"]
            a_sqs = sch.ctr["sACT"]
            m1 = sch.ev("vector", lambda e, st=st: e.tensor_scalar(st[:, 8:12], st[:, 0:4], 1.0 / D, None, ALU.mult),
                        waits=[("sDVE", d_sums)], inc="sDVE")
            m2 = sch.ev("vector", lambda e, st=st: e.tensor_scalar(st[:, 12:16], st[:, 4:8], 1.0 / D, None, ALU.mult),
                        waits=[("sACT", a_sqs)], inc="sDVE")
            m3 = sch.ev("vector", lambda e, st=st: e.tensor_mul(st[:, 16:20], st[:, 8:12], st[:, 8:12]),
                        waits=[("sDVE", m1)], inc="sDVE")
            m4 = sch.ev("vector", lambda e, st=st: e.tensor_sub(st[:, 12:16], st[:, 12:16], st[:, 16:20]),
                        waits=[("sDVE", m3)], inc="sDVE")
            epsc = constcol(EPS)
            sch.ev("scalar", lambda e, st=st, epsc=epsc: e.activation(st[:, 16:20], st[:, 12:16], AF.Sqrt, bias=epsc),
                   waits=[("sDVE", m4)], inc="sACT")
            a_rt = sch.ctr["sACT"]
            m5 = sch.ev("vector", lambda e, st=st: e.reciprocal(st[:, 20:24], st[:, 16:20]),
                        waits=[("sACT", a_rt)], inc="sDVE")
            m6 = sch.ev("vector", lambda e, st=st: e.scalar_tensor_tensor(
                st[:, 0:4], st[:, 8:12], -1.0, st[:, 20:24], ALU.mult, ALU.mult),
                        waits=[("sDVE", m5)], inc="sDVE")
            w_st = ([(stsem, st_done[b - 2])] if b >= 2
                    else [(stsem, prev_st[b % 2])])
            for a in range(4):
                xt = bx[:, a * 1024:(a + 1) * 1024]
                lt = bl[:, a * 1024:(a + 1) * 1024]
                sch.ev("scalar", lambda e, xt=xt, lt=lt, st=st, a=a: e.activation(
                    lt, xt, AF.Identity, bias=st[:, 0 + a:1 + a], scale=st[:, 20 + a:21 + a]),
                       waits=w_st + [("sDVE", m6)], inc="sACT")
            ap_done[b] = sch.ctr["sACT"]
            dstv = ap3(dst_dram, b * 512 * D, [[D, 128], [128 * D, 4], [1, D]])
            srcv = ap3(bl, 0, [[4096, 128], [1024, 4], [1, 1024]])
            sch.ev("gpsimd", lambda e, dstv=dstv, srcv=srcv: e.dma_start(out=dstv, in_=srcv),
                   waits=[("sACT", ap_done[b])], inc=stsem, k=16)
            if dbg_dram is not None:
                dbv = ap3(dbg_dram, b * 512 * D, [[D, 128], [128 * D, 4], [1, D]])
                sch.ev("scalar", lambda e, dbv=dbv, srcv=srcv: e.dma_start(out=dbv, in_=srcv),
                       waits=[("sACT", ap_done[b])], inc="sST", k=16)
            if dbg_dram is not None and b == 0:
                D0S = nc.dram_tensor("DBG0_ST", [128, 24], F32, kind="ExternalOutput").ap()
                D0X = nc.dram_tensor("DBG0_X", [128, 4096], F32, kind="ExternalOutput").ap()
                D0L = nc.dram_tensor("DBG0_L", [128, 4096], BF16, kind="ExternalOutput").ap()
                sch.ev("scalar", lambda e, st=st: e.dma_start(out=D0S, in_=st),
                       waits=[("sACT", ap_done[b])], inc="sST", k=16)
                sch.ev("scalar", lambda e, bx=bx: e.dma_start(out=D0X, in_=bx),
                       waits=[("sACT", ap_done[b])], inc="sST", k=16)
                sch.ev("scalar", lambda e, bl=bl: e.dma_start(out=D0L, in_=bl),
                       waits=[("sACT", ap_done[b])], inc="sST", k=16)
                dbg_guard[b] = sch.ctr["sST"]
            st_done[b] = sch.ctr[stsem]
        return st_done[nblk - 1]


    JB = S // 512
    QBn = TPC // 512

    def emit_group_proj(g, fence, psd_in):
        for (wsb, wt) in ((wk_sb, WK_T), (wv_sb, WV_T)):
            iv = ap3(wt, 256 * g, [[D, 128], [128 * D, 8], [1, 256]])
            ov = ap3(wsb, 0, [[2048, 128], [256, 8], [1, 256]])
            sch.ev("gpsimd", lambda e, iv=iv, ov=ov: e.dma_start(out=ov, in_=iv),
                   waits=fence, inc="sGP", k=16)
        w_w = sch.ctr["sGP"]
        mm_done = {}
        kb_free = {"A0": fence, "A1": fence, "B0": fence, "B1": fence, "C0": fence, "C1": fence}
        kv_mile = {}
        ln_ld = {}
        for jb in range(JB):
            buf = lnT[jb % 2]
            w_ln = list(fence)
            if jb >= 2:
                w_ln = [("sPE", mm_done[jb - 2])]
            lsem = f"sLN{jb % 2}"
            for dc in range(8):
                iv = ap3(ln1_dram, jb * 512 * D + dc * 128, [[D, 512], [1, 128]])
                sch.ev("sync" if dc < 4 else "scalar",
                       lambda e, buf=buf, dc=dc, iv=iv: e.dma_start_transpose(
                           out=buf[:, dc * 512:(dc + 1) * 512], in_=iv),
                       waits=w_ln, inc=lsem, k=16)
            ln_ld[jb] = sch.ctr[lsem]
            for fc in range(2):
                w_mm = [(lsem, ln_ld[jb]), ("sGP", w_w)] + kb_free[f"A{fc}"]
                for dc in range(8):
                    sch.ev("tensor", lambda e, fc=fc, dc=dc, buf=buf: e.matmul(
                        psA[:, fc * 512:(fc + 1) * 512],
                        wk_sb[:, dc * 256 + fc * 128: dc * 256 + fc * 128 + 128],
                        buf[:, dc * 512:(dc + 1) * 512],
                        start=(dc == 0), stop=(dc == 7)),
                           waits=(w_mm if dc == 0 else []), inc=("sPE" if dc == 7 else None))
                p_mm = sch.ctr["sPE"]
                hhA, hhB = 2 * fc, 2 * fc + 1
                bcol = 8 + 2 * g + fc
                aev = sch.ev("scalar", lambda e, hhA=hhA, fc=fc, jb=jb, bcol=bcol: e.activation(
                    K_aug[hhA][0:64, jb * 512:(jb + 1) * 512], psA[0:64, fc * 512:(fc + 1) * 512],
                    AF.Identity, bias=bqkv_sb[0:64, bcol:bcol + 1]),
                       waits=[("sPE", p_mm)], inc="sACT")
                dev = sch.ev("vector", lambda e, hhB=hhB, fc=fc, jb=jb, bcol=bcol: e.tensor_scalar(
                    K_aug[hhB][0:64, jb * 512:(jb + 1) * 512], psA[64:128, fc * 512:(fc + 1) * 512],
                    bqkv_sb[64:128, bcol:bcol + 1], None, ALU.add),
                       waits=[("sPE", p_mm)], inc="sDVE")
                kb_free[f"A{fc}"] = [("sACT", aev), ("sDVE", dev)]
            for sub in range(4):
                jc = 4 * jb + sub
                half = sub % 2
                w_mm = [(lsem, ln_ld[jb]), ("sGP", w_w)] + kb_free[f"B{half}"]
                for dc in range(8):
                    sch.ev("tensor", lambda e, half=half, dc=dc, sub=sub, buf=buf: e.matmul(
                        psB[:, half * 512: half * 512 + 256],
                        buf[:, dc * 512 + sub * 128: dc * 512 + sub * 128 + 128],
                        wv_sb[:, dc * 256:(dc + 1) * 256],
                        start=(dc == 0), stop=(dc == 7)),
                           waits=(w_mm if dc == 0 else []), inc=("sPE" if dc == 7 else None))
                p_mm = sch.ctr["sPE"]
                srcv = ap3(psB, half * 512, [[1024, 128], [64, 4], [1, 64]])
                dstv = ap3(V_sb, jc * 260, [[64 * 260, 128], [65, 4], [1, 64]])
                dev = sch.ev("vector", lambda e, srcv=srcv, dstv=dstv: e.tensor_copy(dstv, srcv),
                             waits=[("sPE", p_mm)], inc="sDVE")
                kb_free[f"B{half}"] = [("sDVE", dev)]
            mm_done[jb] = sch.ctr["sPE"]
        ov = ap3(V_sb, 64, [[64 * 260, 128], [260, 64], [65, 4], [1, 1]])
        v_ones = sch.ev("vector", lambda e, ov=ov: e.memset(ov, 1.0),
                        waits=fence, inc="sDVE")
        ivq = ap3(WQ_T, 256 * g, [[D, 128], [128 * D, 8], [1, 256]])
        ovq = ap3(wq_sb, 0, [[2048, 128], [256, 8], [1, 256]])
        sch.ev("gpsimd", lambda e, ivq=ivq, ovq=ovq: e.dma_start(out=ovq, in_=ivq),
               waits=[("sPE", mm_done[JB - 1])], inc="sGP", k=16)
        w_wq = sch.ctr["sGP"]
        psDb = psD.bitcast(BF16)
        q_copy = {}
        q_mm = {}
        pool_free = {0: list(psd_in) + fence, 1: list(psd_in) + fence}
        region_free = {0: list(psd_in), 1: list(psd_in)}
        for qb in range(QBn):
            buf = lnT[qb % 2]
            w_ln = ([("sPE", mm_done[JB - 2 + (qb % 2)])] if qb < 2
                    else [("sPE", q_mm[qb - 2])])
            for j in range(4):
                k = qb * 4 + j
                lsem = f"sLN{k % 2}"
                tok_buf = P_pool[k % 2]
                iv = ap3(ln1own_dram, (qb * 512 + j * 128) * D, [[D, 128], [1, D]])
                sch.ev("sync" if k % 2 == 0 else "scalar",
                       lambda e, tok_buf=tok_buf, iv=iv: e.dma_start(out=tok_buf, in_=iv),
                       waits=pool_free[k % 2], inc=lsem, k=16)
                ld_k = sch.ctr[lsem]
                p_tr = None
                for dc in range(8):
                    r = (k * 8 + dc) % 2
                    p_tr = sch.ev("tensor", lambda e, tok_buf=tok_buf, dc=dc, r=r: e.transpose(
                        psDb[:, r * 1024:r * 1024 + 128],
                        tok_buf[:, dc * 128:(dc + 1) * 128], identb_sb),
                                  waits=[(lsem, ld_k)] + region_free[r], inc="sPE")
                    d_c = sch.ev("vector", lambda e, buf=buf, dc=dc, j=j, r=r: e.tensor_copy(
                        buf[:, dc * 512 + j * 128:dc * 512 + j * 128 + 128],
                        psDb[:, r * 1024:r * 1024 + 128]),
                                 waits=[("sPE", p_tr)] + w_ln, inc="sDVE")
                    region_free[r] = [("sDVE", d_c)]
                pool_free[k % 2] = [("sPE", p_tr)]
            q_copy[qb] = sch.ctr["sDVE"]
            for fc in range(2):
                w_mm = [("sDVE", q_copy[qb]), ("sGP", w_wq)] + kb_free[f"C{fc}"]
                for dc in range(8):
                    sch.ev("tensor", lambda e, fc=fc, dc=dc, buf=buf: e.matmul(
                        psC[:, fc * 512:(fc + 1) * 512],
                        wq_sb[:, dc * 256 + fc * 128: dc * 256 + fc * 128 + 128],
                        buf[:, dc * 512:(dc + 1) * 512],
                        start=(dc == 0), stop=(dc == 7)),
                           waits=(w_mm if dc == 0 else []), inc=("sPE" if dc == 7 else None))
                p_mm = sch.ctr["sPE"]
                aev = sch.ev("scalar", lambda e, fc=fc, qb=qb, g=g: e.activation(
                    Q_sb[:, fc * TPC + qb * 512: fc * TPC + qb * 512 + 512],
                    psC[:, fc * 512:(fc + 1) * 512],
                    AF.Identity, scale=0.125, bias=bq8_sb[:, 2 * g + fc:2 * g + fc + 1]),
                       waits=[("sPE", p_mm)], inc="sACT")
                kb_free[f"C{fc}"] = [("sACT", aev)]
            q_mm[qb] = sch.ctr["sPE"]
        kv_ready = {"k": [("sACT", sch.ctr["sACT"]), ("sDVE", sch.ctr["sDVE"])],
                    "q": [("sACT", sch.ctr["sACT"])],
                    "v": [("sDVE", max(v_ones, sch.ctr["sDVE"]))],
                    "psd": [("sPE", sch.ctr["sPE"]), ("sDVE", sch.ctr["sDVE"])]}
        return kv_ready

    def emit_head_prep(g, hh, kv_ready, psfree, guards, out):
        h = 4 * g + hh
        qaug = qaugs[hh % 2]
        QSEM, RPSEM = QSEMS[hh], RPSEMS[hh]
        fc, rhalf = hh // 2, 64 * (hh % 2)
        wq = kv_ready["q"] + kv_ready["k"] + list(guards)
        d_q = sch.ev("vector", lambda e, fc=fc, rhalf=rhalf: e.tensor_copy(
            qh1[0:64, :], Q_sb[rhalf:rhalf + 64, fc * TPC:(fc + 1) * TPC]),
                     waits=wq, inc="sDVE")
        d_q = sch.ev("vector", lambda e: e.memset(qh1[64:65, :], 1.0), waits=wq, inc="sDVE")
        for t in "LBR":
            sch.ev("vector", lambda e, t=t, fc=fc, rhalf=rhalf: e.tensor_copy(
                qaug[t][0:64, :], Q_sb[rhalf:rhalf + 64, fc * TPC:(fc + 1) * TPC]),
                   waits=wq, inc="sDVE")
        d_base = sch.ctr["sDVE"]
        out["qaug"] = [("sDVE", d_base)]
        yield
        bank0_free = list(psfree)
        d_ks = sch.ev("vector", lambda e, hh=hh: e.tensor_copy(
            ksub[:, 0:256], ap3(K_aug[hh], 0, [[S, 64], [32, 256]])),
                      waits=kv_ready["k"] + list(guards), inc="sDVE")
        mg = 192 * h + 64
        d_ks = sch.ev("vector", lambda e, hh=hh, mg=mg: e.tensor_copy(
            ksub[:, 256:320], K_aug[hh][0:64, mg:mg + 64]),
                      waits=kv_ready["k"] + list(guards) + [("sDVE", d_ks)], inc="sDVE")
        tr_free = list(psfree)
        mm_free = list(bank0_free)
        for qc in range(16):
            p_m = sch.ev("tensor", lambda e, qc=qc: e.matmul(
                psD[:, 0:320], qh1[0:64, qc * 128:(qc + 1) * 128], ksub,
                start=True, stop=True),
                         waits=mm_free + [("sDVE", d_ks), ("sDVE", d_q)], inc="sPE")
            d_m = sch.ev("vector", lambda e: e.reduce_max(maxk_col2, psD[:, 0:320], axis=AX.X),
                         waits=[("sPE", p_m)], inc="sDVE")
            mm_free = [("sDVE", d_m)]
            yield
            p_tt = sch.ev("tensor", lambda e: e.transpose(psD[0:1, 640:768], maxk_col2,
                                                          ident_sb),
                          waits=[("sDVE", d_m)] + tr_free, inc="sPE")
            d_mc = sch.ev("vector", lambda e, qc=qc: e.tensor_copy(
                tmp_row[0:1, qc * 128:(qc + 1) * 128], psD[0:1, 640:768]),
                          waits=[("sPE", p_tt)], inc="sDVE")
            tr_free = [("sDVE", d_mc)]
            yield
        rfree = mm_free + tr_free
        ctr_free = list(guards)
        r01_free = list(guards)
        r01_psum = list(psfree)
        pe_w = rfree
        mp_free = tr_free
        for qic in range(16):
            p_rp = sch.ev("tensor", lambda e, qic=qic: e.matmul(
                psD[:, 0:261], qh1[:, qic * 128:(qic + 1) * 128], wrbc8[:, 0:261],
                start=True, stop=True),
                          waits=pe_w + [("sDVE", d_q)], inc="sPE")
            p_r01 = sch.ev("tensor", lambda e, qic=qic: e.matmul(
                psD[:, 512:514], qh1[:, qic * 128:(qic + 1) * 128], wr01_8[:, 0:2],
                start=True, stop=True), waits=r01_psum, inc="sPE")
            a_c = sch.ev("scalar", lambda e: e.activation(
                rp_pad_sb[:, 640:901], psD[:, 0:261], AF.Copy),
                         waits=[("sPE", p_r01)] + ctr_free, inc="sACT")
            pe_w = [("sACT", a_c)]
            d_mp = sch.ev("vector", lambda e: e.reduce_max(maxk_col2, psD[:, 0:261], axis=AX.X),
                          waits=[("sPE", p_r01), ("sACT", a_c)], inc="sDVE")
            yield
            p_tp = sch.ev("tensor", lambda e: e.transpose(psD[0:1, 640:768], maxk_col2,
                                                          ident_sb),
                          waits=[("sDVE", d_mp)] + mp_free, inc="sPE")
            d_mq = sch.ev("vector", lambda e, qic=qic: e.tensor_copy(
                qn_row[0:1, qic * 128:(qic + 1) * 128], psD[0:1, 640:768]),
                          waits=[("sPE", p_tp)], inc="sDVE")
            mp_free = [("sDVE", d_mq)]
            pe_w.append(("sPE", p_tp))
            d_c2 = sch.ev("vector", lambda e: e.tensor_copy(rp01[:, 0:2], psD[:, 512:514]),
                          waits=[("sPE", p_tp)] + r01_free, inc="sDVE")
            d_l = sch.ev("vector", lambda e: e.tensor_copy(
                rp_pad_sb[:, 0:640], ap3(rp01, 0, [[2, 128], [0, 640]])),
                         waits=[("sDVE", d_c2)] + ctr_free, inc="sDVE")
            d_r2 = sch.ev("vector", lambda e: e.tensor_copy(
                rp_pad_sb[:, 901:W_PAD], ap3(rp01, 1, [[2, 128], [0, W_PAD - 901]])),
                         waits=[("sDVE", d_l)], inc="sDVE")
            pe_w.append(("sDVE", d_r2))
            ov = ap3(rp_dram[hh], qic * 128 * W_PAD, [[W_PAD, 128], [1, W_PAD]])
            gc = sch.ev("sync", lambda e, ov=ov: e.dma_start(out=ov, in_=rp_pad_sb),
                        waits=[("sACT", a_c), ("sDVE", d_r2)], inc=RPSEM, k=16)
            ctr_free = [(RPSEM, gc)]
            r01_free = [(RPSEM, gc)]
            r01_psum = [("sDVE", d_c2), ("sDVE", d_mq)]
            yield
        out["rp_stored"] = [(RPSEM, sch.ctr[RPSEM])]
        d_c = sch.ev("vector", lambda e: e.scalar_tensor_tensor(
            c_row, tmp_row, 8.0, qn_row, ALU.add, ALU.add),
                     waits=mp_free + [("sDVE", sch.ctr["sDVE"])], inc="sDVE")
        rfree2 = pe_w
        for qb in range(QBn):
            for col in range(2):
                row = rp0_row if col == 0 else rp256_row
                p_r = sch.ev("tensor", lambda e, qb=qb, col=col: e.matmul(
                    psD[0:1, 0:512], wr01_8[:, col:col + 1], qh1[:, qb * 512:(qb + 1) * 512],
                    start=True, stop=True),
                             waits=rfree2 + [("sDVE", d_q)], inc="sPE")
                d_r = sch.ev("vector", lambda e, qb=qb, row=row: e.tensor_copy(
                    row[0:1, qb * 512:(qb + 1) * 512], psD[0:1, 0:512]),
                             waits=[("sPE", p_r)], inc="sDVE")
                rfree2 = [("sDVE", d_r)]
                yield
        rfree = rfree2
        t_marks = []
        qn_free = [("sDVE", d_c)]
        for t, mk, fine_row in (("L", rp0_row, rp0_row), ("R", rp256_row, rp256_row),
                                ("B", None, rp0_row)):
            if mk is None:
                d_t = sch.ev("vector", lambda e: e.tensor_scalar(
                    tmp_row, c_row, -1.0, None, ALU.mult),
                             waits=[("sDVE", d_c)], inc="sDVE")
            else:
                d_t = sch.ev("vector", lambda e, mk=mk: e.tensor_sub(tmp_row, mk, c_row),
                             waits=rfree + [("sDVE", d_c)], inc="sDVE")
            d_co = sch.ev("vector", lambda e: e.tensor_copy(qn2_row, tmp_row),
                          waits=[("sDVE", d_t)] + qn_free, inc="sDVE")
            g_co = sch.ev("gpsimd", lambda e, t=t: e.dma_start(
                out=qaug[t][64:65, :], in_=qn2_row), waits=[("sDVE", d_co)], inc=QSEM, k=16)
            d_f = sch.ev("vector", lambda e, fine_row=fine_row: e.tensor_sub(
                fine_row, tmp_row, qn2_row), waits=[("sDVE", d_co)], inc="sDVE")
            g_f = sch.ev("gpsimd", lambda e, t=t, fine_row=fine_row: e.dma_start(
                out=qaug[t][65:66, :], in_=fine_row),
                         waits=[("sDVE", d_f)], inc=QSEM, k=16)
            qn_free = [(QSEM, g_f)]
            t_marks.append((QSEM, g_f))
            yield
        out["qaug"] = t_marks + [("sDVE", d_base)]
        out["rp_stored"] = [(RPSEM, sch.ctr[RPSEM])]
        out["psD_user"] = pe_w + rfree + mp_free
        out["tail"] = [("sDVE", sch.ctr["sDVE"]), ("sACT", sch.ctr["sACT"]),
                       ("sPE", sch.ctr["sPE"]), (RPSEM, sch.ctr[RPSEM]),
                       (QSEM, sch.ctr[QSEM])]

    ot_chain = {}

    def emit_head_attn(g, hh, prep, sfree_in, ot_free, pos_entry, bg=None):
        h = 4 * g + hh
        qaug = qaugs[hh % 2]
        sbanks = [psA, psB]
        ot_dma_free = ot_chain
        s_free = {0: list(sfree_in[0]), 1: list(sfree_in[1])}
        exp_mile = {}
        pv_mile = {}
        pos_ld = {}
        pos_free = {}
        s_stop = {}
        O_T = psC
        ot_in = list(ot_free)
        for qbp in range(2):

            def load_pos(qic):
                p_first = max(0, 4 * qic - 1)
                n_p = min(63, 4 * qic + 4) - p_first + 1
                width = 128 * n_p
                off = qic * 128 * W_PAD + KC + (128 * p_first - 512 * qic)
                iv = ap3(rp_dram[hh], off, [[W_PAD - 4, 128], [1, width]])
                buf = pos_pool[qic % 2]
                psem = f"sPOS{qic % 2}"
                w = prep["rp_stored"] + pos_free.get(qic - 2, list(pos_entry))
                sch.ev("gpsimd", lambda e, buf=buf, iv=iv, width=width: e.dma_start(
                    out=buf[:, 0:width], in_=iv), waits=w, inc=psem, k=16)
                pos_ld[qic] = (psem, sch.ctr[psem])

            qic_lo, qic_hi = 8 * qbp, 8 * qbp + 8

            def emit_S(p, qbp=qbp, qic_lo=qic_lo, qic_hi=qic_hi):
                if qbp == 0 and p == 0:
                    load_pos(0)
                for qic in range(qic_lo, qic_hi):
                    if qic >= 1 and max(1, 4 * qic - 3) == p and qic not in pos_ld:
                        load_pos(qic)
                bank = sbanks[p % 2]
                ops = []
                for qb2 in range(2):
                    qb = 2 * qbp + qb2
                    cls = classify_qb(p, qb)
                    if cls in "LR":
                        ops.append(("far512", qb2, cls, None))
                    else:
                        for qic in range(4 * qb, 4 * qb + 4):
                            c = classify(p, qic)
                            if c == 'S':
                                ops.append(("band", qb2, c, qic))
                            else:
                                ops.append(("far128", qb2, c, qic))
                started = {0: False, 1: False}
                n_emitted = 0
                w_first = s_free[p % 2] + prep["qaug"]
                pos_w = sorted({o[3] for o in ops if o[0] == "band"})
                for qic in pos_w:
                    w_first = w_first + [pos_ld[qic]]
                last_by_half = {}
                for idx, (kind, qb2, c, qic) in enumerate(ops):
                    if kind == "band":
                        last_by_half[qb2] = idx
                    else:
                        last_by_half[qb2] = idx
                seen = {0: 0, 1: 0}
                tot = {0: sum(1 for o in ops if o[1] == 0) + sum(1 for o in ops if o[1] == 0 and o[0] == "band"),
                       1: sum(1 for o in ops if o[1] == 1) + sum(1 for o in ops if o[1] == 1 and o[0] == "band")}
                for (kind, qb2, c, qic) in ops:
                    qb = 2 * qbp + qb2
                    first = not started[qb2]
                    started[qb2] = True
                    if kind == "far512":
                        seen[qb2] += 1
                        stop = seen[qb2] == tot[qb2]
                        sch.ev("tensor", lambda e, bank=bank, qb2=qb2, c=c, qb=qb, p=p, hh=hh, first=first, stop=stop: e.matmul(
                            bank[:, qb2 * 512:(qb2 + 1) * 512],
                            K_aug[hh][:, p * 128:(p + 1) * 128],
                            qaug[c][:, qb * 512:(qb + 1) * 512],
                            start=first, stop=stop),
                               waits=(w_first if n_emitted == 0 else []),
                               inc=None)
                        n_emitted += 1
                    elif kind == "far128":
                        seen[qb2] += 1
                        stop = seen[qb2] == tot[qb2]
                        col = qb2 * 512 + (qic - 4 * qb) * 128
                        sch.ev("tensor", lambda e, bank=bank, col=col, c=c, qic=qic, p=p, hh=hh, first=first, stop=stop: e.matmul(
                            bank[:, col:col + 128],
                            K_aug[hh][:, p * 128:(p + 1) * 128],
                            qaug[c][:, qic * 128:(qic + 1) * 128],
                            start=first, stop=stop),
                               waits=(w_first if n_emitted == 0 else []),
                               inc=None)
                        n_emitted += 1
                    else:
                        col = qb2 * 512 + (qic - 4 * qb) * 128
                        p_first_q = max(0, 4 * qic - 1)
                        pcol = (p - p_first_q) * 128
                        buf = pos_pool[qic % 2]
                        seen[qb2] += 2
                        stop = seen[qb2] == tot[qb2]
                        sch.ev("tensor", lambda e, bank=bank, col=col, buf=buf, pcol=pcol, first=first: e.matmul(
                            bank[:, col:col + 128], buf[:, pcol:pcol + 128], ident_sb,
                            start=first, stop=False, is_transpose=True),
                               waits=(w_first if n_emitted == 0 else []),
                               inc=None)
                        n_emitted += 1
                        sch.ev("tensor", lambda e, bank=bank, col=col, qic=qic, p=p, hh=hh, stop=stop: e.matmul(
                            bank[:, col:col + 128],
                            K_aug[hh][:, p * 128:(p + 1) * 128],
                            qaug["B"][:, qic * 128:(qic + 1) * 128],
                            start=False, stop=stop),
                               inc=None)
                        n_emitted += 1
                eng, fn, wts, inc0, k0 = sch.events[-1]
                assert eng == "tensor" and inc0 is None
                sch.ctr["sPE"] += 1
                sch.events[-1] = (eng, fn, wts, "sPE", 1)
                p_s = sch.ctr["sPE"]
                for qic in pos_w:
                    pos_free[qic] = [("sPE", p_s)]
                s_stop[p] = p_s

            def emit_exp(p):
                w_exp = [("sPE", s_stop[p])] + pv_mile.get(p - 2, [])
                a_e = sch.ev("scalar", lambda e, bank=sbanks[p % 2], p=p: e.activation(
                    P_pool[p % 2], bank, AF.Exp), waits=w_exp, inc="sACT")
                exp_mile[p] = a_e
                s_free[p % 2] = [("sACT", a_e)]

            def emit_PV(p):
                for qb2 in range(2):
                    sch.ev("tensor", lambda e, qb2=qb2, p=p, hh=hh: e.matmul(
                        O_T[0:65, qb2 * 512:(qb2 + 1) * 512],
                        V_sb[:, p * 260 + hh * 65: p * 260 + hh * 65 + 65],
                        P_pool[p % 2][:, qb2 * 512:(qb2 + 1) * 512],
                        start=(p == 0), stop=(p == 63)),
                           waits=([("sACT", exp_mile[p])] + (ot_in if p == 0 else []))
                                 if qb2 == 0 else [],
                           inc=("sPE" if qb2 == 1 else None))
                pv_mile[p] = [("sPE", sch.ctr["sPE"])]

            for it in range(66):
                if it <= 63:
                    emit_S(it)
                if 0 <= it - 1 <= 63:
                    emit_exp(it - 1)
                if 0 <= it - 2 <= 63:
                    emit_PV(it - 2)
                if bg is not None:
                    next(bg, None)
            pv_last = sch.ctr["sPE"]
            ev_marks = []
            bcast_free = []
            for qb2 in range(2):
                qb = 2 * qbp + qb2
                def _recip(e, qb2=qb2):
                    with nc.allow_low_precision("bf16 recip of softmax denominators is within tolerance"):
                        return e.reciprocal(rcp_sb[64:65, 0:512],
                                            O_T[64:65, qb2 * 512:(qb2 + 1) * 512])
                d_r = sch.ev("vector", _recip, waits=[("sPE", pv_last)] + bcast_free,
                             inc="sDVE")
                p_b = sch.ev("tensor", lambda e, qb2=qb2: e.matmul(
                    O_T[64:128, qb2 * 512:(qb2 + 1) * 512],
                    onesrow65[64:65, 0:64],
                    rcp_sb[64:65, 0:512],
                    start=True, stop=True),
                             waits=[("sDVE", d_r)] + bcast_free, inc="sPE")
                d_b = sch.ev("vector", lambda e, qb2=qb2: e.tensor_copy(
                    rb_sb, O_T[64:128, qb2 * 512:(qb2 + 1) * 512]),
                             waits=[("sPE", p_b)] + ot_dma_free.get("rb", []), inc="sDVE")
                bcast_free = [("sDVE", d_b)]
                d_o = sch.ev("vector", lambda e, qb2=qb2: e.scalar_tensor_tensor(
                    o_tile,
                    O_T[0:64, qb2 * 512:(qb2 + 1) * 512],
                    1.0, rb_sb, ALU.mult, ALU.mult),
                             waits=[("sDVE", d_b)] + ot_dma_free.get("o", []), inc="sDVE")
                ot_dma_free["rb"] = [("sDVE", d_o)]
                ov = ap3(o_dram, (64 * h) * TPC + qb * 512, [[TPC, 64], [1, 512]])
                g_o = sch.ev("sync", lambda e, ov=ov: e.dma_start(
                    out=ov, in_=o_tile),
                             waits=[("sDVE", d_o)], inc="sSC", k=16)
                ot_dma_free["o"] = [("sSC", g_o)]
                ev_marks.append(("sDVE", d_o))
                ev_marks.append(("sPE", p_b))
            ot_in = list(ev_marks)
        if bg is not None:
            for _ in bg:
                pass
        return {"s_free": [s_free[0], s_free[1]], "ot_done": list(ev_marks),
                "pv_last": [("sPE", sch.ctr["sPE"])],
                "exp_done": [("sACT", sch.ctr["sACT"])],
                "o_stored": [("sSC", sch.ctr["sSC"])]}


    def emit_pc_pd(attn_done):
        es_pc = contextlib.ExitStack()
        sbc = lambda name, shape, dt: es_pc.enter_context(nc.sbuf_tensor(name, shape, dt)).ap()
        x2 = [sbc(f"x2_{dc}", [128, TPC], F32) for dc in range(8)]
        ln2 = [sbc(f"ln2_{dc}", [128, 1024], BF16) for dc in range(8)]
        g_sb = sbc("g_sb", [128, 16 * 1024], BF16)
        w3c = [sbc(f"w3c{i}", [128, 4 * 1024], BF16) for i in range(2)]
        w1c4 = [sbc(f"w1c4_{i}", [128, 8 * 512], BF16) for i in range(2)]
        w2c4 = [sbc(f"w2c4_{i}", [128, 8 * 512], BF16) for i in range(2)]
        wb2 = [sbc(f"wb2_{i}", [128, 1024], BF16) for i in range(2)]
        s_tmp = sbc("s_tmp", [128, 1024], BF16)
        sq_scr = sbc("sq_scr", [128, 1024], F32)
        gl = sbc("gl", [128, 8], F32)
        gl2 = sbc("gl2", [128, 8], F32)
        gl3 = sbc("gl3", [128, 8], F32)
        gcol = sbc("gcol", [128, 6], F32)
        wtok = sbc("wtok", [128, 8], BF16)
        w8_rows = sbc("w8_rows", [8, 1024], BF16)
        ident_bf = sbc("ident_bf", [128, 128], BF16)
        onesrow_f = sbc("onesrow_f", [1, 128], F32)
        mean_row = sbc("mean_row", [1, TPC], F32)
        rstd_row = sbc("rstd_row", [1, TPC], F32)
        tmp2_row = sbc("tmp2_row", [1, TPC], F32)
        we_dram = nc.dram_tensor("we_dram", [E, TPC], BF16, kind="Internal")

        d0 = sch.ev("vector", lambda e: e.tensor_copy(ident_bf, ident_sb),
                    waits=attn_done, inc="sDVE")
        sch.ev("vector", lambda e: e.memset(onesrow_f, 1.0), inc="sDVE")
        consts_d = sch.ctr["sDVE"]

        x_mark = {}
        for dc in range(8):
            iv = ap3(xT_own, dc * 128 * TPC, [[TPC, 128], [1, TPC]])
            sch.ev("sync" if dc % 2 == 0 else "scalar",
                   lambda e, iv=iv, dc=dc: e.dma_start(out=x2[dc], in_=iv),
                   waits=attn_done, inc="sLD", k=16)
            x_mark[dc] = ("sLD", sch.ctr["sLD"])
        o_stage = [s_tmp, sq_scr]
        stt_mark = {}
        for k in range(16):
            dc, hf = k // 2, k % 2
            ov = ap3(o_dram, dc * 128 * TPC + hf * 1024, [[TPC, 128], [1, 1024]])
            w = ([stt_mark[k - 2]] if k >= 2 else list(attn_done))
            sch.ev("gpsimd", lambda e, ov=ov, k=k: e.dma_start(
                out=o_stage[k % 2][:, 0:1024], in_=ov),
                   waits=w, inc="sGP", k=16)
            o_mark = ("sGP", sch.ctr["sGP"])
            d = sch.ev("vector", lambda e, dc=dc, hf=hf, k=k: e.scalar_tensor_tensor(
                x2[dc][:, hf * 1024:(hf + 1) * 1024],
                o_stage[k % 2][:, 0:1024],
                bqkv_sb[:, 16 + dc:17 + dc],
                x2[dc][:, hf * 1024:(hf + 1) * 1024], ALU.add, ALU.add),
                       waits=[o_mark, x_mark[dc], ("sDVE", consts_d)], inc="sDVE")
            stt_mark[k] = ("sDVE", d)
        x2_all = [("sDVE", sch.ctr["sDVE"])]

        SUMR = [psA[0:1, 0:512], psA[0:1, 512:1024], psD[0:1, 0:512], psD[0:1, 512:1024]]
        SQR = [psB[0:1, 0:512], psB[0:1, 512:1024], psC[0:1, 0:512], psC[0:1, 512:1024]]
        sqh_prev = {0: [stt_mark[15]], 1: [stt_mark[15]]}
        psum_mark = {}
        psq_mark = {}
        for dc in range(8):
            w_dc = [stt_mark[2 * dc], stt_mark[2 * dc + 1]]
            for qb in range(QBn):
                sch.ev("tensor", lambda e, qb=qb, dc=dc: e.matmul(
                    SUMR[qb], ones_sb, x2[dc][:, qb * 512:(qb + 1) * 512],
                    start=(dc == 0), stop=(dc == 7)),
                       waits=(w_dc + (list(attn_done) if dc == 0 else [])
                              if qb == 0 else []),
                       inc=("sPE" if dc == 7 else None))
                if dc == 7:
                    psum_mark[qb] = sch.ctr["sPE"]
            for qb in range(QBn):
                h = qb % 2
                a_s = sch.ev("scalar", lambda e, qb=qb, dc=dc, h=h: e.activation(
                    sq_scr[:, h * 512:(h + 1) * 512],
                    x2[dc][:, qb * 512:(qb + 1) * 512], AF.Square),
                             waits=sqh_prev[h] + [stt_mark[2 * dc + qb // 2]],
                             inc="sACT")
                m_s = sch.ev("tensor", lambda e, qb=qb, dc=dc, h=h: e.matmul(
                    SQR[qb], ones_sb, sq_scr[:, h * 512:(h + 1) * 512],
                    start=(dc == 0), stop=(dc == 7)),
                       waits=[("sACT", a_s)] + (list(attn_done) if dc == 0 and qb == 0 else []),
                       inc="sPE")
                sqh_prev[h] = [("sPE", sch.ctr["sPE"])]
                if dc == 7:
                    psq_mark[qb] = sch.ctr["sPE"]
        for qb in range(QBn):
            d_m = sch.ev("vector", lambda e, qb=qb: e.tensor_scalar(
                mean_row[0:1, qb * 512:(qb + 1) * 512], SUMR[qb],
                1.0 / D, None, ALU.mult),
                         waits=[("sPE", psum_mark[qb])], inc="sDVE")
            d_v = sch.ev("vector", lambda e, qb=qb: e.tensor_scalar(
                tmp2_row[0:1, qb * 512:(qb + 1) * 512], SQR[qb],
                1.0 / D, None, ALU.mult),
                         waits=[("sPE", psq_mark[qb]), ("sDVE", d_m)], inc="sDVE")
        a_prev = [("sDVE", sch.ctr["sDVE"])]
        d_m2r = sch.ev("vector", lambda e: e.tensor_mul(rstd_row, mean_row, mean_row),
                       waits=a_prev, inc="sDVE")
        d_var = sch.ev("vector", lambda e: e.tensor_sub(tmp2_row, tmp2_row, rstd_row),
                       waits=[("sDVE", d_m2r)], inc="sDVE")
        a_sr = sch.ev("scalar", lambda e: e.activation(rstd_row, tmp2_row, AF.Sqrt,
                                                       bias=constcol(EPS)[0:1, :]),
                      waits=[("sDVE", d_var)], inc="sACT")
        d_rs = sch.ev("vector", lambda e: e.reciprocal(tmp2_row, rstd_row),
                      waits=[("sACT", a_sr)], inc="sDVE")

        psA_free = [("sDVE", sch.ctr["sDVE"]), ("sPE", sch.ctr["sPE"])]
        psB_free = list(psA_free)
        regs = [(psC, 0), (psC, 512), (psD, 0), (psD, 512)]
        regs_free = [list(psA_free) for _ in range(4)]
        gl_free = []
        tr_free = []
        sq_free = [("sDVE", sch.ctr["sDVE"])]
        stmp_free = []
        x2_add_done = []
        we_done = {}
        ln2_done = []
        bc_free = psA_free
        ln2_used = []

        NQH = (E + 1) * 4
        NKH = (E + 1) * 4
        w12_ld, w12_freed = {}, {}
        w3_ld, w3_freed = {}, {}
        wb_ld, wb_freed = {}, {}
        gch_free = {0: list(attn_done), 1: list(attn_done)}

        def emit_w12_load(q):
            halfq, r = divmod(q, NQH)
            e_, hcg = divmod(r, 4)
            if e_ == E:
                i1 = ap3(sw1, hcg * 512, [[HF, 128], [128 * HF, 8], [1, 512]])
                i2 = ap3(sw2, hcg * 512, [[HF, 128], [128 * HF, 8], [1, 512]])
            else:
                off = e_ * D * HF + hcg * 512
                i1 = ap3(ew1, off, [[HF, 128], [128 * HF, 8], [1, 512]])
                i2 = ap3(ew2, off, [[HF, 128], [128 * HF, 8], [1, 512]])
            par = q % 2
            wsem = f"sW12{par}"
            od = [[4096, 128], [512, 8], [1, 512]]
            w = w12_freed.get(q - 2, list(attn_done))
            sch.ev("gpsimd", lambda e, i1=i1, par=par, od=od: e.dma_start(
                out=ap3(w1c4[par], 0, od), in_=i1), waits=w, inc=wsem, k=16)
            sch.ev("gpsimd", lambda e, i2=i2, par=par, od=od: e.dma_start(
                out=ap3(w2c4[par], 0, od), in_=i2), inc=wsem, k=16)
            w12_ld[q] = (wsem, sch.ctr[wsem])

        def emit_w3_load(k):
            halfk, r = divmod(k, NKH)
            e_, ch = divmod(r, 4)
            if e_ == E:
                iv = ap3(sw3, ch * 4 * 128 * D, [[D, 128], [128 * D, 4], [1, D]])
            else:
                iv = ap3(ew3, e_ * HF * D + ch * 4 * 128 * D,
                         [[D, 128], [128 * D, 4], [1, D]])
            par = k % 2
            ksem = f"sW3{par}"
            w = w3_freed.get(k - 2, list(attn_done))
            sch.ev("gpsimd", lambda e, iv=iv, par=par: e.dma_start(
                out=w3c[par], in_=iv), waits=w, inc=ksem, k=16)
            w3_ld[k] = (ksem, sch.ctr[ksem])

        def emit_wb_load(j, half):
            e_ = j - half * E
            par = j % 2
            bsem = f"sWB{par}"
            wv = ap3(we_dram, e_ * TPC + half * 1024, [[0, 128], [1, 1024]])
            w = wb_freed.get(j - 2, []) + we_done[half]
            sch.ev("gpsimd", lambda e, wv=wv, par=par: e.dma_start(
                out=wb2[par], in_=wv), waits=w, inc=bsem, k=16)
            wb_ld[j] = (bsem, sch.ctr[bsem])

        emit_w12_load(0)
        emit_w12_load(1)
        emit_w3_load(0)
        emit_w3_load(1)

        for half in range(2):
            toff = half * 1024
            for qbl in range(2):
                qb = 2 * half + qbl
                p_bm = sch.ev("tensor", lambda e, qb=qb: e.matmul(
                    psA[:, 0:512], onesrow_f, mean_row[0:1, qb * 512:(qb + 1) * 512],
                    start=True, stop=True),
                              waits=[("sDVE", d_rs), ("sDVE", consts_d)] + bc_free + psA_free,
                              inc="sPE")
                p_br = sch.ev("tensor", lambda e, qb=qb: e.matmul(
                    psB[:, 0:512], onesrow_f, tmp2_row[0:1, qb * 512:(qb + 1) * 512],
                    start=True, stop=True), waits=psB_free, inc="sPE")
                p_bc = sch.ctr["sPE"]
                d_prev = sq_free
                for dc in range(8):
                    d_s = sch.ev("vector", lambda e, dc=dc, qb=qb: e.tensor_sub(
                        sq_scr[:, 0:512], x2[dc][:, qb * 512:(qb + 1) * 512], psA[:, 0:512]),
                                 waits=[("sPE", p_bc)] + d_prev + ln2_used, inc="sDVE")
                    d_l = sch.ev("vector", lambda e, dc=dc, qbl=qbl: e.tensor_mul(
                        ln2[dc][:, qbl * 512:(qbl + 1) * 512], sq_scr[:, 0:512], psB[:, 0:512]),
                                 waits=[("sDVE", d_s)], inc="sDVE")
                    d_prev = [("sDVE", d_l)]
                sq_free = d_prev
                psA_free = d_prev
                psB_free = d_prev
                bc_free = d_prev
            ln2_done = [("sDVE", sch.ctr["sDVE"])]

            for tcl in range(8):
                for dc in range(8):
                    sch.ev("tensor", lambda e, tcl=tcl, dc=dc: e.matmul(
                        psC[:, 0:8], ln2[dc][:, tcl * 128:(tcl + 1) * 128],
                        gate_sb[:, dc * 8:(dc + 1) * 8],
                        start=(dc == 0), stop=(dc == 7)),
                           waits=(ln2_done + gl_free + regs_free[0] + [("sCW", 16)]
                                  if dc == 0 else []),
                           inc=("sPE" if dc == 7 else None))
                p_g = sch.ctr["sPE"]
                c1 = sch.ev("vector", lambda e: e.tensor_copy(gl, psC[:, 0:8]),
                            waits=[("sPE", p_g)], inc="sDVE")
                regs_free[0] = [("sDVE", c1)]
                gl_free = [("sDVE", c1)]
                c2 = sch.ev("vector", lambda e: e.reduce_max(gcol[:, 0:1], gl, axis=AX.X),
                            waits=[("sDVE", c1)], inc="sDVE")
                c3 = sch.ev("vector", lambda e: e.tensor_scalar(
                    gl2, gl, gcol[:, 0:1], None, ALU.is_ge), waits=[("sDVE", c2)], inc="sDVE")
                c4 = sch.ev("vector", lambda e: e.scalar_tensor_tensor(
                    gl2, gl2, -1e30, gl, ALU.mult, ALU.add), waits=[("sDVE", c3)], inc="sDVE")
                c5 = sch.ev("vector", lambda e: e.reduce_max(gcol[:, 1:2], gl2, axis=AX.X),
                            waits=[("sDVE", c4)], inc="sDVE")
                c6 = sch.ev("vector", lambda e: e.tensor_scalar(
                    gl2, gl, gcol[:, 1:2], None, ALU.is_ge), waits=[("sDVE", c5)], inc="sDVE")
                c7 = sch.ev("vector", lambda e: e.tensor_scalar(
                    gcol[:, 2:3], gcol[:, 0:1], -1.0, None, ALU.mult),
                            waits=[("sDVE", c2)], inc="sDVE")
                c8 = sch.ev("scalar", lambda e: e.activation(
                    gl3, gl, AF.Exp, bias=gcol[:, 2:3]),
                            waits=[("sDVE", c7), ("sDVE", c6)], inc="sACT")
                c9 = sch.ev("vector", lambda e: e.tensor_mul(gl3, gl3, gl2),
                            waits=[("sACT", c8)], inc="sDVE")
                c10 = sch.ev("vector", lambda e: e.reduce_sum(gcol[:, 3:4], gl3, axis=AX.X),
                             waits=[("sDVE", c9)], inc="sDVE")
                c11 = sch.ev("vector", lambda e: e.reciprocal(gcol[:, 4:5], gcol[:, 3:4]),
                             waits=[("sDVE", c10)], inc="sDVE")
                c12 = sch.ev("vector", lambda e: e.tensor_scalar(
                    wtok, gl3, gcol[:, 4:5], None, ALU.mult), waits=[("sDVE", c11)], inc="sDVE")
                p_t = sch.ev("tensor", lambda e: e.transpose(
                    psD.bitcast(BF16)[0:8, 0:128], wtok, ident_bf),
                             waits=[("sDVE", c12), ("sDVE", consts_d)] + tr_free + regs_free[2],
                             inc="sPE")
                c13 = sch.ev("vector", lambda e, tcl=tcl: e.tensor_copy(
                    w8_rows[:, tcl * 128:(tcl + 1) * 128], psD.bitcast(BF16)[0:8, 0:128]),
                             waits=[("sPE", p_t)], inc="sDVE")
                tr_free = [("sDVE", c13)]
                regs_free[2] = [("sDVE", c13)]
            w8_done = sch.ctr["sDVE"]
            ww = ap3(we_dram, toff, [[TPC, 8], [1, 1024]])
            m = sch.ev("gpsimd", lambda e, ww=ww: e.dma_start(out=ww, in_=w8_rows),
                       waits=[("sDVE", w8_done)], inc="sGP", k=16)
            we_done[half] = [("sGP", m)]

            for e_ in range(E + 1):
                shared = e_ == E
                j = half * E + e_
                if not shared:
                    emit_wb_load(j, half)
                g_ready = {}
                for hcg in range(4):
                    q = (half * (E + 1) + e_) * 4 + hcg
                    ld = w12_ld[q]
                    par = q % 2
                    for hh2 in range(4):
                        hc = hcg * 4 + hh2
                        co = hh2 * 128
                        for nb in range(2):
                            for dc in range(8):
                                sch.ev("tensor", lambda e, dc=dc, nb=nb, par=par, co=co: e.matmul(
                                    psA[:, nb * 512:(nb + 1) * 512],
                                    w1c4[par][:, dc * 512 + co:dc * 512 + co + 128],
                                    ln2[dc][:, nb * 512:(nb + 1) * 512],
                                    start=(dc == 0), stop=(dc == 7)),
                                       waits=(psA_free + [ld] + ln2_done
                                              if (nb == 0 and dc == 0) else []),
                                       inc=("sPE" if dc == 7 else None))
                        p_h1 = sch.ctr["sPE"]
                        for nb in range(2):
                            for dc in range(8):
                                sch.ev("tensor", lambda e, dc=dc, nb=nb, par=par, co=co: e.matmul(
                                    psB[:, nb * 512:(nb + 1) * 512],
                                    w2c4[par][:, dc * 512 + co:dc * 512 + co + 128],
                                    ln2[dc][:, nb * 512:(nb + 1) * 512],
                                    start=(dc == 0), stop=(dc == 7)),
                                       waits=(psB_free + [ld]
                                              if (nb == 0 and dc == 0) else []),
                                       inc=("sPE" if dc == 7 else None))
                        p_h2 = sch.ctr["sPE"]
                        a_si = sch.ev("scalar", lambda e: e.activation(s_tmp, psA, AF.Silu),
                                      waits=[("sPE", p_h1)] + stmp_free, inc="sACT")
                        d_g = sch.ev("vector", lambda e, hc=hc: e.tensor_mul(
                            g_sb[:, hc * 1024:(hc + 1) * 1024], s_tmp, psB),
                                     waits=[("sPE", p_h2), ("sACT", a_si)] + gch_free[hc // 8],
                                     inc="sDVE")
                        psA_free = [("sACT", a_si)]
                        psB_free = [("sDVE", d_g)]
                        stmp_free = [("sDVE", d_g)]
                        if not shared:
                            d_g = sch.ev("vector", lambda e, hc=hc, wpar=j % 2: e.tensor_mul(
                                g_sb[:, hc * 1024:(hc + 1) * 1024],
                                g_sb[:, hc * 1024:(hc + 1) * 1024], wb2[wpar]),
                                         waits=[wb_ld[j], ("sDVE", d_g)], inc="sDVE")
                        g_ready[hc] = d_g
                    w12_freed[q] = [("sPE", p_h2)]
                    if q + 2 < 2 * NQH:
                        emit_w12_load(q + 2)
                if not shared:
                    wb_freed[j] = [("sDVE", g_ready[15])]
                for ch in range(4):
                    k = (half * (E + 1) + e_) * 4 + ch
                    kpar = k % 2
                    gw = [("sDVE", g_ready[ch * 4 + 3]), w3_ld[k]]
                    for nb in range(2):
                        for dout in range(8):
                            r = (nb * 8 + dout) % 4
                            bank, cb = regs[r]
                            for i in range(4):
                                hc = ch * 4 + i
                                sch.ev("tensor", lambda e, i=i, hc=hc, dout=dout, nb=nb, kpar=kpar, bank=bank, cb=cb: e.matmul(
                                    bank[:, cb:cb + 512],
                                    w3c[kpar][:, i * 1024 + dout * 128:i * 1024 + dout * 128 + 128],
                                    g_sb[:, hc * 1024 + nb * 512:hc * 1024 + nb * 512 + 512],
                                    start=(i == 0), stop=(i == 3)),
                                       waits=(gw + regs_free[r] if i == 0 else []),
                                       inc=("sPE" if i == 3 else None))
                            p_w3 = sch.ctr["sPE"]
                            d_x = sch.ev("vector", lambda e, dout=dout, nb=nb, toff=toff, bank=bank, cb=cb: e.tensor_add(
                                x2[dout][:, toff + nb * 512:toff + (nb + 1) * 512],
                                x2[dout][:, toff + nb * 512:toff + (nb + 1) * 512],
                                bank[:, cb:cb + 512]),
                                         waits=[("sPE", p_w3)], inc="sDVE")
                            regs_free[r] = [("sDVE", d_x)]
                    w3_freed[k] = [("sPE", sch.ctr["sPE"])]
                    if ch % 2 == 1:
                        gch_free[ch // 2] = list(w3_freed[k])
                    if k + 2 < 2 * NKH:
                        emit_w3_load(k + 2)
            ln2_used = [("sPE", sch.ctr["sPE"])]
            x2_add_done = [("sDVE", sch.ctr["sDVE"])]

        for dc in range(8):
            ovv = ap3(OUT, dc * 128 * TPC, [[TPC, 128], [1, TPC]])
            sch.ev("scalar", lambda e, ovv=ovv, dc=dc: e.dma_start(out=ovv, in_=x2[dc]),
                   waits=x2_add_done, inc="sST", k=16)
        return es_pc

    ld0 = emit_const_loads()
    DBGL = nc.dram_tensor("DBG_LN1", [S, D], BF16, kind="ExternalOutput").ap() if DEBUG else None
    ln1_done = emit_ln_phase(x_rows, ln1_dram, S // 512, ld0, dbg_dram=DBGL)
    ln1own_done = emit_ln_phase(x_own_rows, ln1own_dram, TPC // 512, ld0)

    es_p1.close()

    es_attn = contextlib.ExitStack()
    sba = lambda name, shape, dt: es_attn.enter_context(nc.sbuf_tensor(name, shape, dt)).ap()
    wk_sb = sba("wk_sb", [128, 8 * 256], BF16)
    wv_sb = sba("wv_sb", [128, 8 * 256], BF16)
    wq_sb = wk_sb
    lnT = [sba(f"lnT{i}", [128, 8 * 512], BF16) for i in range(2)]
    K_aug = [sba(f"K_aug{hh}", [66, S], BF16) for hh in range(HPG)]
    V_sb = sba("V_sb", [128, 64 * 260], BF16)
    Q_sb = sba("Q_sb", [128, 2 * TPC], BF16)
    qh1 = sba("qh1", [65, TPC], BF16)
    qaugs = [{t: sba(f"qaug{t}{i}", [66, TPC], BF16) for t in "LBR"}
             for i in range(2)]
    rp_pad_sb = sba("rp_pad_sb", [128, W_PAD], BF16)
    rp01 = sba("rp01", [128, 2], BF16)
    pos_pool = [sba(f"pos{i}", [128, 768], F32) for i in range(2)]
    P_pool = [sba(f"P{i}", [128, 1024], BF16) for i in range(2)]
    rcp_sb = sba("rcp_sb", [65, 512], BF16)
    rb_sb = sba("rb_sb", [64, 512], BF16)
    o_tile = sba("o_tile", [64, 512], BF16)
    maxk_col2 = sba("maxk_col2", [128, 1], F32)
    ksub = sba("ksub", [64, 320], BF16)
    qn_row = sba("qn_row", [1, TPC], BF16)
    qn2_row = sba("qn2_row", [1, TPC], BF16)
    c_row = sba("c_row", [1, TPC], F32)
    tmp_row = sba("tmp_row", [1, TPC], F32)
    rp0_row = sba("rp0_row", [1, TPC], BF16)
    rp256_row = sba("rp256_row", [1, TPC], BF16)

    fence0 = [(s, v) for s, v in sch.ctr.items()]
    for hh in range(HPG):
        sch.ev("vector", lambda e, hh=hh: e.memset(K_aug[hh][64:66, :], 1.0),
               waits=fence0, inc="sDVE")
    fence0 = fence0 + [("sDVE", sch.ctr["sDVE"])]
    fence = fence0
    sfree_in = [fence0, fence0]
    ot_free = fence0
    psD_free = fence0
    prep_guards = []
    dbg_marks = []
    pos_prev = fence0
    for g in range(RUN_GROUPS):
        kv = emit_group_proj(g, fence, psD_free)
        out0 = {}
        for _ in emit_head_prep(g, 0, kv, psfree=kv["psd"], guards=fence, out=out0):
            pass
        preps = {0: out0}
        for hh in range(RUN_HEADS):
            bg = None
            if hh + 1 < RUN_HEADS:
                outn = {}
                bg = emit_head_prep(g, hh + 1, kv, psfree=preps[hh]["psD_user"],
                                    guards=preps[hh]["tail"] + list(prep_guards or fence),
                                    out=outn)
                preps[hh + 1] = outn
                if not INTERLEAVE_PREP:
                    for _ in bg:
                        pass
                    bg = None
            res = emit_head_attn(g, hh, preps[hh], sfree_in, ot_free, pos_prev, bg=bg)
            sfree_in = res["s_free"]
            ot_free = res["ot_done"]
            pos_prev = res["pv_last"]
            prep_guards = res["pv_last"] + res["exp_done"] + res["o_stored"] + [
                ("sGP", sch.ctr["sGP"]), ("sDVE", sch.ctr["sDVE"]),
                ("sPOS0", sch.ctr["sPOS0"]), ("sPOS1", sch.ctr["sPOS1"])] + [
                (s, sch.ctr[s]) for s in QSEMS + RPSEMS]
        psD_free = preps[RUN_HEADS - 1]["psD_user"]
        fence = prep_guards + [("sACT", sch.ctr["sACT"])]

    attn_done = prep_guards + [(s, v) for s, v in sch.ctr.items()]
    if DEBUG:
        DBG_OACC = nc.dram_tensor("DBG_OACC", [D, TPC], BF16, kind="ExternalOutput").ap()
        mo = sch.ev("scalar", lambda e: e.dma_start(out=DBG_OACC, in_=o_dram.ap()),
                    waits=attn_done, inc="sST", k=16)
        attn_done = attn_done + [("sST", mo)]

    es_attn.close()
    es_pc = emit_pc_pd(attn_done)

    st_total = sch.ctr["sST"]
    sch.ev("sync", lambda e: e.wait_ge(sch.sems["sST"], st_total))

    block = es.enter_context(nc.Block())
    sch.build(block)
    es.close()
    return nc


_prog_cache = {}


def _get_prog():
    if "nc" not in _prog_cache:
        _prog_cache["nc"] = build_program()
    return _prog_cache["nc"]



def stage_inputs(core, **inputs):
    b, cp = core // 4, core % 4
    x = np.asarray(inputs["L0_attn_x"])
    Wqkv = np.asarray(inputs["L0_attn_Wqkv"])
    bqkv = np.asarray(inputs["L0_attn_bqkv"])
    Wr = np.asarray(inputs["L0_attn_Wr"])
    br = np.asarray(inputs["L0_attn_br"])

    d = {}
    d["x_rows"] = np.ascontiguousarray(x[b])
    d["x_own_rows"] = np.ascontiguousarray(x[b, cp::4])
    d["xT_own"] = np.ascontiguousarray(x[b, cp::4].T)

    Wh = Wqkv.reshape(NH, 3, HD, D)
    d["WQ_T"] = np.ascontiguousarray(Wh[:, 0].reshape(D, D).T)
    d["WK_T"] = np.ascontiguousarray(Wh[:, 1].reshape(D, D).T)
    d["WV_T"] = np.ascontiguousarray(Wh[:, 2].reshape(D, D).T)
    bh = bqkv.reshape(NH, 3, HD)
    bq = bh[:, 0].reshape(D)
    bk = bh[:, 1].reshape(D)
    bv = bh[:, 2].reshape(D)
    d["bQKV"] = np.ascontiguousarray(
        np.concatenate([bq.reshape(8, 128).T, bk.reshape(8, 128).T, bv.reshape(8, 128).T], axis=1))
    WrB = np.concatenate([Wr.T, br[None, :]], axis=0)
    idx = np.clip(np.arange(261) - cp, 0, 256)
    d["WrBc"] = np.ascontiguousarray(WrB[:, idx])
    d["Wr0"] = np.ascontiguousarray(WrB[:, 0:1])
    d["Wr256"] = np.ascontiguousarray(WrB[:, 256:257])
    d["gate_w"] = np.ascontiguousarray(inputs["L0_ffn_gate_w"])
    for k in ("ew1", "ew2", "ew3", "sw1", "sw2", "sw3"):
        d[k] = np.ascontiguousarray(inputs["L0_ffn_" + k])
    return d


def kernel(**inputs):
    nc = _get_prog()
    in_maps = [stage_inputs(c, **inputs) for c in range(8)]
    res = run_bass_kernel_spmd(nc, in_maps, core_ids=list(range(8)))
    out = np.empty((B, S, D), dtype=np.float32)
    for c in range(8):
        b, cp = c // 4, c % 4
        out[b, cp::4, :] = res.results[c]["OUT"].T
    return out


# version 19 (final)
# speedup vs baseline: 1.0302x; 1.0132x over previous
import contextlib
import math
import os
import numpy as np
import ml_dtypes
import concourse.bass as bass
import concourse.mybir as mybir
from concourse.ap import AP
from concourse.bass_utils import run_bass_kernel_spmd

F32 = mybir.dt.float32
BF16 = mybir.dt.bfloat16
AF = mybir.ActivationFunctionType
ALU = mybir.AluOpType
AX = mybir.AxisListType

B, S, D = 2, 8192, 1024
NH, HD = 16, 64
RW = 128
R = 2 * RW + 1
E, HF = 8, 2048
TPC = S // 4
HPG, NG = 4, 4
W_PAD, KC = 1536, 768
EPS = 1e-5

DEBUG = False
INTERLEAVE_PREP = True
RUN_GROUPS = NG
RUN_HEADS = HPG
RUN_ATTN = True
PREP_STAGES = 9



class Sched:

    ENGS = ("sync", "gpsimd", "tensor", "scalar", "vector")

    def __init__(self, nc, es):
        self.nc = nc
        self.events = []
        self.sems = {}
        self.ctr = {}
        self.es = es

    def sem(self, name):
        self.sems[name] = self.es.enter_context(self.nc.semaphore(name))
        self.ctr[name] = 0

    def ev(self, eng, fn, waits=(), inc=None, k=1):
        waits = [(s, v) for (s, v) in waits if v > 0]
        if inc is not None:
            self.ctr[inc] += k
        self.events.append((eng, fn, waits, inc, k))
        return self.ctr[inc] if inc is not None else None

    def build(self, block):
        n_ev = len(self.events)
        for eng_name in self.ENGS:
            def body(eng, eng_name=eng_name):
                waited = {}
                for (e, fn, waits, inc, k) in self.events:
                    if e != eng_name:
                        continue
                    for (s, v) in waits:
                        if waited.get(s, 0) < v:
                            eng.wait_ge(self.sems[s], v)
                            waited[s] = v
                    inst = fn(eng)
                    if inc is not None:
                        inst.then_inc(self.sems[inc], k)
                assert len(self.events) == n_ev, "events created during replay!"
            getattr(block, eng_name)(body)


def classify(p, qic):
    lo = 128 * p - 4 * (128 * qic + 127) - 3
    hi = 128 * p + 127 - 4 * 128 * qic
    if lo >= 129:
        return 'R'
    if hi <= -129:
        return 'L'
    return 'S'


def classify_qb(p, qb):
    cs = {classify(p, qic) for qic in range(4 * qb, 4 * qb + 4)}
    if cs == {'R'}:
        return 'R'
    if cs == {'L'}:
        return 'L'
    return 'M'



def ap3(t, off, dims):
    return AP(t.tensor if isinstance(t, AP) else t, off, dims)


def build_program():
    nc = bass.Bass("TRN2", target_bir_lowering=False, debug=False,
                   detect_race_conditions=bool(os.environ.get("KRACE")))

    def inp(name, shape):
        return nc.dram_tensor(name, shape, F32, kind="ExternalInput").ap()

    x_rows = inp("x_rows", [S, D])
    x_own_rows = inp("x_own_rows", [TPC, D])
    xT_own = inp("xT_own", [D, TPC])
    WQ_T = inp("WQ_T", [D, D])
    WK_T = inp("WK_T", [D, D])
    WV_T = inp("WV_T", [D, D])
    bQKV = inp("bQKV", [128, 24])
    WrBc = inp("WrBc", [65, 261])
    Wr0 = inp("Wr0", [65, 1])
    Wr256 = inp("Wr256", [65, 1])
    gate_w_in = inp("gate_w", [D, E])
    ew1 = inp("ew1", [E, D, HF])
    ew2 = inp("ew2", [E, D, HF])
    ew3 = inp("ew3", [E, HF, D])
    sw1 = inp("sw1", [D, HF])
    sw2 = inp("sw2", [D, HF])
    sw3 = inp("sw3", [HF, D])

    OUT = nc.dram_tensor("OUT", [D, TPC], F32, kind="ExternalOutput").ap()

    ln1_dram = nc.dram_tensor("ln1_dram", [S, D], BF16, kind="Internal")
    ln1own_dram = nc.dram_tensor("ln1own_dram", [TPC, D], BF16, kind="Internal")
    rp_dram = [nc.dram_tensor(f"rp_dram{h}", [TPC, W_PAD], BF16, kind="Internal")
               for h in range(HPG)]

    ident_f32 = nc.inline_tensor(np.eye(128, dtype=np.float32), "ident_f32")
    ident_b16 = nc.inline_tensor(np.eye(128, dtype=ml_dtypes.bfloat16), "ident_b16")
    ones_col_f32 = nc.inline_tensor(np.ones((128, 1), dtype=np.float32), "ones_col")

    es = contextlib.ExitStack()
    sch = Sched(nc, es)
    for s in ("sLD", "sGP", "sSC", "sPE", "sACT", "sDVE", "sST",
              "sLN0", "sLN1", "sLNX0", "sLNX1", "sLNS0", "sLNS1",
              "sPOS0", "sPOS1", "sQAUG", "sQG1", "sQG2", "sQG3", "sCW",
              "sW120", "sW121", "sW30", "sW31", "sWB0", "sWB1"):
        sch.sem(s)
    QSEMS = ("sQAUG", "sQG1", "sQG2", "sQG3")
    RPSEMS = ("sLNX0", "sLNX1", "sLNS0", "sLNS1")

    sb = nc.alloc_sbuf_tensor
    ident_sb = sb("ident_sb", [128, 128], F32).ap()
    identb_sb = sb("identb_sb", [128, 128], BF16).ap()
    ones_sb = sb("ones_sb", [128, 1], F32).ap()
    bqkv_sb = sb("bqkv_sb", [128, 24], F32).ap()
    bq8_sb = sb("bq8_sb", [128, 8], F32).ap()
    wrbc8 = sb("wrbc8", [65, 261], BF16).ap()
    wr0_8 = sb("wr0_8", [65, 1], BF16).ap()
    wr256_8 = sb("wr256_8", [65, 1], BF16).ap()

    nrmWr = sb("nrmWr", [1, 2], F32).ap()
    gate_sb = sb("gate_sb", [128, 8 * 8], BF16).ap()
    constcols = sb("constcols", [128, 4], F32).ap()
    ones_bf = sb("ones_bf", [128, 1], BF16).ap()
    onesrow65 = sb("onesrow65", [65, 128], BF16).ap()
    wr01_8 = sb("wr01_8", [65, 2], BF16).ap()
    wr01_f = sb("wr01_f", [65, 2], F32).ap()
    scal_sb = sb("scal_sb", [1, 16], F32).ap()
    maxbr64 = sb("maxbr64", [65, 1], F32).ap()

    maxk_row = sb("maxk_row", [1, NH], F32).ap()
    o_dram = nc.dram_tensor("o_dram", [D, TPC], BF16, kind="Internal")

    psA = nc.alloc_psum_tensor("psA", [128, 1024], F32).ap()
    psB = nc.alloc_psum_tensor("psB", [128, 1024], F32).ap()
    psC = nc.alloc_psum_tensor("psC", [128, 1024], F32).ap()
    psD = nc.alloc_psum_tensor("psD", [128, 1024], F32).ap()

    CONST_COLS = {}

    def constcol(val):
        if val not in CONST_COLS:
            i = len(CONST_COLS)
            CONST_COLS[val] = constcols[:, i:i + 1]
            sch.ev("vector", lambda e, i=i, val=val: e.memset(constcols[:, i:i + 1], val), inc="sDVE")
        return CONST_COLS[val]

    def emit_const_loads():
        sch.ev("sync", lambda e: e.dma_start(out=ident_sb, in_=ident_f32.ap()), inc="sLD", k=16)
        sch.ev("sync", lambda e: e.dma_start(out=identb_sb, in_=ident_b16.ap()), inc="sLD", k=16)
        sch.ev("sync", lambda e: e.dma_start(out=ones_sb, in_=ones_col_f32.ap()), inc="sLD", k=16)
        sch.ev("sync", lambda e: e.dma_start(out=bqkv_sb, in_=bQKV), inc="sLD", k=16)
        sch.ev("sync", lambda e: e.dma_start(out=wr_stat[:, 0:261], in_=WrBc), inc="sLD", k=16)
        sch.ev("sync", lambda e: e.dma_start(out=wr01_f[:, 0:1], in_=Wr0), inc="sLD", k=16)
        sch.ev("sync", lambda e: e.dma_start(out=wr01_f[:, 1:2], in_=Wr256), inc="sLD", k=16)
        ld0 = sch.ctr["sLD"]
        gv = ap3(gate_w_in, 0, [[8, 128], [128 * 8, 8], [1, 8]])
        go = ap3(gate_sb, 0, [[64, 128], [8, 8], [1, 8]])
        sch.ev("gpsimd", lambda e: e.dma_start(out=go, in_=gv), inc="sCW", k=16)
        sch.ev("vector", lambda e: e.tensor_scalar(wrbc8[0:64, :], wr_stat[0:64, :], 8.0, None, ALU.mult),
               waits=[("sLD", ld0)], inc="sDVE")
        sch.ev("vector", lambda e: e.memset(scal_sb, 0.0), inc="sDVE")
        sch.ev("vector", lambda e: e.memset(maxk_row, 0.0), inc="sDVE")
        d_wr = sch.ev("vector", lambda e: e.tensor_copy(wrbc8[64:65, :], wr_stat[64:65, :]), inc="sDVE")
        constcol(EPS)
        sch.ev("vector", lambda e: e.memset(ones_bf, 1.0), inc="sDVE")
        sch.ev("vector", lambda e: e.memset(onesrow65[64:65, :], 1.0), inc="sDVE")
        ld1 = sch.ctr["sLD"]
        sch.ev("vector", lambda e: e.tensor_scalar(wr01_8[0:64, :], wr01_f[0:64, :], 8.0, None, ALU.mult),
               waits=[("sLD", ld1)], inc="sDVE")
        sch.ev("vector", lambda e: e.tensor_copy(wr01_8[64:65, :], wr01_f[64:65, :]), inc="sDVE")
        sch.ev("vector", lambda e: e.tensor_scalar(bq8_sb, bqkv_sb[:, 0:8], 0.125, None, ALU.mult),
               waits=[("sLD", ld0)], inc="sDVE")
        d_br = sch.ev("vector", lambda e: e.reduce_max(maxbr64[64:65, 0:1], wrbc8[64:65, 0:261], axis=AX.X),
                      waits=[("sDVE", d_wr)], inc="sDVE")
        sch.ev("gpsimd", lambda e: e.dma_start(out=ap3(scal_sb, 1, [[16, 1], [1, 1]]),
                                               in_=ap3(maxbr64, 64, [[1, 1], [1, 1]])),
               waits=[("sDVE", d_br)], inc="sGP", k=16)
        a_sq = sch.ev("scalar", lambda e: e.activation(wr_sq, wrbc8[0:64, :], AF.Square),
                      waits=[("sDVE", d_wr)], inc="sACT")
        p_nw = sch.ev("tensor", lambda e: e.matmul(psA[0:1, 0:261], ones_bf[0:64, :], wr_sq,
                                                   start=True, stop=True),
                      waits=[("sACT", a_sq), ("sDVE", sch.ctr["sDVE"])], inc="sPE")
        a_nw = sch.ev("scalar", lambda e: e.activation(row261, psA[0:1, 0:261], AF.Sqrt),
                      waits=[("sPE", p_nw)], inc="sACT")
        sch.ev("vector", lambda e: e.reduce_max(scal_sb[0:1, 0:1], row261, axis=AX.X),
               waits=[("sACT", a_nw)], inc="sDVE")
        return ld0

    es_p1 = contextlib.ExitStack()
    sbp = lambda name, shape, dt: es_p1.enter_context(nc.sbuf_tensor(name, shape, dt)).ap()
    wr_stat = sbp("wr_stat", [65, 261], F32)
    row261 = sbp("row261", [1, 261], F32)
    wr_sq = sbp("wr_sq", [64, 261], BF16)
    p1_bufx = [sbp(f"p1x{i}", [128, 4096], F32) for i in range(2)]
    p1_bufl = [sbp(f"p1l{i}", [128, 4096], BF16) for i in range(2)]
    p1_sq = [sbp(f"p1sq{i}", [128, 1024], F32) for i in range(2)]
    p1_stat = [sbp(f"p1st{i}", [128, 24], F32) for i in range(2)]

    def emit_ln_phase(src_ap, dst_dram, nblk, ld0, dbg_dram=None):
        prev_dve = sch.ctr["sDVE"]
        prev_act = sch.ctr["sACT"]
        prev_st = {0: sch.ctr["sLNS0"], 1: sch.ctr["sLNS1"]}
        entry = [("sDVE", prev_dve), ("sACT", prev_act)]
        ap_done = {}
        st_done = {}
        dbg_guard = {}
        for b in range(nblk):
            bx, bl = p1_bufx[b % 2], p1_bufl[b % 2]
            sq, st = p1_sq[b % 2], p1_stat[b % 2]
            ldsem = f"sLNX{b % 2}"
            stsem = f"sLNS{b % 2}"
            waits = [("sLD", ld0)]
            if b >= 2:
                waits.append(("sACT", ap_done[b - 2]))
                if (b - 2) in dbg_guard:
                    waits.append(("sST", dbg_guard[b - 2]))
            else:
                waits += entry
            src = ap3(src_ap, b * 512 * D, [[D, 128], [128 * D, 4], [1, D]])
            dst_sb = ap3(bx, 0, [[4096, 128], [1024, 4], [1, 1024]])
            sch.ev("sync", lambda e, src=src, dst_sb=dst_sb: e.dma_start(out=dst_sb, in_=src),
                   waits=waits, inc=ldsem, k=16)
            ldb = sch.ctr[ldsem]
            w_prev = [("sACT", ap_done[b - 2])] if b >= 2 else []
            for a in range(4):
                xt = bx[:, a * 1024:(a + 1) * 1024]
                sch.ev("vector", lambda e, xt=xt, st=st, a=a: e.reduce_sum(st[:, a:a + 1], xt, axis=AX.X),
                       waits=[(ldsem, ldb)] + w_prev, inc="sDVE")
                sch.ev("scalar", lambda e, xt=xt, sq=sq, st=st, a=a: e.activation(
                    sq, xt, AF.Square, accum_out=st[:, 4 + a:5 + a]),
                       waits=[(ldsem, ldb)] + w_prev, inc="sACT")
            d_sums = sch.ctr["sDVE"]
            a_sqs = sch.ctr["sACT"]
            m1 = sch.ev("vector", lambda e, st=st: e.tensor_scalar(st[:, 8:12], st[:, 0:4], 1.0 / D, None, ALU.mult),
                        waits=[("sDVE", d_sums)], inc="sDVE")
            m2 = sch.ev("vector", lambda e, st=st: e.tensor_scalar(st[:, 12:16], st[:, 4:8], 1.0 / D, None, ALU.mult),
                        waits=[("sACT", a_sqs)], inc="sDVE")
            m3 = sch.ev("vector", lambda e, st=st: e.tensor_mul(st[:, 16:20], st[:, 8:12], st[:, 8:12]),
                        waits=[("sDVE", m1)], inc="sDVE")
            m4 = sch.ev("vector", lambda e, st=st: e.tensor_sub(st[:, 12:16], st[:, 12:16], st[:, 16:20]),
                        waits=[("sDVE", m3)], inc="sDVE")
            epsc = constcol(EPS)
            sch.ev("scalar", lambda e, st=st, epsc=epsc: e.activation(st[:, 16:20], st[:, 12:16], AF.Sqrt, bias=epsc),
                   waits=[("sDVE", m4)], inc="sACT")
            a_rt = sch.ctr["sACT"]
            m5 = sch.ev("vector", lambda e, st=st: e.reciprocal(st[:, 20:24], st[:, 16:20]),
                        waits=[("sACT", a_rt)], inc="sDVE")
            m6 = sch.ev("vector", lambda e, st=st: e.scalar_tensor_tensor(
                st[:, 0:4], st[:, 8:12], -1.0, st[:, 20:24], ALU.mult, ALU.mult),
                        waits=[("sDVE", m5)], inc="sDVE")
            w_st = ([(stsem, st_done[b - 2])] if b >= 2
                    else [(stsem, prev_st[b % 2])])
            for a in range(4):
                xt = bx[:, a * 1024:(a + 1) * 1024]
                lt = bl[:, a * 1024:(a + 1) * 1024]
                sch.ev("scalar", lambda e, xt=xt, lt=lt, st=st, a=a: e.activation(
                    lt, xt, AF.Identity, bias=st[:, 0 + a:1 + a], scale=st[:, 20 + a:21 + a]),
                       waits=w_st + [("sDVE", m6)], inc="sACT")
            ap_done[b] = sch.ctr["sACT"]
            dstv = ap3(dst_dram, b * 512 * D, [[D, 128], [128 * D, 4], [1, D]])
            srcv = ap3(bl, 0, [[4096, 128], [1024, 4], [1, 1024]])
            sch.ev("gpsimd", lambda e, dstv=dstv, srcv=srcv: e.dma_start(out=dstv, in_=srcv),
                   waits=[("sACT", ap_done[b])], inc=stsem, k=16)
            if dbg_dram is not None:
                dbv = ap3(dbg_dram, b * 512 * D, [[D, 128], [128 * D, 4], [1, D]])
                sch.ev("scalar", lambda e, dbv=dbv, srcv=srcv: e.dma_start(out=dbv, in_=srcv),
                       waits=[("sACT", ap_done[b])], inc="sST", k=16)
            if dbg_dram is not None and b == 0:
                D0S = nc.dram_tensor("DBG0_ST", [128, 24], F32, kind="ExternalOutput").ap()
                D0X = nc.dram_tensor("DBG0_X", [128, 4096], F32, kind="ExternalOutput").ap()
                D0L = nc.dram_tensor("DBG0_L", [128, 4096], BF16, kind="ExternalOutput").ap()
                sch.ev("scalar", lambda e, st=st: e.dma_start(out=D0S, in_=st),
                       waits=[("sACT", ap_done[b])], inc="sST", k=16)
                sch.ev("scalar", lambda e, bx=bx: e.dma_start(out=D0X, in_=bx),
                       waits=[("sACT", ap_done[b])], inc="sST", k=16)
                sch.ev("scalar", lambda e, bl=bl: e.dma_start(out=D0L, in_=bl),
                       waits=[("sACT", ap_done[b])], inc="sST", k=16)
                dbg_guard[b] = sch.ctr["sST"]
            st_done[b] = sch.ctr[stsem]
        return st_done[nblk - 1]


    JB = S // 512
    QBn = TPC // 512

    def emit_group_proj(g, fence, psd_in):
        for (wsb, wt) in ((wk_sb, WK_T), (wv_sb, WV_T)):
            iv = ap3(wt, 256 * g, [[D, 128], [128 * D, 8], [1, 256]])
            ov = ap3(wsb, 0, [[2048, 128], [256, 8], [1, 256]])
            sch.ev("gpsimd", lambda e, iv=iv, ov=ov: e.dma_start(out=ov, in_=iv),
                   waits=fence, inc="sGP", k=16)
        w_w = sch.ctr["sGP"]
        mm_done = {}
        kb_free = {"A0": fence, "A1": fence, "B0": fence, "B1": fence, "C0": fence, "C1": fence}
        kv_mile = {}
        ln_ld = {}
        for jb in range(JB):
            buf = lnT[jb % 2]
            w_ln = list(fence)
            if jb >= 2:
                w_ln = [("sPE", mm_done[jb - 2])]
            lsem = f"sLN{jb % 2}"
            for dc in range(8):
                iv = ap3(ln1_dram, jb * 512 * D + dc * 128, [[D, 512], [1, 128]])
                sch.ev("sync" if dc < 4 else "scalar",
                       lambda e, buf=buf, dc=dc, iv=iv: e.dma_start_transpose(
                           out=buf[:, dc * 512:(dc + 1) * 512], in_=iv),
                       waits=w_ln, inc=lsem, k=16)
            ln_ld[jb] = sch.ctr[lsem]
            for fc in range(2):
                w_mm = [(lsem, ln_ld[jb]), ("sGP", w_w)] + kb_free[f"A{fc}"]
                for dc in range(8):
                    sch.ev("tensor", lambda e, fc=fc, dc=dc, buf=buf: e.matmul(
                        psA[:, fc * 512:(fc + 1) * 512],
                        wk_sb[:, dc * 256 + fc * 128: dc * 256 + fc * 128 + 128],
                        buf[:, dc * 512:(dc + 1) * 512],
                        start=(dc == 0), stop=(dc == 7)),
                           waits=(w_mm if dc == 0 else []), inc=("sPE" if dc == 7 else None))
                p_mm = sch.ctr["sPE"]
                hhA, hhB = 2 * fc, 2 * fc + 1
                bcol = 8 + 2 * g + fc
                aev = sch.ev("scalar", lambda e, hhA=hhA, fc=fc, jb=jb, bcol=bcol: e.activation(
                    K_aug[hhA][0:64, jb * 512:(jb + 1) * 512], psA[0:64, fc * 512:(fc + 1) * 512],
                    AF.Identity, bias=bqkv_sb[0:64, bcol:bcol + 1]),
                       waits=[("sPE", p_mm)], inc="sACT")
                dev = sch.ev("vector", lambda e, hhB=hhB, fc=fc, jb=jb, bcol=bcol: e.tensor_scalar(
                    K_aug[hhB][0:64, jb * 512:(jb + 1) * 512], psA[64:128, fc * 512:(fc + 1) * 512],
                    bqkv_sb[64:128, bcol:bcol + 1], None, ALU.add),
                       waits=[("sPE", p_mm)], inc="sDVE")
                kb_free[f"A{fc}"] = [("sACT", aev), ("sDVE", dev)]
            for sub in range(4):
                jc = 4 * jb + sub
                half = sub % 2
                w_mm = [(lsem, ln_ld[jb]), ("sGP", w_w)] + kb_free[f"B{half}"]
                for dc in range(8):
                    sch.ev("tensor", lambda e, half=half, dc=dc, sub=sub, buf=buf: e.matmul(
                        psB[:, half * 512: half * 512 + 256],
                        buf[:, dc * 512 + sub * 128: dc * 512 + sub * 128 + 128],
                        wv_sb[:, dc * 256:(dc + 1) * 256],
                        start=(dc == 0), stop=(dc == 7)),
                           waits=(w_mm if dc == 0 else []), inc=("sPE" if dc == 7 else None))
                p_mm = sch.ctr["sPE"]
                srcv = ap3(psB, half * 512, [[1024, 128], [64, 4], [1, 64]])
                dstv = ap3(V_sb, jc * 260, [[64 * 260, 128], [65, 4], [1, 64]])
                dev = sch.ev("vector", lambda e, srcv=srcv, dstv=dstv: e.tensor_copy(dstv, srcv),
                             waits=[("sPE", p_mm)], inc="sDVE")
                kb_free[f"B{half}"] = [("sDVE", dev)]
            mm_done[jb] = sch.ctr["sPE"]
        ov = ap3(V_sb, 64, [[64 * 260, 128], [260, 64], [65, 4], [1, 1]])
        v_ones = sch.ev("vector", lambda e, ov=ov: e.memset(ov, 1.0),
                        waits=fence, inc="sDVE")
        ivq = ap3(WQ_T, 256 * g, [[D, 128], [128 * D, 8], [1, 256]])
        ovq = ap3(wq_sb, 0, [[2048, 128], [256, 8], [1, 256]])
        sch.ev("gpsimd", lambda e, ivq=ivq, ovq=ovq: e.dma_start(out=ovq, in_=ivq),
               waits=[("sPE", mm_done[JB - 1])], inc="sGP", k=16)
        w_wq = sch.ctr["sGP"]
        psDb = psD.bitcast(BF16)
        q_copy = {}
        q_mm = {}
        pool_free = {0: list(psd_in) + fence, 1: list(psd_in) + fence}
        region_free = {0: list(psd_in), 1: list(psd_in)}
        for qb in range(QBn):
            buf = lnT[qb % 2]
            w_ln = ([("sPE", mm_done[JB - 2 + (qb % 2)])] if qb < 2
                    else [("sPE", q_mm[qb - 2])])
            for j in range(4):
                k = qb * 4 + j
                lsem = f"sLN{k % 2}"
                tok_buf = P_pool[k % 2]
                iv = ap3(ln1own_dram, (qb * 512 + j * 128) * D, [[D, 128], [1, D]])
                sch.ev("sync" if k % 2 == 0 else "scalar",
                       lambda e, tok_buf=tok_buf, iv=iv: e.dma_start(out=tok_buf, in_=iv),
                       waits=pool_free[k % 2], inc=lsem, k=16)
                ld_k = sch.ctr[lsem]
                p_tr = None
                for dc in range(8):
                    r = (k * 8 + dc) % 2
                    p_tr = sch.ev("tensor", lambda e, tok_buf=tok_buf, dc=dc, r=r: e.transpose(
                        psDb[:, r * 1024:r * 1024 + 128],
                        tok_buf[:, dc * 128:(dc + 1) * 128], identb_sb),
                                  waits=[(lsem, ld_k)] + region_free[r], inc="sPE")
                    d_c = sch.ev("vector", lambda e, buf=buf, dc=dc, j=j, r=r: e.tensor_copy(
                        buf[:, dc * 512 + j * 128:dc * 512 + j * 128 + 128],
                        psDb[:, r * 1024:r * 1024 + 128]),
                                 waits=[("sPE", p_tr)] + w_ln, inc="sDVE")
                    region_free[r] = [("sDVE", d_c)]
                pool_free[k % 2] = [("sPE", p_tr)]
            q_copy[qb] = sch.ctr["sDVE"]
            for fc in range(2):
                w_mm = [("sDVE", q_copy[qb]), ("sGP", w_wq)] + kb_free[f"C{fc}"]
                for dc in range(8):
                    sch.ev("tensor", lambda e, fc=fc, dc=dc, buf=buf: e.matmul(
                        psC[:, fc * 512:(fc + 1) * 512],
                        wq_sb[:, dc * 256 + fc * 128: dc * 256 + fc * 128 + 128],
                        buf[:, dc * 512:(dc + 1) * 512],
                        start=(dc == 0), stop=(dc == 7)),
                           waits=(w_mm if dc == 0 else []), inc=("sPE" if dc == 7 else None))
                p_mm = sch.ctr["sPE"]
                aev = sch.ev("scalar", lambda e, fc=fc, qb=qb, g=g: e.activation(
                    Q_sb[:, fc * TPC + qb * 512: fc * TPC + qb * 512 + 512],
                    psC[:, fc * 512:(fc + 1) * 512],
                    AF.Identity, scale=0.125, bias=bq8_sb[:, 2 * g + fc:2 * g + fc + 1]),
                       waits=[("sPE", p_mm)], inc="sACT")
                kb_free[f"C{fc}"] = [("sACT", aev)]
            q_mm[qb] = sch.ctr["sPE"]
        kv_ready = {"k": [("sACT", sch.ctr["sACT"]), ("sDVE", sch.ctr["sDVE"])],
                    "q": [("sACT", sch.ctr["sACT"])],
                    "v": [("sDVE", max(v_ones, sch.ctr["sDVE"]))],
                    "psd": [("sPE", sch.ctr["sPE"]), ("sDVE", sch.ctr["sDVE"])]}
        return kv_ready

    def emit_head_prep(g, hh, kv_ready, psfree, guards, out):
        h = 4 * g + hh
        qaug = qaugs[hh % 2]
        QSEM, RPSEM = QSEMS[hh], RPSEMS[hh]
        fc, rhalf = hh // 2, 64 * (hh % 2)
        wq = kv_ready["q"] + kv_ready["k"] + list(guards)
        d_q = sch.ev("vector", lambda e, fc=fc, rhalf=rhalf: e.tensor_copy(
            qh1[0:64, :], Q_sb[rhalf:rhalf + 64, fc * TPC:(fc + 1) * TPC]),
                     waits=wq, inc="sDVE")
        d_q = sch.ev("vector", lambda e: e.memset(qh1[64:65, :], 1.0), waits=wq, inc="sDVE")
        for t in "LBR":
            sch.ev("vector", lambda e, t=t, fc=fc, rhalf=rhalf: e.tensor_copy(
                qaug[t][0:64, :], Q_sb[rhalf:rhalf + 64, fc * TPC:(fc + 1) * TPC]),
                   waits=wq, inc="sDVE")
        d_base = sch.ctr["sDVE"]
        out["qaug"] = [("sDVE", d_base)]
        yield
        bank0_free = list(psfree)
        d_ks = sch.ev("vector", lambda e, hh=hh: e.tensor_copy(
            ksub[:, 0:256], ap3(K_aug[hh], 0, [[S, 64], [32, 256]])),
                      waits=kv_ready["k"] + list(guards), inc="sDVE")
        mg = 192 * h + 64
        d_ks = sch.ev("vector", lambda e, hh=hh, mg=mg: e.tensor_copy(
            ksub[:, 256:320], K_aug[hh][0:64, mg:mg + 64]),
                      waits=kv_ready["k"] + list(guards) + [("sDVE", d_ks)], inc="sDVE")
        tr_free = list(psfree)
        mm_free = list(bank0_free)
        for qc in range(16):
            p_m = sch.ev("tensor", lambda e, qc=qc: e.matmul(
                psD[:, 0:320], qh1[0:64, qc * 128:(qc + 1) * 128], ksub,
                start=True, stop=True),
                         waits=mm_free + [("sDVE", d_ks), ("sDVE", d_q)], inc="sPE")
            d_m = sch.ev("vector", lambda e: e.reduce_max(maxk_col2, psD[:, 0:320], axis=AX.X),
                         waits=[("sPE", p_m)], inc="sDVE")
            mm_free = [("sDVE", d_m)]
            yield
            p_tt = sch.ev("tensor", lambda e: e.transpose(psD[0:1, 640:768], maxk_col2,
                                                          ident_sb),
                          waits=[("sDVE", d_m)] + tr_free, inc="sPE")
            d_mc = sch.ev("vector", lambda e, qc=qc: e.tensor_copy(
                tmp_row[0:1, qc * 128:(qc + 1) * 128], psD[0:1, 640:768]),
                          waits=[("sPE", p_tt)], inc="sDVE")
            tr_free = [("sDVE", d_mc)]
            yield
        rfree = mm_free + tr_free
        ctr_free = list(guards)
        r01_free = list(guards)
        r01_psum = list(psfree)
        pe_w = rfree
        mp_free = tr_free
        for qic in range(16):
            p_rp = sch.ev("tensor", lambda e, qic=qic: e.matmul(
                psD[:, 0:261], qh1[:, qic * 128:(qic + 1) * 128], wrbc8[:, 0:261],
                start=True, stop=True),
                          waits=pe_w + [("sDVE", d_q)], inc="sPE")
            p_r01 = sch.ev("tensor", lambda e, qic=qic: e.matmul(
                psD[:, 512:514], qh1[:, qic * 128:(qic + 1) * 128], wr01_8[:, 0:2],
                start=True, stop=True), waits=r01_psum, inc="sPE")
            a_c = sch.ev("scalar", lambda e: e.activation(
                rp_pad_sb[:, 640:901], psD[:, 0:261], AF.Copy),
                         waits=[("sPE", p_r01)] + ctr_free, inc="sACT")
            pe_w = [("sACT", a_c)]
            d_mp = sch.ev("vector", lambda e: e.reduce_max(maxk_col2, psD[:, 0:261], axis=AX.X),
                          waits=[("sPE", p_r01), ("sACT", a_c)], inc="sDVE")
            yield
            p_tp = sch.ev("tensor", lambda e: e.transpose(psD[0:1, 640:768], maxk_col2,
                                                          ident_sb),
                          waits=[("sDVE", d_mp)] + mp_free, inc="sPE")
            d_mq = sch.ev("vector", lambda e, qic=qic: e.tensor_copy(
                qn_row[0:1, qic * 128:(qic + 1) * 128], psD[0:1, 640:768]),
                          waits=[("sPE", p_tp)], inc="sDVE")
            mp_free = [("sDVE", d_mq)]
            pe_w.append(("sPE", p_tp))
            d_c2 = sch.ev("vector", lambda e: e.tensor_copy(rp01[:, 0:2], psD[:, 512:514]),
                          waits=[("sPE", p_tp)] + r01_free, inc="sDVE")
            d_l = sch.ev("vector", lambda e: e.tensor_copy(
                rp_pad_sb[:, 0:640], ap3(rp01, 0, [[2, 128], [0, 640]])),
                         waits=[("sDVE", d_c2)] + ctr_free, inc="sDVE")
            d_r2 = sch.ev("vector", lambda e: e.tensor_copy(
                rp_pad_sb[:, 901:W_PAD], ap3(rp01, 1, [[2, 128], [0, W_PAD - 901]])),
                         waits=[("sDVE", d_l)], inc="sDVE")
            pe_w.append(("sDVE", d_r2))
            ov = ap3(rp_dram[hh], qic * 128 * W_PAD, [[W_PAD, 128], [1, W_PAD]])
            gc = sch.ev("sync", lambda e, ov=ov: e.dma_start(out=ov, in_=rp_pad_sb),
                        waits=[("sACT", a_c), ("sDVE", d_r2)], inc=RPSEM, k=16)
            ctr_free = [(RPSEM, gc)]
            r01_free = [(RPSEM, gc)]
            r01_psum = [("sDVE", d_c2), ("sDVE", d_mq)]
            yield
        out["rp_stored"] = [(RPSEM, sch.ctr[RPSEM])]
        d_c = sch.ev("vector", lambda e: e.scalar_tensor_tensor(
            c_row, tmp_row, 8.0, qn_row, ALU.add, ALU.add),
                     waits=mp_free + [("sDVE", sch.ctr["sDVE"])], inc="sDVE")
        rfree2 = pe_w
        for qb in range(QBn):
            for col in range(2):
                row = rp0_row if col == 0 else rp256_row
                p_r = sch.ev("tensor", lambda e, qb=qb, col=col: e.matmul(
                    psD[0:1, 0:512], wr01_8[:, col:col + 1], qh1[:, qb * 512:(qb + 1) * 512],
                    start=True, stop=True),
                             waits=rfree2 + [("sDVE", d_q)], inc="sPE")
                d_r = sch.ev("vector", lambda e, qb=qb, row=row: e.tensor_copy(
                    row[0:1, qb * 512:(qb + 1) * 512], psD[0:1, 0:512]),
                             waits=[("sPE", p_r)], inc="sDVE")
                rfree2 = [("sDVE", d_r)]
                yield
        rfree = rfree2
        t_marks = []
        qn_free = [("sDVE", d_c)]
        for t, mk, fine_row in (("L", rp0_row, rp0_row), ("R", rp256_row, rp256_row),
                                ("B", None, rp0_row)):
            if mk is None:
                d_t = sch.ev("vector", lambda e: e.tensor_scalar(
                    tmp_row, c_row, -1.0, None, ALU.mult),
                             waits=[("sDVE", d_c)], inc="sDVE")
            else:
                d_t = sch.ev("vector", lambda e, mk=mk: e.tensor_sub(tmp_row, mk, c_row),
                             waits=rfree + [("sDVE", d_c)], inc="sDVE")
            d_co = sch.ev("vector", lambda e: e.tensor_copy(qn2_row, tmp_row),
                          waits=[("sDVE", d_t)] + qn_free, inc="sDVE")
            g_co = sch.ev("gpsimd", lambda e, t=t: e.dma_start(
                out=qaug[t][64:65, :], in_=qn2_row), waits=[("sDVE", d_co)], inc=QSEM, k=16)
            d_f = sch.ev("vector", lambda e, fine_row=fine_row: e.tensor_sub(
                fine_row, tmp_row, qn2_row), waits=[("sDVE", d_co)], inc="sDVE")
            g_f = sch.ev("gpsimd", lambda e, t=t, fine_row=fine_row: e.dma_start(
                out=qaug[t][65:66, :], in_=fine_row),
                         waits=[("sDVE", d_f)], inc=QSEM, k=16)
            qn_free = [(QSEM, g_f)]
            t_marks.append((QSEM, g_f))
            yield
        out["qaug"] = t_marks + [("sDVE", d_base)]
        out["rp_stored"] = [(RPSEM, sch.ctr[RPSEM])]
        out["psD_user"] = pe_w + rfree + mp_free
        out["tail"] = [("sDVE", sch.ctr["sDVE"]), ("sACT", sch.ctr["sACT"]),
                       ("sPE", sch.ctr["sPE"]), (RPSEM, sch.ctr[RPSEM]),
                       (QSEM, sch.ctr[QSEM])]

    ot_chain = {}

    def emit_head_attn(g, hh, prep, sfree_in, ot_free, pos_entry, bg=None):
        h = 4 * g + hh
        qaug = qaugs[hh % 2]
        sbanks = [psA, psB]
        ot_dma_free = ot_chain
        s_free = {0: list(sfree_in[0]), 1: list(sfree_in[1])}
        exp_mile = {}
        pv_mile = {}
        pos_ld = {}
        pos_free = {}
        s_stop = {}
        O_T = psC
        ot_in = list(ot_free)
        for qbp in range(2):

            def load_pos(qic):
                p_first = max(0, 4 * qic - 1)
                n_p = min(63, 4 * qic + 4) - p_first + 1
                width = 128 * n_p
                off = qic * 128 * W_PAD + KC + (128 * p_first - 512 * qic)
                iv = ap3(rp_dram[hh], off, [[W_PAD - 4, 128], [1, width]])
                buf = pos_pool[qic % 2]
                psem = f"sPOS{qic % 2}"
                w = prep["rp_stored"] + pos_free.get(qic - 2, list(pos_entry))
                sch.ev("gpsimd", lambda e, buf=buf, iv=iv, width=width: e.dma_start(
                    out=buf[:, 0:width], in_=iv), waits=w, inc=psem, k=16)
                pos_ld[qic] = (psem, sch.ctr[psem])

            qic_lo, qic_hi = 8 * qbp, 8 * qbp + 8

            def emit_S(p, qbp=qbp, qic_lo=qic_lo, qic_hi=qic_hi):
                if qbp == 0 and p == 0:
                    load_pos(0)
                for qic in range(qic_lo, qic_hi):
                    if qic >= 1 and max(1, 4 * qic - 6) == p and qic not in pos_ld:
                        load_pos(qic)
                bank = sbanks[p % 2]
                ops = []
                for qb2 in range(2):
                    qb = 2 * qbp + qb2
                    cls = classify_qb(p, qb)
                    if cls in "LR":
                        ops.append(("far512", qb2, cls, None))
                    else:
                        for qic in range(4 * qb, 4 * qb + 4):
                            c = classify(p, qic)
                            if c == 'S':
                                ops.append(("band", qb2, c, qic))
                            else:
                                ops.append(("far128", qb2, c, qic))
                started = {0: False, 1: False}
                n_emitted = 0
                w_first = s_free[p % 2] + prep["qaug"]
                pos_w = sorted({o[3] for o in ops if o[0] == "band"})
                for qic in pos_w:
                    w_first = w_first + [pos_ld[qic]]
                last_by_half = {}
                for idx, (kind, qb2, c, qic) in enumerate(ops):
                    if kind == "band":
                        last_by_half[qb2] = idx
                    else:
                        last_by_half[qb2] = idx
                seen = {0: 0, 1: 0}
                tot = {0: sum(1 for o in ops if o[1] == 0) + sum(1 for o in ops if o[1] == 0 and o[0] == "band"),
                       1: sum(1 for o in ops if o[1] == 1) + sum(1 for o in ops if o[1] == 1 and o[0] == "band")}
                for (kind, qb2, c, qic) in ops:
                    qb = 2 * qbp + qb2
                    first = not started[qb2]
                    started[qb2] = True
                    if kind == "far512":
                        seen[qb2] += 1
                        stop = seen[qb2] == tot[qb2]
                        sch.ev("tensor", lambda e, bank=bank, qb2=qb2, c=c, qb=qb, p=p, hh=hh, first=first, stop=stop: e.matmul(
                            bank[:, qb2 * 512:(qb2 + 1) * 512],
                            K_aug[hh][:, p * 128:(p + 1) * 128],
                            qaug[c][:, qb * 512:(qb + 1) * 512],
                            start=first, stop=stop),
                               waits=(w_first if n_emitted == 0 else []),
                               inc=None)
                        n_emitted += 1
                    elif kind == "far128":
                        seen[qb2] += 1
                        stop = seen[qb2] == tot[qb2]
                        col = qb2 * 512 + (qic - 4 * qb) * 128
                        sch.ev("tensor", lambda e, bank=bank, col=col, c=c, qic=qic, p=p, hh=hh, first=first, stop=stop: e.matmul(
                            bank[:, col:col + 128],
                            K_aug[hh][:, p * 128:(p + 1) * 128],
                            qaug[c][:, qic * 128:(qic + 1) * 128],
                            start=first, stop=stop),
                               waits=(w_first if n_emitted == 0 else []),
                               inc=None)
                        n_emitted += 1
                    else:
                        col = qb2 * 512 + (qic - 4 * qb) * 128
                        p_first_q = max(0, 4 * qic - 1)
                        pcol = (p - p_first_q) * 128
                        buf = pos_pool[qic % 2]
                        seen[qb2] += 2
                        stop = seen[qb2] == tot[qb2]
                        sch.ev("tensor", lambda e, bank=bank, col=col, buf=buf, pcol=pcol, first=first: e.matmul(
                            bank[:, col:col + 128], buf[:, pcol:pcol + 128], ident_sb,
                            start=first, stop=False, is_transpose=True),
                               waits=(w_first if n_emitted == 0 else []),
                               inc=None)
                        n_emitted += 1
                        sch.ev("tensor", lambda e, bank=bank, col=col, qic=qic, p=p, hh=hh, stop=stop: e.matmul(
                            bank[:, col:col + 128],
                            K_aug[hh][:, p * 128:(p + 1) * 128],
                            qaug["B"][:, qic * 128:(qic + 1) * 128],
                            start=False, stop=stop),
                               inc=None)
                        n_emitted += 1
                eng, fn, wts, inc0, k0 = sch.events[-1]
                assert eng == "tensor" and inc0 is None
                sch.ctr["sPE"] += 1
                sch.events[-1] = (eng, fn, wts, "sPE", 1)
                p_s = sch.ctr["sPE"]
                for qic in pos_w:
                    pos_free[qic] = [("sPE", p_s)]
                s_stop[p] = p_s

            def emit_exp(p):
                w_exp = [("sPE", s_stop[p])] + pv_mile.get(p - 3, [])
                a_e = sch.ev("scalar", lambda e, bank=sbanks[p % 2], p=p: e.activation(
                    P_pool[p % 3], bank, AF.Exp), waits=w_exp, inc="sACT")
                exp_mile[p] = a_e
                s_free[p % 2] = [("sACT", a_e)]

            def emit_PV(p):
                for qb2 in range(2):
                    sch.ev("tensor", lambda e, qb2=qb2, p=p, hh=hh: e.matmul(
                        O_T[0:65, qb2 * 512:(qb2 + 1) * 512],
                        V_sb[:, p * 260 + hh * 65: p * 260 + hh * 65 + 65],
                        P_pool[p % 3][:, qb2 * 512:(qb2 + 1) * 512],
                        start=(p == 0), stop=(p == 63)),
                           waits=([("sACT", exp_mile[p])] + (ot_in if p == 0 else []))
                                 if qb2 == 0 else [],
                           inc=("sPE" if qb2 == 1 else None))
                pv_mile[p] = [("sPE", sch.ctr["sPE"])]

            for it in range(66):
                if it <= 63:
                    emit_S(it)
                if 0 <= it - 1 <= 63:
                    emit_exp(it - 1)
                if 0 <= it - 2 <= 63:
                    emit_PV(it - 2)
                if bg is not None:
                    next(bg, None)
            pv_last = sch.ctr["sPE"]
            ev_marks = []
            bcast_free = []
            for qb2 in range(2):
                qb = 2 * qbp + qb2
                def _recip(e, qb2=qb2):
                    with nc.allow_low_precision("bf16 recip of softmax denominators is within tolerance"):
                        return e.reciprocal(rcp_sb[64:65, 0:512],
                                            O_T[64:65, qb2 * 512:(qb2 + 1) * 512])
                d_r = sch.ev("vector", _recip, waits=[("sPE", pv_last)] + bcast_free,
                             inc="sDVE")
                p_b = sch.ev("tensor", lambda e, qb2=qb2: e.matmul(
                    O_T[64:128, qb2 * 512:(qb2 + 1) * 512],
                    onesrow65[64:65, 0:64],
                    rcp_sb[64:65, 0:512],
                    start=True, stop=True),
                             waits=[("sDVE", d_r)] + bcast_free, inc="sPE")
                d_b = sch.ev("vector", lambda e, qb2=qb2: e.tensor_copy(
                    rb_sb, O_T[64:128, qb2 * 512:(qb2 + 1) * 512]),
                             waits=[("sPE", p_b)] + ot_dma_free.get("rb", []), inc="sDVE")
                bcast_free = [("sDVE", d_b)]
                d_o = sch.ev("vector", lambda e, qb2=qb2: e.scalar_tensor_tensor(
                    o_tile,
                    O_T[0:64, qb2 * 512:(qb2 + 1) * 512],
                    1.0, rb_sb, ALU.mult, ALU.mult),
                             waits=[("sDVE", d_b)] + ot_dma_free.get("o", []), inc="sDVE")
                ot_dma_free["rb"] = [("sDVE", d_o)]
                ov = ap3(o_dram, (64 * h) * TPC + qb * 512, [[TPC, 64], [1, 512]])
                g_o = sch.ev("sync", lambda e, ov=ov: e.dma_start(
                    out=ov, in_=o_tile),
                             waits=[("sDVE", d_o)], inc="sSC", k=16)
                ot_dma_free["o"] = [("sSC", g_o)]
                ev_marks.append(("sDVE", d_o))
                ev_marks.append(("sPE", p_b))
            ot_in = list(ev_marks)
        if bg is not None:
            for _ in bg:
                pass
        return {"s_free": [s_free[0], s_free[1]], "ot_done": list(ev_marks),
                "pv_last": [("sPE", sch.ctr["sPE"])],
                "exp_done": [("sACT", sch.ctr["sACT"])],
                "o_stored": [("sSC", sch.ctr["sSC"])]}


    def emit_pc_pd(attn_done):
        es_pc = contextlib.ExitStack()
        sbc = lambda name, shape, dt: es_pc.enter_context(nc.sbuf_tensor(name, shape, dt)).ap()
        x2 = [sbc(f"x2_{dc}", [128, TPC], F32) for dc in range(8)]
        ln2 = [sbc(f"ln2_{dc}", [128, 1024], BF16) for dc in range(8)]
        g_sb = sbc("g_sb", [128, 16 * 1024], BF16)
        w3c = [sbc(f"w3c{i}", [128, 4 * 1024], BF16) for i in range(2)]
        w1c4 = [sbc(f"w1c4_{i}", [128, 8 * 512], BF16) for i in range(2)]
        w2c4 = [sbc(f"w2c4_{i}", [128, 8 * 512], BF16) for i in range(2)]
        wb2 = [sbc(f"wb2_{i}", [128, 1024], BF16) for i in range(2)]
        s_tmp = sbc("s_tmp", [128, 1024], BF16)
        sq_scr = sbc("sq_scr", [128, 1024], F32)
        gl = sbc("gl", [128, 8], F32)
        gl2 = sbc("gl2", [128, 8], F32)
        gl3 = sbc("gl3", [128, 8], F32)
        gcol = sbc("gcol", [128, 6], F32)
        wtok = sbc("wtok", [128, 8], BF16)
        w8_rows = sbc("w8_rows", [8, 1024], BF16)
        ident_bf = sbc("ident_bf", [128, 128], BF16)
        onesrow_f = sbc("onesrow_f", [1, 128], F32)
        mean_row = sbc("mean_row", [1, TPC], F32)
        rstd_row = sbc("rstd_row", [1, TPC], F32)
        tmp2_row = sbc("tmp2_row", [1, TPC], F32)
        we_dram = nc.dram_tensor("we_dram", [E, TPC], BF16, kind="Internal")

        d0 = sch.ev("vector", lambda e: e.tensor_copy(ident_bf, ident_sb),
                    waits=attn_done, inc="sDVE")
        sch.ev("vector", lambda e: e.memset(onesrow_f, 1.0), inc="sDVE")
        consts_d = sch.ctr["sDVE"]

        x_mark = {}
        for dc in range(8):
            iv = ap3(xT_own, dc * 128 * TPC, [[TPC, 128], [1, TPC]])
            sch.ev("sync" if dc % 2 == 0 else "scalar",
                   lambda e, iv=iv, dc=dc: e.dma_start(out=x2[dc], in_=iv),
                   waits=attn_done, inc="sLD", k=16)
            x_mark[dc] = ("sLD", sch.ctr["sLD"])
        o_stage = [s_tmp, sq_scr]
        stt_mark = {}
        for k in range(16):
            dc, hf = k // 2, k % 2
            ov = ap3(o_dram, dc * 128 * TPC + hf * 1024, [[TPC, 128], [1, 1024]])
            w = ([stt_mark[k - 2]] if k >= 2 else list(attn_done))
            sch.ev("gpsimd", lambda e, ov=ov, k=k: e.dma_start(
                out=o_stage[k % 2][:, 0:1024], in_=ov),
                   waits=w, inc="sGP", k=16)
            o_mark = ("sGP", sch.ctr["sGP"])
            d = sch.ev("vector", lambda e, dc=dc, hf=hf, k=k: e.scalar_tensor_tensor(
                x2[dc][:, hf * 1024:(hf + 1) * 1024],
                o_stage[k % 2][:, 0:1024],
                bqkv_sb[:, 16 + dc:17 + dc],
                x2[dc][:, hf * 1024:(hf + 1) * 1024], ALU.add, ALU.add),
                       waits=[o_mark, x_mark[dc], ("sDVE", consts_d)], inc="sDVE")
            stt_mark[k] = ("sDVE", d)
        x2_all = [("sDVE", sch.ctr["sDVE"])]

        a_prev = []
        sqh_prev = {0: x2_all, 1: x2_all}
        for qb in range(QBn):
            for dc in range(8):
                sch.ev("tensor", lambda e, qb=qb, dc=dc: e.matmul(
                    psA[0:1, 0:512], ones_sb, x2[dc][:, qb * 512:(qb + 1) * 512],
                    start=(dc == 0), stop=(dc == 7)),
                       waits=(x2_all + a_prev if dc == 0 else []),
                       inc=("sPE" if dc == 7 else None))
            p_sum = sch.ctr["sPE"]
            for dc in range(8):
                h = dc % 2
                a_s = sch.ev("scalar", lambda e, qb=qb, dc=dc, h=h: e.activation(
                    sq_scr[:, h * 512:(h + 1) * 512],
                    x2[dc][:, qb * 512:(qb + 1) * 512], AF.Square),
                             waits=sqh_prev[h] + (a_prev if dc == 0 else []), inc="sACT")
                m_s = sch.ev("tensor", lambda e, dc=dc, h=h: e.matmul(
                    psB[0:1, 0:512], ones_sb, sq_scr[:, h * 512:(h + 1) * 512],
                    start=(dc == 0), stop=(dc == 7)),
                       waits=[("sACT", a_s)], inc="sPE")
                sqh_prev[h] = [("sPE", m_s)]
            p_sq = sch.ctr["sPE"]
            d_m = sch.ev("vector", lambda e, qb=qb: e.tensor_scalar(
                mean_row[0:1, qb * 512:(qb + 1) * 512], psA[0:1, 0:512],
                1.0 / D, None, ALU.mult),
                         waits=[("sPE", p_sum)], inc="sDVE")
            d_v = sch.ev("vector", lambda e, qb=qb: e.tensor_scalar(
                tmp2_row[0:1, qb * 512:(qb + 1) * 512], psB[0:1, 0:512],
                1.0 / D, None, ALU.mult),
                         waits=[("sPE", p_sq), ("sDVE", d_m)], inc="sDVE")
            a_prev = [("sDVE", d_v)]
        d_m2r = sch.ev("vector", lambda e: e.tensor_mul(rstd_row, mean_row, mean_row),
                       waits=a_prev, inc="sDVE")
        d_var = sch.ev("vector", lambda e: e.tensor_sub(tmp2_row, tmp2_row, rstd_row),
                       waits=[("sDVE", d_m2r)], inc="sDVE")
        a_sr = sch.ev("scalar", lambda e: e.activation(rstd_row, tmp2_row, AF.Sqrt,
                                                       bias=constcol(EPS)[0:1, :]),
                      waits=[("sDVE", d_var)], inc="sACT")
        d_rs = sch.ev("vector", lambda e: e.reciprocal(tmp2_row, rstd_row),
                      waits=[("sACT", a_sr)], inc="sDVE")

        psA_free = [("sDVE", sch.ctr["sDVE"]), ("sPE", sch.ctr["sPE"])]
        psB_free = list(psA_free)
        regs = [(psC, 0), (psC, 512), (psD, 0), (psD, 512)]
        regs_free = [list(psA_free) for _ in range(4)]
        gl_free = []
        tr_free = []
        sq_free = [("sDVE", sch.ctr["sDVE"])]
        stmp_free = []
        x2_add_done = []
        we_done = {}
        ln2_done = []
        bc_free = psA_free
        ln2_used = []

        NQH = (E + 1) * 4
        NKH = (E + 1) * 4
        w12_ld, w12_freed = {}, {}
        w3_ld, w3_freed = {}, {}
        wb_ld, wb_freed = {}, {}
        gch_free = {0: list(attn_done), 1: list(attn_done)}

        def emit_w12_load(q):
            halfq, r = divmod(q, NQH)
            e_, hcg = divmod(r, 4)
            if e_ == E:
                i1 = ap3(sw1, hcg * 512, [[HF, 128], [128 * HF, 8], [1, 512]])
                i2 = ap3(sw2, hcg * 512, [[HF, 128], [128 * HF, 8], [1, 512]])
            else:
                off = e_ * D * HF + hcg * 512
                i1 = ap3(ew1, off, [[HF, 128], [128 * HF, 8], [1, 512]])
                i2 = ap3(ew2, off, [[HF, 128], [128 * HF, 8], [1, 512]])
            par = q % 2
            wsem = f"sW12{par}"
            od = [[4096, 128], [512, 8], [1, 512]]
            w = w12_freed.get(q - 2, list(attn_done))
            sch.ev("gpsimd", lambda e, i1=i1, par=par, od=od: e.dma_start(
                out=ap3(w1c4[par], 0, od), in_=i1), waits=w, inc=wsem, k=16)
            sch.ev("gpsimd", lambda e, i2=i2, par=par, od=od: e.dma_start(
                out=ap3(w2c4[par], 0, od), in_=i2), inc=wsem, k=16)
            w12_ld[q] = (wsem, sch.ctr[wsem])

        def emit_w3_load(k):
            halfk, r = divmod(k, NKH)
            e_, ch = divmod(r, 4)
            if e_ == E:
                iv = ap3(sw3, ch * 4 * 128 * D, [[D, 128], [128 * D, 4], [1, D]])
            else:
                iv = ap3(ew3, e_ * HF * D + ch * 4 * 128 * D,
                         [[D, 128], [128 * D, 4], [1, D]])
            par = k % 2
            ksem = f"sW3{par}"
            w = w3_freed.get(k - 2, list(attn_done))
            sch.ev("gpsimd", lambda e, iv=iv, par=par: e.dma_start(
                out=w3c[par], in_=iv), waits=w, inc=ksem, k=16)
            w3_ld[k] = (ksem, sch.ctr[ksem])

        def emit_wb_load(j, half):
            e_ = j - half * E
            par = j % 2
            bsem = f"sWB{par}"
            wv = ap3(we_dram, e_ * TPC + half * 1024, [[0, 128], [1, 1024]])
            w = wb_freed.get(j - 2, []) + we_done[half]
            sch.ev("gpsimd", lambda e, wv=wv, par=par: e.dma_start(
                out=wb2[par], in_=wv), waits=w, inc=bsem, k=16)
            wb_ld[j] = (bsem, sch.ctr[bsem])

        emit_w12_load(0)
        emit_w12_load(1)
        emit_w3_load(0)
        emit_w3_load(1)

        for half in range(2):
            toff = half * 1024
            for qbl in range(2):
                qb = 2 * half + qbl
                p_bm = sch.ev("tensor", lambda e, qb=qb: e.matmul(
                    psA[:, 0:512], onesrow_f, mean_row[0:1, qb * 512:(qb + 1) * 512],
                    start=True, stop=True),
                              waits=[("sDVE", d_rs), ("sDVE", consts_d)] + bc_free + psA_free,
                              inc="sPE")
                p_br = sch.ev("tensor", lambda e, qb=qb: e.matmul(
                    psB[:, 0:512], onesrow_f, tmp2_row[0:1, qb * 512:(qb + 1) * 512],
                    start=True, stop=True), waits=psB_free, inc="sPE")
                p_bc = sch.ctr["sPE"]
                d_prev = sq_free
                for dc in range(8):
                    d_s = sch.ev("vector", lambda e, dc=dc, qb=qb: e.tensor_sub(
                        sq_scr[:, 0:512], x2[dc][:, qb * 512:(qb + 1) * 512], psA[:, 0:512]),
                                 waits=[("sPE", p_bc)] + d_prev + ln2_used, inc="sDVE")
                    d_l = sch.ev("vector", lambda e, dc=dc, qbl=qbl: e.tensor_mul(
                        ln2[dc][:, qbl * 512:(qbl + 1) * 512], sq_scr[:, 0:512], psB[:, 0:512]),
                                 waits=[("sDVE", d_s)], inc="sDVE")
                    d_prev = [("sDVE", d_l)]
                sq_free = d_prev
                psA_free = d_prev
                psB_free = d_prev
                bc_free = d_prev
            ln2_done = [("sDVE", sch.ctr["sDVE"])]

            for tcl in range(8):
                for dc in range(8):
                    sch.ev("tensor", lambda e, tcl=tcl, dc=dc: e.matmul(
                        psC[:, 0:8], ln2[dc][:, tcl * 128:(tcl + 1) * 128],
                        gate_sb[:, dc * 8:(dc + 1) * 8],
                        start=(dc == 0), stop=(dc == 7)),
                           waits=(ln2_done + gl_free + regs_free[0] + [("sCW", 16)]
                                  if dc == 0 else []),
                           inc=("sPE" if dc == 7 else None))
                p_g = sch.ctr["sPE"]
                c1 = sch.ev("vector", lambda e: e.tensor_copy(gl, psC[:, 0:8]),
                            waits=[("sPE", p_g)], inc="sDVE")
                regs_free[0] = [("sDVE", c1)]
                gl_free = [("sDVE", c1)]
                c2 = sch.ev("vector", lambda e: e.reduce_max(gcol[:, 0:1], gl, axis=AX.X),
                            waits=[("sDVE", c1)], inc="sDVE")
                c3 = sch.ev("vector", lambda e: e.tensor_scalar(
                    gl2, gl, gcol[:, 0:1], None, ALU.is_ge), waits=[("sDVE", c2)], inc="sDVE")
                c4 = sch.ev("vector", lambda e: e.scalar_tensor_tensor(
                    gl2, gl2, -1e30, gl, ALU.mult, ALU.add), waits=[("sDVE", c3)], inc="sDVE")
                c5 = sch.ev("vector", lambda e: e.reduce_max(gcol[:, 1:2], gl2, axis=AX.X),
                            waits=[("sDVE", c4)], inc="sDVE")
                c6 = sch.ev("vector", lambda e: e.tensor_scalar(
                    gl2, gl, gcol[:, 1:2], None, ALU.is_ge), waits=[("sDVE", c5)], inc="sDVE")
                c7 = sch.ev("vector", lambda e: e.tensor_scalar(
                    gcol[:, 2:3], gcol[:, 0:1], -1.0, None, ALU.mult),
                            waits=[("sDVE", c2)], inc="sDVE")
                c8 = sch.ev("scalar", lambda e: e.activation(
                    gl3, gl, AF.Exp, bias=gcol[:, 2:3]),
                            waits=[("sDVE", c7), ("sDVE", c6)], inc="sACT")
                c9 = sch.ev("vector", lambda e: e.tensor_mul(gl3, gl3, gl2),
                            waits=[("sACT", c8)], inc="sDVE")
                c10 = sch.ev("vector", lambda e: e.reduce_sum(gcol[:, 3:4], gl3, axis=AX.X),
                             waits=[("sDVE", c9)], inc="sDVE")
                c11 = sch.ev("vector", lambda e: e.reciprocal(gcol[:, 4:5], gcol[:, 3:4]),
                             waits=[("sDVE", c10)], inc="sDVE")
                c12 = sch.ev("vector", lambda e: e.tensor_scalar(
                    wtok, gl3, gcol[:, 4:5], None, ALU.mult), waits=[("sDVE", c11)], inc="sDVE")
                p_t = sch.ev("tensor", lambda e: e.transpose(
                    psD.bitcast(BF16)[0:8, 0:128], wtok, ident_bf),
                             waits=[("sDVE", c12), ("sDVE", consts_d)] + tr_free + regs_free[2],
                             inc="sPE")
                c13 = sch.ev("vector", lambda e, tcl=tcl: e.tensor_copy(
                    w8_rows[:, tcl * 128:(tcl + 1) * 128], psD.bitcast(BF16)[0:8, 0:128]),
                             waits=[("sPE", p_t)], inc="sDVE")
                tr_free = [("sDVE", c13)]
                regs_free[2] = [("sDVE", c13)]
            w8_done = sch.ctr["sDVE"]
            ww = ap3(we_dram, toff, [[TPC, 8], [1, 1024]])
            m = sch.ev("gpsimd", lambda e, ww=ww: e.dma_start(out=ww, in_=w8_rows),
                       waits=[("sDVE", w8_done)], inc="sGP", k=16)
            we_done[half] = [("sGP", m)]

            for e_ in range(E + 1):
                shared = e_ == E
                j = half * E + e_
                if not shared:
                    emit_wb_load(j, half)
                g_ready = {}
                for hcg in range(4):
                    q = (half * (E + 1) + e_) * 4 + hcg
                    ld = w12_ld[q]
                    par = q % 2
                    for hh2 in range(4):
                        hc = hcg * 4 + hh2
                        co = hh2 * 128
                        for nb in range(2):
                            for dc in range(8):
                                sch.ev("tensor", lambda e, dc=dc, nb=nb, par=par, co=co: e.matmul(
                                    psA[:, nb * 512:(nb + 1) * 512],
                                    w1c4[par][:, dc * 512 + co:dc * 512 + co + 128],
                                    ln2[dc][:, nb * 512:(nb + 1) * 512],
                                    start=(dc == 0), stop=(dc == 7)),
                                       waits=(psA_free + [ld] + ln2_done
                                              if (nb == 0 and dc == 0) else []),
                                       inc=("sPE" if dc == 7 else None))
                        p_h1 = sch.ctr["sPE"]
                        for nb in range(2):
                            for dc in range(8):
                                sch.ev("tensor", lambda e, dc=dc, nb=nb, par=par, co=co: e.matmul(
                                    psB[:, nb * 512:(nb + 1) * 512],
                                    w2c4[par][:, dc * 512 + co:dc * 512 + co + 128],
                                    ln2[dc][:, nb * 512:(nb + 1) * 512],
                                    start=(dc == 0), stop=(dc == 7)),
                                       waits=(psB_free + [ld]
                                              if (nb == 0 and dc == 0) else []),
                                       inc=("sPE" if dc == 7 else None))
                        p_h2 = sch.ctr["sPE"]
                        a_si = sch.ev("scalar", lambda e: e.activation(s_tmp, psA, AF.Silu),
                                      waits=[("sPE", p_h1)] + stmp_free, inc="sACT")
                        d_g = sch.ev("vector", lambda e, hc=hc: e.tensor_mul(
                            g_sb[:, hc * 1024:(hc + 1) * 1024], s_tmp, psB),
                                     waits=[("sPE", p_h2), ("sACT", a_si)] + gch_free[hc // 8],
                                     inc="sDVE")
                        psA_free = [("sACT", a_si)]
                        psB_free = [("sDVE", d_g)]
                        stmp_free = [("sDVE", d_g)]
                        if not shared:
                            d_g = sch.ev("vector", lambda e, hc=hc, wpar=j % 2: e.tensor_mul(
                                g_sb[:, hc * 1024:(hc + 1) * 1024],
                                g_sb[:, hc * 1024:(hc + 1) * 1024], wb2[wpar]),
                                         waits=[wb_ld[j], ("sDVE", d_g)], inc="sDVE")
                        g_ready[hc] = d_g
                    w12_freed[q] = [("sPE", p_h2)]
                    if q + 2 < 2 * NQH:
                        emit_w12_load(q + 2)
                if not shared:
                    wb_freed[j] = [("sDVE", g_ready[15])]
                for ch in range(4):
                    k = (half * (E + 1) + e_) * 4 + ch
                    kpar = k % 2
                    gw = [("sDVE", g_ready[ch * 4 + 3]), w3_ld[k]]
                    for nb in range(2):
                        for dout in range(8):
                            r = (nb * 8 + dout) % 4
                            bank, cb = regs[r]
                            for i in range(4):
                                hc = ch * 4 + i
                                sch.ev("tensor", lambda e, i=i, hc=hc, dout=dout, nb=nb, kpar=kpar, bank=bank, cb=cb: e.matmul(
                                    bank[:, cb:cb + 512],
                                    w3c[kpar][:, i * 1024 + dout * 128:i * 1024 + dout * 128 + 128],
                                    g_sb[:, hc * 1024 + nb * 512:hc * 1024 + nb * 512 + 512],
                                    start=(i == 0), stop=(i == 3)),
                                       waits=(gw + regs_free[r] if i == 0 else []),
                                       inc=("sPE" if i == 3 else None))
                            p_w3 = sch.ctr["sPE"]
                            d_x = sch.ev("vector", lambda e, dout=dout, nb=nb, toff=toff, bank=bank, cb=cb: e.tensor_add(
                                x2[dout][:, toff + nb * 512:toff + (nb + 1) * 512],
                                x2[dout][:, toff + nb * 512:toff + (nb + 1) * 512],
                                bank[:, cb:cb + 512]),
                                         waits=[("sPE", p_w3)], inc="sDVE")
                            regs_free[r] = [("sDVE", d_x)]
                    w3_freed[k] = [("sPE", sch.ctr["sPE"])]
                    if ch % 2 == 1:
                        gch_free[ch // 2] = list(w3_freed[k])
                    if k + 2 < 2 * NKH:
                        emit_w3_load(k + 2)
            ln2_used = [("sPE", sch.ctr["sPE"])]
            x2_add_done = [("sDVE", sch.ctr["sDVE"])]

        for dc in range(8):
            ovv = ap3(OUT, dc * 128 * TPC, [[TPC, 128], [1, TPC]])
            sch.ev("scalar", lambda e, ovv=ovv, dc=dc: e.dma_start(out=ovv, in_=x2[dc]),
                   waits=x2_add_done, inc="sST", k=16)
        return es_pc

    ld0 = emit_const_loads()
    DBGL = nc.dram_tensor("DBG_LN1", [S, D], BF16, kind="ExternalOutput").ap() if DEBUG else None
    ln1_done = emit_ln_phase(x_rows, ln1_dram, S // 512, ld0, dbg_dram=DBGL)
    ln1own_done = emit_ln_phase(x_own_rows, ln1own_dram, TPC // 512, ld0)

    es_p1.close()

    es_attn = contextlib.ExitStack()
    sba = lambda name, shape, dt: es_attn.enter_context(nc.sbuf_tensor(name, shape, dt)).ap()
    wk_sb = sba("wk_sb", [128, 8 * 256], BF16)
    wv_sb = sba("wv_sb", [128, 8 * 256], BF16)
    wq_sb = wk_sb
    lnT = [sba(f"lnT{i}", [128, 8 * 512], BF16) for i in range(2)]
    K_aug = [sba(f"K_aug{hh}", [66, S], BF16) for hh in range(HPG)]
    V_sb = sba("V_sb", [128, 64 * 260], BF16)
    Q_sb = sba("Q_sb", [128, 2 * TPC], BF16)
    qh1 = sba("qh1", [65, TPC], BF16)
    qaugs = [{t: sba(f"qaug{t}{i}", [66, TPC], BF16) for t in "LBR"}
             for i in range(2)]
    rp_pad_sb = sba("rp_pad_sb", [128, W_PAD], BF16)
    rp01 = sba("rp01", [128, 2], BF16)
    pos_pool = [sba(f"pos{i}", [128, 768], F32) for i in range(2)]
    P_pool = [sba(f"P{i}", [128, 1024], BF16) for i in range(2)]
    P_pool.append(lnT[0][:, 3072:4096])
    rcp_sb = sba("rcp_sb", [65, 512], BF16)
    rb_sb = sba("rb_sb", [64, 512], BF16)
    o_tile = sba("o_tile", [64, 512], BF16)
    maxk_col2 = sba("maxk_col2", [128, 1], F32)
    ksub = sba("ksub", [64, 320], BF16)
    qn_row = sba("qn_row", [1, TPC], BF16)
    qn2_row = sba("qn2_row", [1, TPC], BF16)
    c_row = sba("c_row", [1, TPC], F32)
    tmp_row = sba("tmp_row", [1, TPC], F32)
    rp0_row = sba("rp0_row", [1, TPC], BF16)
    rp256_row = sba("rp256_row", [1, TPC], BF16)

    fence0 = [(s, v) for s, v in sch.ctr.items()]
    for hh in range(HPG):
        sch.ev("vector", lambda e, hh=hh: e.memset(K_aug[hh][64:66, :], 1.0),
               waits=fence0, inc="sDVE")
    fence0 = fence0 + [("sDVE", sch.ctr["sDVE"])]
    fence = fence0
    sfree_in = [fence0, fence0]
    ot_free = fence0
    psD_free = fence0
    prep_guards = []
    dbg_marks = []
    pos_prev = fence0
    for g in range(RUN_GROUPS):
        kv = emit_group_proj(g, fence, psD_free)
        out0 = {}
        for _ in emit_head_prep(g, 0, kv, psfree=kv["psd"], guards=fence, out=out0):
            pass
        preps = {0: out0}
        for hh in range(RUN_HEADS):
            bg = None
            if hh + 1 < RUN_HEADS:
                outn = {}
                bg = emit_head_prep(g, hh + 1, kv, psfree=preps[hh]["psD_user"],
                                    guards=preps[hh]["tail"] + list(prep_guards or fence),
                                    out=outn)
                preps[hh + 1] = outn
                if not INTERLEAVE_PREP:
                    for _ in bg:
                        pass
                    bg = None
            res = emit_head_attn(g, hh, preps[hh], sfree_in, ot_free, pos_prev, bg=bg)
            sfree_in = res["s_free"]
            ot_free = res["ot_done"]
            pos_prev = res["pv_last"]
            prep_guards = res["pv_last"] + res["exp_done"] + res["o_stored"] + [
                ("sGP", sch.ctr["sGP"]), ("sDVE", sch.ctr["sDVE"]),
                ("sPOS0", sch.ctr["sPOS0"]), ("sPOS1", sch.ctr["sPOS1"])] + [
                (s, sch.ctr[s]) for s in QSEMS + RPSEMS]
        psD_free = preps[RUN_HEADS - 1]["psD_user"]
        fence = prep_guards + [("sACT", sch.ctr["sACT"])]

    attn_done = prep_guards + [(s, v) for s, v in sch.ctr.items()]
    if DEBUG:
        DBG_OACC = nc.dram_tensor("DBG_OACC", [D, TPC], BF16, kind="ExternalOutput").ap()
        mo = sch.ev("scalar", lambda e: e.dma_start(out=DBG_OACC, in_=o_dram.ap()),
                    waits=attn_done, inc="sST", k=16)
        attn_done = attn_done + [("sST", mo)]

    es_attn.close()
    es_pc = emit_pc_pd(attn_done)

    st_total = sch.ctr["sST"]
    sch.ev("sync", lambda e: e.wait_ge(sch.sems["sST"], st_total))

    block = es.enter_context(nc.Block())
    sch.build(block)
    es.close()
    return nc


_prog_cache = {}


def _get_prog():
    if "nc" not in _prog_cache:
        _prog_cache["nc"] = build_program()
    return _prog_cache["nc"]



def stage_inputs(core, **inputs):
    b, cp = core // 4, core % 4
    x = np.asarray(inputs["L0_attn_x"])
    Wqkv = np.asarray(inputs["L0_attn_Wqkv"])
    bqkv = np.asarray(inputs["L0_attn_bqkv"])
    Wr = np.asarray(inputs["L0_attn_Wr"])
    br = np.asarray(inputs["L0_attn_br"])

    d = {}
    d["x_rows"] = np.ascontiguousarray(x[b])
    d["x_own_rows"] = np.ascontiguousarray(x[b, cp::4])
    d["xT_own"] = np.ascontiguousarray(x[b, cp::4].T)

    Wh = Wqkv.reshape(NH, 3, HD, D)
    d["WQ_T"] = np.ascontiguousarray(Wh[:, 0].reshape(D, D).T)
    d["WK_T"] = np.ascontiguousarray(Wh[:, 1].reshape(D, D).T)
    d["WV_T"] = np.ascontiguousarray(Wh[:, 2].reshape(D, D).T)
    bh = bqkv.reshape(NH, 3, HD)
    bq = bh[:, 0].reshape(D)
    bk = bh[:, 1].reshape(D)
    bv = bh[:, 2].reshape(D)
    d["bQKV"] = np.ascontiguousarray(
        np.concatenate([bq.reshape(8, 128).T, bk.reshape(8, 128).T, bv.reshape(8, 128).T], axis=1))
    WrB = np.concatenate([Wr.T, br[None, :]], axis=0)
    idx = np.clip(np.arange(261) - cp, 0, 256)
    d["WrBc"] = np.ascontiguousarray(WrB[:, idx])
    d["Wr0"] = np.ascontiguousarray(WrB[:, 0:1])
    d["Wr256"] = np.ascontiguousarray(WrB[:, 256:257])
    d["gate_w"] = np.ascontiguousarray(inputs["L0_ffn_gate_w"])
    for k in ("ew1", "ew2", "ew3", "sw1", "sw2", "sw3"):
        d[k] = np.ascontiguousarray(inputs["L0_ffn_" + k])
    return d


def kernel(**inputs):
    nc = _get_prog()
    in_maps = [stage_inputs(c, **inputs) for c in range(8)]
    res = run_bass_kernel_spmd(nc, in_maps, core_ids=list(range(8)))
    out = np.empty((B, S, D), dtype=np.float32)
    for c in range(8):
        b, cp = c // 4, c % 4
        out[b, cp::4, :] = res.results[c]["OUT"].T
    return out
